# version 01
# baseline (speedup 1.0000x reference)
import numpy as np
import concourse.bass as bass
import concourse.mybir as mybir
from concourse.bass_utils import run_bass_kernel_spmd
from contextlib import ExitStack

F32 = mybir.dt.float32
BF16 = mybir.dt.bfloat16
I32 = mybir.dt.int32
AF = mybir.ActivationFunctionType
ALU = mybir.AluOpType
AXX = mybir.AxisListType.X

NTILE = 32
NQT = 16
NQ = 2048
NK = 4096
DM = 1024
FF = 4096
NPAIR = 8
NFC = 32
EPS = 1e-5

DEBUG_TAPS = []

_REG = {}


def _ensure_custom_ops():
    if _REG:
        return
    import concourse.dve_ops as dve_ops
    from concourse.dve_op import DveOp
    from concourse.dve_spec import Spec, Src0, Src1, Zero, One, C2, lower
    from concourse.dve_table_gen import dve_ver_for

    have = {o.name: o for o in dve_ops.OPS}

    def _f3_ref(in0, in1, s0, s1, imm2):
        d = in0 - in1
        dt_ = in0.dtype
        ge = lambda t: (d >= t).astype(dt_)
        return ((in1 + dt_.type(imm2)) + ge(-1.0)) + (ge(0.0) + ge(1.0))

    D = Src0 - Src1
    body3 = ((Src1 + C2) + (D >= (Zero - One))) + ((D >= Zero) + (D >= One))

    ver = dve_ver_for("TRN2")
    for name, body, ref in (("MK_FLOOR3", body3, _f3_ref),):
        if name in have:
            _REG[name] = have[name]
            continue
        spec = Spec(body=body)
        sha = lower(spec, ver=ver, perf_en=False, name=name).sha(ver)
        op = DveOp(name, spec, subdim=False, uops_sha={ver: sha}, reference=ref)
        dve_ops.OPS.append(op)
        dve_ops.CUSTOM_DVE_SPECS[name] = spec
        dve_ops.CUSTOM_DVE_REFS[name] = ref
        dve_ops._SUB_OPCODE_FOR_NAME[name] = (
            dve_ops._CUSTOM_DVE_ROW_BASE + len(dve_ops.OPS) - 1)
        _REG[name] = op


class Sem:
    def __init__(self, nc, ctx, name):
        self.h = ctx.enter_context(nc.semaphore(name))
        self.n = 0
        self.name = name


class Plan:
    ENGS = ("sync", "tensor", "vector", "scalar", "gpsimd")

    def __init__(self):
        self.q = {e: [] for e in self.ENGS}
        self.lastw = {e: {} for e in self.ENGS}

    def emit(self, eng, fn, waits=(), inc=None, inck=1):
        lw = self.lastw[eng]
        real = []
        for s, v in waits:
            if v > 0 and lw.get(s.name, -1) < v:
                real.append((s, v))
                lw[s.name] = v
        if inc is not None:
            inc.n += inck
        self.q[eng].append((real, fn, inc, inck))

    def barrier(self, s):
        target = s.n + 5
        for e in self.ENGS:
            self.q[e].append(([], lambda eng: eng.nop(), s, 1))
        s.n = target
        for e in self.ENGS:
            self.q[e].append(([(s, target)], None, None, 0))
            self.lastw[e][s.name] = target

    def run(self, eng_name, eng):
        for waits, fn, inc, k in self.q[eng_name]:
            for s, v in waits:
                eng.wait_ge(s.h, v)
            if fn is None:
                continue
            ins = fn(eng)
            if inc is not None and ins is not None:
                ins.then_inc(inc.h, k)


def build_program():
    _ensure_custom_ops()
    F3 = _REG["MK_FLOOR3"]
    nc = bass.Bass("TRN2", target_bir_lowering=False, debug=False)

    xq_d = nc.dram_tensor("xq", [NQ, DM], F32, kind="ExternalInput").ap()
    xr_d = nc.dram_tensor("xr", [NQ, DM], F32, kind="ExternalInput").ap()
    wq_d = nc.dram_tensor("wq", [DM, DM], F32, kind="ExternalInput").ap()
    wk_d = nc.dram_tensor("wk", [DM, DM], F32, kind="ExternalInput").ap()
    wv_d = nc.dram_tensor("wv", [DM, DM], F32, kind="ExternalInput").ap()
    wo_d = nc.dram_tensor("wo", [DM, DM], F32, kind="ExternalInput").ap()
    bo_d = nc.dram_tensor("bo", [DM], F32, kind="ExternalInput").ap()
    w1_d = nc.dram_tensor("w1", [DM, FF], F32, kind="ExternalInput").ap()
    b1_d = nc.dram_tensor("b1", [FF], F32, kind="ExternalInput").ap()
    w2_d = nc.dram_tensor("w2", [FF, DM], F32, kind="ExternalInput").ap()
    b2_d = nc.dram_tensor("b2", [DM], F32, kind="ExternalInput").ap()
    out_d = nc.dram_tensor("out", [NQ, DM], F32, kind="ExternalOutput").ap()
    resid_d = nc.dram_tensor("residsc", [NQ, DM], F32, kind="Internal").ap()
    rec_d = nc.dram_tensor("recsc", [2, NQ], F32, kind="Internal").ap()

    taps = {}

    def mktap(name, shape, dt):
        if name in DEBUG_TAPS:
            taps[name] = nc.dram_tensor("tap_" + name, shape, dt,
                                        kind="ExternalOutput").ap()

    mktap("xn", [128, 8, 256], BF16)
    mktap("qT", [128, 256], BF16)
    mktap("kT", [128, 256], BF16)
    mktap("vA", [128, 132], BF16)
    mktap("pt", [128, NQ], BF16)
    mktap("avT", [128, 8, 256], BF16)
    mktap("r1", [128, DM], F32)
    mktap("x2T", [128, 8, 256], BF16)
    mktap("hq", [128, 1024], BF16)
    mktap("zf", [128, 1024], F32)
    mktap("tn", [128, 1024], F32)
    mktap("tk4", [128, 1024], F32)
    mktap("tr", [128, 1024], F32)
    mktap("ty", [128, 1024], F32)

    with ExitStack() as ctx:
        S = lambda n: Sem(nc, ctx, n)
        s_x = S("s_x")
        s_w = S("s_w")
        s_tp = S("s_tp")
        s_rec = S("s_rec")
        s_res = S("s_res")
        s_out = S("s_out")
        s_pe = S("s_pe")
        s_dve = S("s_dve")
        s_act = S("s_act")
        s_pool = S("s_pool")
        s_exp = S("s_exp")
        s_bar = S("s_bar")

        sb = lambda n, shp, dt=F32: ctx.enter_context(nc.sbuf_tensor(n, shp, dt))
        consts = sb("consts", [128, 8])
        onesbf = sb("onesbf", [1, 128], BF16)
        xst = sb("xst", [128, 2, DM])
        stats = sb("stats", [128, 40])
        mv = sb("mv", [128, 4])
        outst = sb("outst", [128, 2, DM])
        b1t = sb("b1t", [128, 2])
        b1s = sb("b1s", [128, 2])
        xn2T = sb("xn2T", [128, 8, NQ], BF16)
        ps_a = ctx.enter_context(nc.psum_tensor("ps_a", [128, 2048], F32))

        plan = Plan()
        E = plan.emit

        def _init(gp):
            gp.memset(consts[:, 0:1], EPS)
            gp.memset(consts[:, 1:2], -1.0)
            gp.memset(consts[:, 2:3], 0.5)
            gp.memset(consts[:, 3:4], 0.25)
            return gp.memset(onesbf[:, :], 1.0)

        E("gpsimd", _init, inc=s_pool)
        init_done = s_pool.n

        def plan_ln(xv, pre_waits, sq_ps):
            E("vector", (lambda xv: lambda e: e.tensor_reduce(
                out=mv[:, 0:1], in_=xv, axis=AXX, op=ALU.add))(xv),
                waits=list(pre_waits), inc=s_dve)
            E("scalar", (lambda xv, sq_ps: lambda e: e.activation(
                out=sq_ps, in_=xv, func=AF.Square, bias=0.0, scale=1.0))(
                    xv, sq_ps),
                waits=[(s_dve, s_dve.n)], inc=s_act)
            E("vector", (lambda sq_ps: lambda e: e.tensor_reduce(
                out=mv[:, 1:2], in_=sq_ps, axis=AXX, op=ALU.add))(sq_ps),
                waits=[(s_act, s_act.n)], inc=s_dve)
            E("vector", lambda e: e.tensor_scalar(
                out=mv[:, 0:1], in0=mv[:, 0:1], scalar1=1.0 / 1024.0,
                scalar2=None, op0=ALU.mult),
                waits=[(s_dve, s_dve.n)], inc=s_dve)
            E("scalar", lambda e: e.activation(
                out=mv[:, 2:3], in_=mv[:, 0:1], func=AF.Square,
                bias=0.0, scale=1.0),
                waits=[(s_dve, s_dve.n)], inc=s_act)
            E("vector", lambda e: e.scalar_tensor_tensor(
                out=mv[:, 1:2], in0=mv[:, 1:2], scalar=1.0 / 1024.0,
                in1=mv[:, 2:3], op0=ALU.mult, op1=ALU.subtract),
                waits=[(s_act, s_act.n)], inc=s_dve)
            E("scalar", lambda e: e.activation(
                out=mv[:, 2:3], in_=mv[:, 1:2], func=AF.Ln,
                bias=consts[:, 0:1], scale=1.0),
                waits=[(s_dve, s_dve.n)], inc=s_act)
            E("scalar", lambda e: e.activation(
                out=mv[:, 3:4], in_=mv[:, 2:3], func=AF.Exp,
                bias=0.0, scale=-0.5),
                waits=[(s_act, s_act.n)], inc=s_act)
            return s_act.n


        ab2 = ExitStack()
        sb2 = lambda n, shp, dt=F32: ab2.enter_context(nc.sbuf_tensor(n, shp, dt))
        recb = sb2("recb", [64, NQ])
        avT = sb2("avT", [128, 8, NQ], BF16)
        ab1 = ExitStack()
        sb1 = lambda n, shp, dt=F32: ab1.enter_context(nc.sbuf_tensor(n, shp, dt))
        xnT = sb1("xnT", [128, 8, NK], BF16)
        xntok = sb1("xntok", [128, 2, DM], BF16)
        wstg = sb1("wstg", [128, 3, DM])
        wqbf = sb1("wqbf", [128, 8, 128], BF16)
        wkbf = sb1("wkbf", [128, 8, 128], BF16)
        wvbf = sb1("wvbf", [128, 8, 128], BF16)
        qT = sb1("qT", [128, NQ], BF16)
        kT = sb1("kT", [128, NK], BF16)
        vstg = sb1("vstg", [128, 2, 512], BF16)
        vtok = sb1("vtok", [128, 2, 128], BF16)
        vAug = sb1("vAug", [128, 32, 132], BF16)
        ptile = sb1("ptile", [128, 2, NQ], BF16)

        recsb = outst[0:1, :, :].rearrange("p a b -> p (a b)")
        avsb = xst[0:64, :, :].rearrange("p a b -> p (a b)")
        ps_b_st = ExitStack()
        ps_b = ps_b_st.enter_context(nc.psum_tensor("ps_b", [128, 2048], F32))

        def _vaug_ones(gp):
            gp.memset(vAug[:, :, 64:65], 1.0)
            return gp.memset(vAug[:, :, 130:131], 1.0)

        E("gpsimd", _vaug_ones, inc=s_pool)
        vaug_ones_cnt = s_pool.n

        dve_norm = {}
        tp_tile = {}
        act_rstd = {}
        for tt in range(NTILE):
            src = xq_d if tt < NQT else xr_d
            row = (tt % NQT) * 128
            par = tt % 2
            E("sync", (lambda par, s0: lambda e: e.dma_start(
                out=xst[:, par, :], in_=s0))(par, src[row:row + 128, :]),
                waits=[(s_dve, dve_norm.get(tt - 2, 0)),
                       (s_pool, init_done)], inc=s_x, inck=16)
            xv = xst[:, par, :]
            act_rstd[tt] = plan_ln(xv, [(s_x, s_x.n),
                                        (s_act, act_rstd.get(tt - 1, 0))],
                                   ps_a[:, 0:1024])
            E("vector", (lambda xv, par: lambda e: e.tensor_scalar(
                out=xntok[:, par, :], in0=xv, scalar1=mv[:, 0:1],
                scalar2=mv[:, 3:4], op0=ALU.subtract, op1=ALU.mult))(xv, par),
                waits=[(s_act, s_act.n), (s_tp, tp_tile.get(tt - 2, 0))],
                inc=s_dve)
            dve_norm[tt] = s_dve.n
            for a in range(8):
                qn = "sync" if (tt + a) % 2 == 0 else "scalar"
                E(qn, (lambda a, par, tt: lambda e: e.dma_start(
                    out=xnT[:, a, tt * 128:(tt + 1) * 128],
                    in_=xntok[:, par, a * 128:(a + 1) * 128], transpose=True))(
                        a, par, tt),
                    waits=[(s_dve, s_dve.n)], inc=s_tp, inck=16)
            tp_tile[tt] = s_tp.n
        tp_after_A = s_tp.n

        if "xn" in taps:
            E("gpsimd", lambda e: e.dma_start(out=taps["xn"],
                                              in_=xnT[:, :, 0:256]),
              waits=[(s_tp, tp_after_A)], inc=s_out, inck=16)

        wbufs = (wqbf, wkbf, wvbf)
        wsrc = (wq_d, wk_d, wv_d)
        cast_cnt = {}
        wdma_cnt = {}

        def plan_wdma(p):
            for wi in range(3):
                sidx = 3 * p + wi
                spar = sidx % 3
                src = wsrc[wi][:, p * 128:(p + 1) * 128].rearrange(
                    "(a q) m -> q a m", q=128)
                E("sync", (lambda spar, src: lambda e: e.dma_start(
                    out=wstg[:, spar, :].rearrange("q (a m) -> q a m", a=8),
                    in_=src))(spar, src),
                    waits=[(s_pool, cast_cnt.get(sidx - 3, 0))],
                    inc=s_w, inck=16)
                wdma_cnt[sidx] = s_w.n

        def plan_wcast(p, pe_guard):
            for wi in range(3):
                sidx = 3 * p + wi
                spar = sidx % 3
                E("gpsimd", (lambda spar, wi: lambda e: e.tensor_copy(
                    out=wbufs[wi][:, :, :],
                    in_=wstg[:, spar, :].rearrange("q (a m) -> q a m", a=8)))(
                        spar, wi),
                    waits=[(s_w, wdma_cnt[sidx]), (s_pe, pe_guard)],
                    inc=s_pool)
                cast_cnt[sidx] = s_pool.n

        plan_wdma(0)
        plan_wcast(0, 0)
        proj_pe_done = 0

        for p in range(NPAIR):
            pp = p % 2
            if p + 1 < NPAIR:
                plan_wdma(p + 1)
            for wi, (wbf, dst, nch) in enumerate(((wqbf, qT, 4), (wkbf, kT, 8))):
                need_w = cast_cnt[3 * p + wi]
                for c in range(nch):
                    bank = c % 4

                    def mmfn(wbf=wbf, c=c, bank=bank):
                        def f(e):
                            ins = None
                            for a in range(8):
                                ins = e.matmul(
                                    ps_a[:, bank * 512:(bank + 1) * 512],
                                    wbf[:, a, :],
                                    xnT[:, a, c * 512:(c + 1) * 512],
                                    start=(a == 0), stop=(a == 7))
                            return ins
                        return f

                    E("tensor", mmfn(),
                      waits=[(s_pool, need_w), (s_tp, tp_after_A),
                             (s_exp, s_exp.n), (s_dve, s_dve.n)],
                      inc=s_pe)
                    E("vector", (lambda dst, c, bank: lambda e: e.tensor_copy(
                        out=dst[:, c * 512:(c + 1) * 512],
                        in_=ps_a[:, bank * 512:(bank + 1) * 512]))(dst, c, bank),
                        waits=[(s_pe, s_pe.n)], inc=s_dve)
            need_w = cast_cnt[3 * p + 2]
            for c in range(8):
                bank = c % 4

                def mmfn(c=c, bank=bank):
                    def f(e):
                        ins = None
                        for a in range(8):
                            ins = e.matmul(
                                ps_a[:, bank * 512:(bank + 1) * 512],
                                wvbf[:, a, :],
                                xnT[:, a, c * 512:(c + 1) * 512],
                                start=(a == 0), stop=(a == 7))
                        return ins
                    return f

                E("tensor", mmfn(),
                  waits=[(s_pool, need_w), (s_exp, s_exp.n),
                         (s_dve, s_dve.n), (s_tp, s_tp.n)],
                  inc=s_pe)
                E("vector", (lambda c, bank: lambda e: e.tensor_copy(
                    out=vstg[:, c % 2, :],
                    in_=ps_a[:, bank * 512:(bank + 1) * 512]))(c, bank),
                    waits=[(s_pe, s_pe.n), (s_tp, s_tp.n - (64 if c >= 2 else 0))],
                    inc=s_dve)
                for j in range(4):
                    kt = 4 * c + j
                    E("sync", (lambda c, j, kt: lambda e: e.dma_start(
                        out=vtok[:, kt % 2, :],
                        in_=vstg[:, c % 2, j * 128:(j + 1) * 128],
                        transpose=True))(c, j, kt),
                        waits=[(s_dve, s_dve.n), (s_pool, vaug_ones_cnt)],
                        inc=s_tp, inck=16)
                    E("vector", (lambda kt: lambda e: e.tensor_copy(
                        out=vAug[:, kt, :].rearrange(
                            "p (g c) -> p g c", c=66)[:, :, 0:64],
                        in_=vtok[:, kt % 2, :].rearrange(
                            "p (g c) -> p g c", c=64)))(kt),
                        waits=[(s_tp, s_tp.n)], inc=s_dve)
            qkv_done = s_dve.n
            vaug_done = s_tp.n
            proj_pe_done = s_pe.n
            if p + 1 < NPAIR:
                plan_wcast(p + 1, proj_pe_done)

            if p == 0 and "qT" in taps:
                E("gpsimd", lambda e: e.dma_start(out=taps["qT"],
                                                  in_=qT[:, 0:256]),
                  waits=[(s_dve, qkv_done)], inc=s_out, inck=16)
            if p == 0 and "kT" in taps:
                E("gpsimd", lambda e: e.dma_start(out=taps["kT"],
                                                  in_=kT[:, 0:256]),
                  waits=[(s_dve, qkv_done)], inc=s_out, inck=16)
            if p == 0 and "vA" in taps:
                E("gpsimd", lambda e: e.dma_start(
                    out=taps["vA"], in_=vAug[:, 0, :]),
                    waits=[(s_tp, vaug_done)], inc=s_out, inck=16)

            for h in range(2):
                hb = 64 * h
                for kt in range(32):
                    kpar = kt % 2

                    def scfn(hb=hb, kt=kt):
                        def f(e):
                            ins = None
                            for qc in range(4):
                                ins = e.matmul(
                                    ps_a[:, qc * 512:(qc + 1) * 512],
                                    kT[hb:hb + 64, kt * 128:(kt + 1) * 128],
                                    qT[hb:hb + 64, qc * 512:(qc + 1) * 512],
                                    start=True, stop=True)
                            return ins
                        return f

                    scw = [(s_exp, s_exp.n), (s_dve, qkv_done),
                           (s_tp, vaug_done)]
                    if kt == 0:
                        scw += [(s_act, s_act.n), (s_dve, s_dve.n)]
                    E("tensor", scfn(), waits=scw, inc=s_pe)
                    E("scalar", (lambda kpar: lambda e: e.activation(
                        out=ptile[:, kpar, :], in_=ps_a[:, 0:2048],
                        func=AF.Exp, bias=consts[:, 1:2],
                        scale=1.0 / 32.0))(kpar),
                        waits=[(s_pe, s_pe.n)], inc=s_exp)

                    def avfn(h=h, kt=kt, kpar=kpar):
                        base = 0 if h == 0 else 66

                        def f(e):
                            ins = None
                            for qc in range(4):
                                ins = e.matmul(
                                    ps_b[0:65, qc * 512:(qc + 1) * 512],
                                    vAug[:, kt, base:base + 65],
                                    ptile[:, kpar, qc * 512:(qc + 1) * 512],
                                    start=(kt == 0), stop=(kt == 31))
                            return ins
                        return f

                    E("tensor", avfn(), waits=[(s_exp, s_exp.n)], inc=s_pe)
                av_done = s_pe.n
                if p == 0 and h == 0 and "pt" in taps:
                    E("gpsimd", lambda e: e.dma_start(out=taps["pt"],
                                                      in_=ptile[:, 1, :]),
                      waits=[(s_exp, s_exp.n)], inc=s_out, inck=16)
                hpar = h
                E("vector", lambda e: e.reciprocal(
                    out=recsb, in_=ps_b[64:65, 0:2048]),
                    waits=[(s_pe, av_done)], inc=s_dve)
                E("gpsimd", (lambda hpar: lambda e: e.dma_start(
                    out=rec_d[hpar, None, :], in_=recsb))(hpar),
                    waits=[(s_dve, s_dve.n)], inc=s_rec, inck=16)
                E("gpsimd", (lambda hpar: lambda e: e.dma_start(
                    out=recb[:, :],
                    in_=rec_d[hpar, None, :].broadcast_to([64, NQ])))(hpar),
                    waits=[(s_rec, s_rec.n), (s_dve, s_dve.n)],
                    inc=s_rec, inck=16)
                E("scalar", lambda e: e.copy(out=avsb[:, :],
                                             in_=ps_b[0:64, 0:2048]),
                  waits=[(s_pe, av_done)], inc=s_act)
                E("vector", (lambda p, hb: lambda e: e.tensor_mul(
                    out=avT[hb:hb + 64, p, :], in0=avsb[:, :],
                    in1=recb[:, :]))(p, hb),
                    waits=[(s_act, s_act.n), (s_rec, s_rec.n)], inc=s_dve)
        avT_done = s_dve.n
        if "avT" in taps:
            E("gpsimd", lambda e: e.dma_start(out=taps["avT"],
                                              in_=avT[:, :, 0:256]),
              waits=[(s_dve, avT_done)], inc=s_out, inck=16)

        plan.barrier(s_bar)
        ab1.close()

        cst_tmp = ExitStack()
        sbc = lambda n, shp, dt=F32: cst_tmp.enter_context(
            nc.sbuf_tensor(n, shp, dt))
        wostg = sbc("wostg", [128, 2, DM])
        wobf = sbc("wobf", [128, 8, DM], BF16)
        x2tok = sbc("x2tok", [128, 2, DM], BF16)
        r1t = sbc("r1t", [128, 2, DM])
        bo_bf = sbc("bo_bf", [1, DM], BF16)
        E("sync", lambda e: e.dma_start(out=r1t[0:1, 0, :], in_=bo_d[None, :]),
          inc=s_w, inck=16)
        E("gpsimd", lambda e: e.tensor_copy(out=bo_bf[:, :], in_=r1t[0:1, 0, :]),
          waits=[(s_w, s_w.n)], inc=s_pool)
        bocast = s_pool.n

        for a in range(8):
            par = a % 2
            E("sync", (lambda a, par: lambda e: e.dma_start(
                out=wostg[:, par, :],
                in_=wo_d[a * 128:(a + 1) * 128, :]))(a, par),
                waits=[(s_pool, s_pool.n if a >= 2 else 0)],
                inc=s_w, inck=16)
            E("gpsimd", (lambda a, par: lambda e: e.tensor_copy(
                out=wobf[:, a, :], in_=wostg[:, par, :]))(a, par),
                waits=[(s_w, s_w.n)], inc=s_pool)
        wo_cast = s_pool.n

        r1_cnt = {}
        res_cnt = {}
        x2tp = {}
        rstd2 = {}
        for tt in range(NQT):
            par = tt % 2
            E("sync", (lambda tt, par: lambda e: e.dma_start(
                out=xst[:, par, :],
                in_=xq_d[tt * 128:(tt + 1) * 128, :]))(tt, par),
                waits=[(s_dve, max(avT_done, r1_cnt.get(tt - 2, 0)))],
                inc=s_x, inck=16)

            def wofn(tt=tt, par=par):
                def f(e):
                    ins = None
                    for nc2 in range(2):
                        sl = slice(par * 1024 + nc2 * 512,
                                   par * 1024 + nc2 * 512 + 512)
                        for a in range(8):
                            e.matmul(ps_a[:, sl],
                                     avT[:, a, tt * 128:(tt + 1) * 128],
                                     wobf[:, a, nc2 * 512:(nc2 + 1) * 512],
                                     start=(a == 0), stop=False)
                        ins = e.matmul(ps_a[:, sl], onesbf[:, 0:128],
                                       bo_bf[:, nc2 * 512:(nc2 + 1) * 512],
                                       start=False, stop=True)
                    return ins
                return f

            E("tensor", wofn(),
              waits=[(s_pool, max(wo_cast, bocast)),
                     (s_dve, r1_cnt.get(tt - 2, 0)), (s_exp, s_exp.n)],
              inc=s_pe)
            pe_wo = s_pe.n
            E("vector", (lambda par: lambda e: e.tensor_add(
                out=r1t[:, par, :],
                in0=ps_a[:, par * 1024:par * 1024 + 1024],
                in1=xst[:, par, :]))(par),
                waits=[(s_pe, pe_wo), (s_x, s_x.n),
                       (s_res, res_cnt.get(tt - 2, 0)),
                       (s_tp, x2tp.get(tt - 2, 0)),
                       (s_act, rstd2.get(tt - 2, 0)),
                       (s_pool, bocast)],
                inc=s_dve)
            r1_cnt[tt] = s_dve.n
            E("gpsimd", (lambda tt, par: lambda e: e.dma_start(
                out=resid_d[tt * 128:(tt + 1) * 128, :],
                in_=r1t[:, par, :]))(tt, par),
                waits=[(s_dve, s_dve.n)], inc=s_res, inck=16)
            res_cnt[tt] = s_res.n
            if tt == 0 and "r1" in taps:
                E("gpsimd", lambda e: e.dma_start(out=taps["r1"],
                                                  in_=r1t[:, 0, :]),
                  waits=[(s_dve, s_dve.n)], inc=s_out, inck=16)
            rstd2[tt] = plan_ln(r1t[:, par, :],
                                 [(s_dve, s_dve.n),
                                  (s_act, rstd2.get(tt - 1, 0))],
                                 ps_b[:, 0:1024])
            E("vector", (lambda par: lambda e: e.tensor_scalar(
                out=x2tok[:, par, :], in0=r1t[:, par, :],
                scalar1=mv[:, 0:1], scalar2=mv[:, 3:4],
                op0=ALU.subtract, op1=ALU.mult))(par),
                waits=[(s_act, s_act.n)], inc=s_dve)
            for a in range(8):
                qn = "sync" if (tt + a) % 2 == 0 else "scalar"
                E(qn, (lambda a, par, tt: lambda e: e.dma_start(
                    out=xn2T[:, a, tt * 128:(tt + 1) * 128],
                    in_=x2tok[:, par, a * 128:(a + 1) * 128],
                    transpose=True))(a, par, tt),
                    waits=[(s_dve, s_dve.n)], inc=s_tp, inck=16)
            x2tp[tt] = s_tp.n
        xn2_done = s_tp.n
        if "x2T" in taps:
            E("gpsimd", lambda e: e.dma_start(out=taps["x2T"],
                                              in_=xn2T[:, :, 0:256]),
              waits=[(s_tp, xn2_done)], inc=s_out, inck=16)

        plan.barrier(s_bar)
        cst_tmp.close()
        ab2.close()
        ps_b_st.close()

        est = ExitStack()
        sbe = lambda n, shp, dt=F32: est.enter_context(
            nc.sbuf_tensor(n, shp, dt))
        hq = sbe("hq", [128, NFC, 1024], BF16)
        w1stg = sbe("w1stg", [128, 2, DM])
        w1bf = sbe("w1bf", [128, 2, 8, 128], BF16)
        w2stg = sbe("w2stg", [128, 2, DM])
        w2bf = sbe("w2bf", [128, 3, DM], BF16)
        TP = sbe("tpool", [128, 2, 4, 1024])
        TB = sbe("tbit", [128, 2, 3, 1024], BF16)
        TI = sbe("tint", [128, 2, 1024], I32)
        zfb = sbe("zfb", [128, 2, 1024])
        b2_bf = sbe("b2_bf", [1, DM], BF16)
        E("sync", lambda e: e.dma_start(out=zfb[0:1, 0, :], in_=b2_d[None, :]),
          inc=s_w, inck=16)
        E("gpsimd", lambda e: e.tensor_copy(out=b2_bf[:, :], in_=zfb[0:1, 0, :]),
          waits=[(s_w, s_w.n)], inc=s_pool)
        b2cast = s_pool.n
        ps_c = est.enter_context(nc.psum_tensor("ps_c", [128, 2048], F32))

        for th in range(2):
            qoff = th * 1024
            w1cast = {}
            zf_act = {}
            ch_dve = {}
            ch_pool = {}
            mm1pe = {}
            b1s_dve = {}
            for f in range(NFC):
                par = f % 2
                src = w1_d[:, f * 128:(f + 1) * 128].rearrange(
                    "(a q) m -> q a m", q=128)
                E("sync", (lambda par, src: lambda e: e.dma_start(
                    out=w1stg[:, par, :].rearrange("q (a m) -> q a m", a=8),
                    in_=src))(par, src),
                    waits=[(s_pool, w1cast.get(f - 2, 0))],
                    inc=s_w, inck=16)
                E("sync", (lambda par, f: lambda e: e.dma_start(
                    out=b1t[:, par:par + 1],
                    in_=b1_d[f * 128:(f + 1) * 128].rearrange(
                        "(q o) -> q o", o=1)))(par, f),
                    waits=[(s_dve, b1s_dve.get(f - 2, 0))],
                    inc=s_w, inck=16)
                E("gpsimd", (lambda par: lambda e: e.tensor_copy(
                    out=w1bf[:, par, :, :],
                    in_=w1stg[:, par, :].rearrange("q (a m) -> q a m",
                                                   a=8)))(par),
                    waits=[(s_w, s_w.n - 16),
                           (s_pe, mm1pe.get(f - 2, 0))], inc=s_pool)
                w1cast[f] = s_pool.n
                E("vector", (lambda par: lambda e: e.tensor_scalar(
                    out=b1s[:, par:par + 1], in0=b1t[:, par:par + 1],
                    scalar1=256.0, scalar2=0.5, op0=ALU.mult,
                    op1=ALU.add))(par),
                    waits=[(s_w, s_w.n)], inc=s_dve)
                b1s_cnt = s_dve.n
                b1s_dve[f] = s_dve.n

                def mm1fn(par=par, qoff=qoff):
                    def fx(e):
                        ins = None
                        for c in range(2):
                            sl = slice(par * 1024 + c * 512,
                                       par * 1024 + c * 512 + 512)
                            for a in range(8):
                                ins = e.matmul(
                                    ps_a[:, sl], w1bf[:, par, a, :],
                                    xn2T[:, a,
                                         qoff + c * 512:qoff + c * 512 + 512],
                                    start=(a == 0), stop=(a == 7))
                        return ins
                    return fx

                E("tensor", mm1fn(),
                  waits=[(s_pool, w1cast[f]), (s_tp, xn2_done),
                         (s_act, zf_act.get(f - 2, 0))],
                  inc=s_pe)
                mm1pe[f] = s_pe.n

                T = lambda i: TP[:, par, i, :]
                IT = TI[:, par, :]
                zf = zfb[:, par, :]
                wch = [(s_pe, s_pe.n), (s_dve, b1s_cnt),
                       (s_dve, ch_dve.get(f - 2, 0)),
                       (s_pool, ch_pool.get(f - 2, 0))]
                E("scalar", (lambda zf, par: lambda e: e.activation(
                    out=zf, in_=ps_a[:, par * 1024:par * 1024 + 1024],
                    func=AF.Identity, bias=b1s[:, par:par + 1],
                    scale=256.0))(zf, par),
                    waits=wch, inc=s_act)
                zf_act[f] = s_act.n
                A, B, C, D = T(0), T(1), T(2), T(3)
                U = lambda i: TB[:, par, i, :]
                U0, U1, U2 = U(0), U(1), U(2)
                GE = ALU.is_ge
                E("gpsimd", (lambda IT, zf: lambda e: e.tensor_copy(
                    out=IT, in_=zf))(IT, zf),
                    waits=[(s_act, s_act.n)], inc=s_pool)
                E("vector", (lambda IT, A=A: lambda e: e.tensor_copy(
                    out=A, in_=IT))(IT),
                    waits=[(s_pool, s_pool.n)], inc=s_dve)
                E("vector", (lambda zf, A=A, B=B: lambda e: e.tensor_sub(
                    out=B, in0=zf, in1=A))(zf), inc=s_dve)
                for i, thr in enumerate((-1.0, 0.0, 1.0)):
                    E("vector", (lambda B=B, u=U(i), thr=thr: lambda e:
                       e.tensor_scalar(out=u, in0=B, scalar1=thr,
                                       scalar2=None, op0=GE))(),
                       inc=s_dve)
                E("vector", (lambda U0=U0, U1=U1: lambda e: e.tensor_add(
                    out=U0, in0=U0, in1=U1))(), inc=s_dve)
                E("vector", (lambda U0=U0, U2=U2: lambda e: e.tensor_add(
                    out=U0, in0=U0, in1=U2))(), inc=s_dve)
                E("vector", (lambda A=A, U0=U0, C=C: lambda e:
                   e.scalar_tensor_tensor(out=C, in0=A, scalar=-2.0,
                                          in1=U0, op0=ALU.add,
                                          op1=ALU.add))(), inc=s_dve)
                if th == 0 and f == 0 and "tn" in taps:
                    E("gpsimd", (lambda C=C: lambda e: e.dma_start(
                        out=taps["tn"], in_=C))(),
                        waits=[(s_dve, s_dve.n)], inc=s_out, inck=16)
                E("vector", (lambda zf, B=B: lambda e: e.tensor_scalar(
                    out=B, in0=zf, scalar1=1.0 / 64.0, scalar2=None,
                    op0=ALU.mult))(zf), inc=s_dve)
                E("gpsimd", (lambda IT, B=B: lambda e: e.tensor_copy(
                    out=IT, in_=B))(IT),
                    waits=[(s_dve, s_dve.n)], inc=s_pool)
                E("vector", (lambda IT, A=A: lambda e: e.tensor_copy(
                    out=A, in_=IT))(IT),
                    waits=[(s_pool, s_pool.n)], inc=s_dve)
                E("vector", (lambda B=B, A=A, U0=U0: lambda e:
                   e.tensor_tensor(out=U0, in0=B, in1=A, op=GE))(),
                   inc=s_dve)
                E("vector", (lambda A=A, U0=U0, D=D: lambda e:
                   e.scalar_tensor_tensor(out=D, in0=A, scalar=-1.0,
                                          in1=U0, op0=ALU.add,
                                          op1=ALU.add))(), inc=s_dve)
                E("vector", (lambda D=D, C=C, B=B: lambda e:
                   e.scalar_tensor_tensor(out=B, in0=D, scalar=-64.0,
                                          in1=C, op0=ALU.mult,
                                          op1=ALU.add))(), inc=s_dve)
                dve_r = s_dve.n
                if th == 0 and f == 0 and "tr" in taps:
                    E("gpsimd", (lambda D=D: lambda e: e.dma_start(
                        out=taps["tk4"], in_=D))(),
                        waits=[(s_dve, s_dve.n)], inc=s_out, inck=16)
                    E("gpsimd", (lambda B=B: lambda e: e.dma_start(
                        out=taps["tr"], in_=B))(),
                        inc=s_out, inck=16)
                E("scalar", (lambda D=D, A=A: lambda e: e.activation(
                    out=A, in_=D, func=AF.Gelu, bias=0.0, scale=0.25))(),
                    waits=[(s_dve, dve_r)], inc=s_act)
                E("scalar", (lambda D=D, C=C: lambda e: e.activation(
                    out=C, in_=D, func=AF.Gelu, bias=consts[:, 3:4],
                    scale=0.25))(), inc=s_act)
                act_g = s_act.n
                E("vector", (lambda C=C, A=A: lambda e: e.tensor_sub(
                    out=C, in0=C, in1=A))(),
                    waits=[(s_act, act_g)], inc=s_dve)
                E("vector", (lambda C=C, B=B: lambda e:
                   e.scalar_tensor_tensor(out=C, in0=C, scalar=1.0 / 64.0,
                                          in1=B, op0=ALU.mult,
                                          op1=ALU.mult))(), inc=s_dve)
                E("vector", (lambda C=C, A=A: lambda e: e.tensor_add(
                    out=C, in0=C, in1=A))(), inc=s_dve)
                dve_y = s_dve.n
                if th == 0 and f == 0 and "ty" in taps:
                    E("gpsimd", (lambda C=C: lambda e: e.dma_start(
                        out=taps["ty"], in_=C))(),
                        waits=[(s_dve, s_dve.n)], inc=s_out, inck=16)
                E("scalar", (lambda zf, C=C: lambda e: e.activation(
                    out=zf, in_=C, func=AF.Identity, bias=consts[:, 2:3],
                    scale=256.0))(zf),
                    waits=[(s_dve, dve_y)], inc=s_act)
                E("gpsimd", (lambda IT, zf: lambda e: e.tensor_copy(
                    out=IT, in_=zf))(IT, zf),
                    waits=[(s_act, s_act.n)], inc=s_pool)
                ch_pool[f] = s_pool.n
                E("vector", (lambda IT, A=A: lambda e: e.tensor_copy(
                    out=A, in_=IT))(IT),
                    waits=[(s_pool, s_pool.n)], inc=s_dve)
                E("vector", (lambda zf, A=A, B=B: lambda e: e.tensor_sub(
                    out=B, in0=zf, in1=A))(zf), inc=s_dve)
                E("vector", (lambda B=B, U0=U0: lambda e: e.tensor_scalar(
                    out=U0, in0=B, scalar1=-2.0, scalar2=None,
                    op0=GE))(), inc=s_dve)
                E("vector", (lambda B=B, U1=U1: lambda e: e.tensor_scalar(
                    out=U1, in0=B, scalar1=-1.0, scalar2=None,
                    op0=GE))(), inc=s_dve)
                E("vector", (lambda U0=U0, U1=U1: lambda e: e.tensor_add(
                    out=U0, in0=U0, in1=U1))(), inc=s_dve)
                E("vector", (lambda B=B, U1=U1: lambda e: e.tensor_scalar(
                    out=U1, in0=B, scalar1=0.0, scalar2=None,
                    op0=GE))(), inc=s_dve)
                E("vector", (lambda B=B, U2=U2: lambda e: e.tensor_scalar(
                    out=U2, in0=B, scalar1=1.0, scalar2=None,
                    op0=GE))(), inc=s_dve)
                E("vector", (lambda U1=U1, U2=U2: lambda e: e.tensor_add(
                    out=U1, in0=U1, in1=U2))(), inc=s_dve)
                E("vector", (lambda U0=U0, U1=U1: lambda e: e.tensor_add(
                    out=U0, in0=U0, in1=U1))(), inc=s_dve)
                E("vector", (lambda B=B, U1=U1: lambda e: e.tensor_scalar(
                    out=U1, in0=B, scalar1=2.0, scalar2=None,
                    op0=GE))(), inc=s_dve)
                E("vector", (lambda U0=U0, U1=U1: lambda e: e.tensor_add(
                    out=U0, in0=U0, in1=U1))(), inc=s_dve)
                E("vector", (lambda A=A, U0=U0, f=f: lambda e:
                   e.scalar_tensor_tensor(out=hq[:, f, :], in0=A,
                                          scalar=-3.0, in1=U0,
                                          op0=ALU.add, op1=ALU.add))(),
                   inc=s_dve)
                ch_dve[f] = s_dve.n
                if th == 0 and f == 0 and "zf" in taps:
                    E("gpsimd", (lambda zf: lambda e: e.dma_start(
                        out=taps["zf"], in_=zf))(zf),
                        waits=[(s_act, zf_act[0])], inc=s_out, inck=16)
                if th == 0 and f == 0 and "hq" in taps:
                    E("gpsimd", lambda e: e.dma_start(out=taps["hq"],
                                                      in_=hq[:, 0, :]),
                      waits=[(s_dve, s_dve.n)], inc=s_out, inck=16)
            chains_done = s_dve.n

            w2cast = {}
            mm2pe = {}
            q3act = 0
            out_cnt = {}
            for sp in range(2):
                tiles = [th * 8 + sp * 4 + i for i in range(4)]
                for f in range(NFC):
                    par = f % 2
                    E("sync", (lambda par, f: lambda e: e.dma_start(
                        out=w2stg[:, par, :],
                        in_=w2_d[f * 128:(f + 1) * 128, :]))(par, f),
                        waits=[(s_pool, w2cast.get((sp, f - 2), 0))],
                        inc=s_w, inck=16)
                    E("gpsimd", (lambda par, f: lambda e: e.tensor_scalar(
                        out=w2bf[:, f % 3, :], in0=w2stg[:, par, :],
                        scalar1=1.0 / 256.0, scalar2=None,
                        op0=ALU.mult))(par, f),
                        waits=[(s_w, s_w.n),
                               (s_pe, mm2pe.get((sp, f - 3), 0))],
                        inc=s_pool)
                    w2cast[(sp, f)] = s_pool.n

                    def mm2fn(f=f, tiles=tiles, th=th):
                        def fx(e):
                            ins = None
                            for i, tt in enumerate(tiles):
                                ps = ps_a if i < 2 else ps_c
                                off = (i % 2) * 1024
                                lt = hq[:, f,
                                        (tt % 8) * 128:(tt % 8) * 128 + 128]
                                for c in range(2):
                                    ins = e.matmul(
                                        ps[:, off + c * 512:
                                           off + c * 512 + 512],
                                        lt,
                                        w2bf[:, f % 3,
                                             c * 512:(c + 1) * 512],
                                        start=(f == 0), stop=False)
                            return ins
                        return fx

                    E("tensor", mm2fn(),
                      waits=[(s_pool, w2cast[(sp, f)]),
                             (s_dve, chains_done), (s_act, q3act)],
                      inc=s_pe)
                    mm2pe[(sp, f)] = s_pe.n

                def biasfn(tiles=tiles):
                    def fx(e):
                        ins = None
                        for i, tt in enumerate(tiles):
                            ps = ps_a if i < 2 else ps_c
                            off = (i % 2) * 1024
                            for c in range(2):
                                ins = e.matmul(
                                    ps[:, off + c * 512:off + c * 512 + 512],
                                    onesbf[:, 0:128],
                                    b2_bf[:, c * 512:(c + 1) * 512],
                                    start=False, stop=True)
                        return ins
                    return fx

                E("tensor", biasfn(), waits=[(s_pool, b2cast)], inc=s_pe)
                mm2_all = s_pe.n

                for i, tt in enumerate(tiles):
                    ps = ps_a if i < 2 else ps_c
                    off = (i % 2) * 1024
                    par = tt % 2
                    E("sync", (lambda tt, par: lambda e: e.dma_start(
                        out=xst[:, par, :],
                        in_=resid_d[tt * 128:(tt + 1) * 128, :]))(tt, par),
                        waits=[(s_res, 16 * NQT),
                               (s_dve, s_dve.n)],
                        inc=s_x, inck=16)
                    ip = i % 2
                    T2 = TP[:, ip, 2, :]
                    T0 = TP[:, ip, 0, :]
                    T1 = TP[:, ip, 1, :]
                    V0 = TB[:, ip, 0, :]
                    V1 = TB[:, ip, 1, :]
                    V2 = TB[:, ip, 2, :]
                    IT = TI[:, ip, :]
                    E("scalar", (lambda ps, off, T2: lambda e: e.activation(
                        out=T2, in_=ps[:, off:off + 1024], func=AF.Identity,
                        bias=consts[:, 2:3], scale=256.0))(ps, off, T2),
                        waits=[(s_pe, mm2_all), (s_dve, s_dve.n),
                               (s_pool, s_pool.n)],
                        inc=s_act)
                    q3act = s_act.n
                    E("gpsimd", (lambda IT, T2: lambda e: e.tensor_copy(
                        out=IT, in_=T2))(IT, T2),
                        waits=[(s_act, s_act.n)], inc=s_pool)
                    E("vector", (lambda IT, T0: lambda e: e.tensor_copy(
                        out=T0, in_=IT))(IT, T0),
                        waits=[(s_pool, s_pool.n)], inc=s_dve)
                    E("vector", (lambda T2, T0, T1: lambda e: e.tensor_sub(
                        out=T1, in0=T2, in1=T0))(T2, T0, T1), inc=s_dve)
                    for j, thr in enumerate((-1.0, 0.0, 1.0)):
                        E("vector", (lambda T1, u, thr=thr: lambda e:
                           e.tensor_scalar(out=u, in0=T1, scalar1=thr,
                                           scalar2=None, op0=ALU.is_ge))(
                               T1, (V0, V1, V2)[j]),
                           inc=s_dve)
                    E("vector", (lambda V0, V1: lambda e: e.tensor_add(
                        out=V0, in0=V0, in1=V1))(V0, V1), inc=s_dve)
                    E("vector", (lambda V0, V2: lambda e: e.tensor_add(
                        out=V0, in0=V0, in1=V2))(V0, V2), inc=s_dve)
                    E("vector", (lambda T0, V0, T1: lambda e:
                       e.scalar_tensor_tensor(out=T1, in0=T0, scalar=-2.0,
                                              in1=V0, op0=ALU.add,
                                              op1=ALU.add))(T0, V0, T1),
                       inc=s_dve)
                    E("vector", (lambda T1, par, tt: lambda e:
                       e.scalar_tensor_tensor(
                           out=outst[:, tt % 2, :], in0=T1,
                           scalar=1.0 / 256.0, in1=xst[:, par, :],
                           op0=ALU.mult, op1=ALU.add))(T1, par, tt),
                       waits=[(s_x, s_x.n),
                              (s_out, out_cnt.get(tt - 2, 0))],
                       inc=s_dve)
                    E("gpsimd", (lambda tt: lambda e: e.dma_start(
                        out=out_d[tt * 128:(tt + 1) * 128, :],
                        in_=outst[:, tt % 2, :]))(tt),
                        waits=[(s_dve, s_dve.n)], inc=s_out, inck=16)
                    out_cnt[tt] = s_out.n
            if th == 0:
                plan.barrier(s_bar)

        E("sync", None, waits=[(s_out, s_out.n)])

        blk = ctx.enter_context(nc.Block())

        @blk.sync
        def _(eng):
            plan.run("sync", eng)

        @blk.tensor
        def _(eng):
            plan.run("tensor", eng)

        @blk.vector
        def _(eng):
            plan.run("vector", eng)

        @blk.scalar
        def _(eng):
            plan.run("scalar", eng)

        @blk.gpsimd
        def _(eng):
            plan.run("gpsimd", eng)

        est.close()
    return nc, sorted(taps.keys())


def kernel(L0_attn_x, L0_attn_Wq, L0_attn_Wk, L0_attn_Wv, L0_attn_Wo,
           L0_attn_bo, L0_ffn_w1, L0_ffn_b1, L0_ffn_w2, L0_ffn_b2):
    x = np.ascontiguousarray(np.asarray(L0_attn_x, dtype=np.float32))
    wts = {k: np.ascontiguousarray(np.asarray(v, dtype=np.float32))
           for k, v in (("wq", L0_attn_Wq), ("wk", L0_attn_Wk),
                        ("wv", L0_attn_Wv), ("wo", L0_attn_Wo),
                        ("bo", L0_attn_bo), ("w1", L0_ffn_w1),
                        ("b1", L0_ffn_b1), ("w2", L0_ffn_w2),
                        ("b2", L0_ffn_b2))}
    nc, tapnames = build_program()
    in_maps = []
    for c in range(8):
        b, qh = c // 2, c % 2
        xq = np.ascontiguousarray(x[b, qh * NQ:(qh + 1) * NQ, :])
        xr = np.ascontiguousarray(x[b, (1 - qh) * NQ:(2 - qh) * NQ, :])
        in_maps.append(dict(wts, xq=xq, xr=xr))
    res = run_bass_kernel_spmd(nc, in_maps, core_ids=list(range(8)))
    out = np.empty((4, 4096, 1024), np.float32)
    for c in range(8):
        out[c // 2, (c % 2) * NQ:(c % 2 + 1) * NQ, :] = res.results[c]["out"]
    if tapnames:
        kernel.taps = [{t: np.asarray(r["tap_" + t]) for t in tapnames}
                       for r in res.results]
    return out


# version 03
# speedup vs baseline: 1.2697x; 1.2697x over previous
import numpy as np
import concourse.bass as bass
import concourse.mybir as mybir
from concourse.bass_utils import run_bass_kernel_spmd
from contextlib import ExitStack

F32 = mybir.dt.float32
BF16 = mybir.dt.bfloat16
I32 = mybir.dt.int32
AF = mybir.ActivationFunctionType
ALU = mybir.AluOpType
AXX = mybir.AxisListType.X

NTILE = 32
NQT = 16
NQ = 2048
NK = 4096
DM = 1024
FF = 4096
NPAIR = 8
NFC = 32
EPS = 1e-5

DEBUG_TAPS = []

_REG = {}


def _ensure_custom_ops():
    if _REG:
        return
    import concourse.dve_ops as dve_ops
    from concourse.dve_op import DveOp
    from concourse.dve_spec import Spec, Src0, Src1, Zero, One, C2, lower
    from concourse.dve_table_gen import dve_ver_for

    have = {o.name: o for o in dve_ops.OPS}

    def _f3_ref(in0, in1, s0, s1, imm2):
        d = in0 - in1
        dt_ = in0.dtype
        ge = lambda t: (d >= t).astype(dt_)
        return ((in1 + dt_.type(imm2)) + ge(-1.0)) + (ge(0.0) + ge(1.0))

    D = Src0 - Src1
    body3 = ((Src1 + C2) + (D >= (Zero - One))) + ((D >= Zero) + (D >= One))

    ver = dve_ver_for("TRN2")
    for name, body, ref in (("MK_FLOOR3", body3, _f3_ref),):
        if name in have:
            _REG[name] = have[name]
            continue
        spec = Spec(body=body)
        sha = lower(spec, ver=ver, perf_en=False, name=name).sha(ver)
        op = DveOp(name, spec, subdim=False, uops_sha={ver: sha}, reference=ref)
        dve_ops.OPS.append(op)
        dve_ops.CUSTOM_DVE_SPECS[name] = spec
        dve_ops.CUSTOM_DVE_REFS[name] = ref
        dve_ops._SUB_OPCODE_FOR_NAME[name] = (
            dve_ops._CUSTOM_DVE_ROW_BASE + len(dve_ops.OPS) - 1)
        _REG[name] = op


class Sem:
    def __init__(self, nc, ctx, name):
        self.h = ctx.enter_context(nc.semaphore(name))
        self.n = 0
        self.name = name


class Plan:
    ENGS = ("sync", "tensor", "vector", "scalar", "gpsimd")

    def __init__(self):
        self.q = {e: [] for e in self.ENGS}
        self.lastw = {e: {} for e in self.ENGS}

    def emit(self, eng, fn, waits=(), inc=None, inck=1):
        lw = self.lastw[eng]
        real = []
        for s, v in waits:
            if v > 0 and lw.get(s.name, -1) < v:
                real.append((s, v))
                lw[s.name] = v
        if inc is not None:
            inc.n += inck
        self.q[eng].append((real, fn, inc, inck))

    def barrier(self, s):
        target = s.n + 5
        for e in self.ENGS:
            self.q[e].append(([], lambda eng: eng.nop(), s, 1))
        s.n = target
        for e in self.ENGS:
            self.q[e].append(([(s, target)], None, None, 0))
            self.lastw[e][s.name] = target

    def run(self, eng_name, eng):
        for waits, fn, inc, k in self.q[eng_name]:
            for s, v in waits:
                eng.wait_ge(s.h, v)
            if fn is None:
                continue
            ins = fn(eng)
            if inc is not None and ins is not None:
                ins.then_inc(inc.h, k)


def build_program():
    _ensure_custom_ops()
    F3 = _REG["MK_FLOOR3"]
    nc = bass.Bass("TRN2", target_bir_lowering=False, debug=False)

    xq_d = nc.dram_tensor("xq", [NQ, DM], F32, kind="ExternalInput").ap()
    xr_d = nc.dram_tensor("xr", [NQ, DM], F32, kind="ExternalInput").ap()
    wq_d = nc.dram_tensor("wq", [DM, DM], F32, kind="ExternalInput").ap()
    wk_d = nc.dram_tensor("wk", [DM, DM], F32, kind="ExternalInput").ap()
    wv_d = nc.dram_tensor("wv", [DM, DM], F32, kind="ExternalInput").ap()
    wo_d = nc.dram_tensor("wo", [DM, DM], F32, kind="ExternalInput").ap()
    bo_d = nc.dram_tensor("bo", [DM], F32, kind="ExternalInput").ap()
    w1_d = nc.dram_tensor("w1", [DM, FF], F32, kind="ExternalInput").ap()
    b1_d = nc.dram_tensor("b1", [FF], F32, kind="ExternalInput").ap()
    w2_d = nc.dram_tensor("w2", [FF, DM], F32, kind="ExternalInput").ap()
    b2_d = nc.dram_tensor("b2", [DM], F32, kind="ExternalInput").ap()
    out_d = nc.dram_tensor("out", [NQ, DM], F32, kind="ExternalOutput").ap()
    resid_d = nc.dram_tensor("residsc", [NQ, DM], F32, kind="Internal").ap()
    rec_d = nc.dram_tensor("recsc", [2, NQ], F32, kind="Internal").ap()

    taps = {}

    def mktap(name, shape, dt):
        if name in DEBUG_TAPS:
            taps[name] = nc.dram_tensor("tap_" + name, shape, dt,
                                        kind="ExternalOutput").ap()

    mktap("xn", [128, 8, 256], BF16)
    mktap("qT", [128, 256], BF16)
    mktap("kT", [128, 256], BF16)
    mktap("vA", [128, 132], BF16)
    mktap("pt", [128, NQ], BF16)
    mktap("avT", [128, 8, 256], BF16)
    mktap("r1", [128, DM], F32)
    mktap("x2T", [128, 8, 256], BF16)
    mktap("hq", [128, 1024], BF16)
    mktap("zf", [128, 1024], F32)
    mktap("resfull", [NQ, DM], F32)
    mktap("x2full", [128, 8, NQ], BF16)
    mktap("hqfull", [128, NFC, 1024], BF16)
    mktap("tn", [128, 1024], F32)
    mktap("tk4", [128, 1024], F32)
    mktap("tr", [128, 1024], F32)
    mktap("ty", [128, 1024], F32)

    with ExitStack() as ctx:
        S = lambda n: Sem(nc, ctx, n)
        s_x = S("s_x")
        s_w = S("s_w")
        s_wb = S("s_wb")
        s_wq3 = [S("s_wq0"), S("s_wq1"), S("s_wq2")]
        s_tp = S("s_tp")
        s_rec = S("s_rec")
        s_res = S("s_res")
        s_out = S("s_out")
        s_pe = S("s_pe")
        s_dve = S("s_dve")
        s_act = S("s_act")
        s_pool = S("s_pool")
        s_exp = S("s_exp")
        s_bar = S("s_bar")

        sb = lambda n, shp, dt=F32: ctx.enter_context(nc.sbuf_tensor(n, shp, dt))
        consts = sb("consts", [128, 8])
        onesbf = sb("onesbf", [1, 128], BF16)
        xst = sb("xst", [128, 2, DM])
        stats = sb("stats", [128, 40])
        mv = sb("mv", [128, 4])
        outst = sb("outst", [128, 2, DM])
        b1t = sb("b1t", [128, 2])
        b1s = sb("b1s", [128, 2])
        xn2T = sb("xn2T", [128, 8, NQ], BF16)
        ps_a = ctx.enter_context(nc.psum_tensor("ps_a", [128, 2048], F32))

        plan = Plan()
        E = plan.emit

        def _init(gp):
            gp.memset(consts[:, 0:1], EPS)
            gp.memset(consts[:, 1:2], -1.0)
            gp.memset(consts[:, 2:3], 0.5)
            gp.memset(consts[:, 3:4], 0.25)
            return gp.memset(onesbf[:, :], 1.0)

        E("gpsimd", _init, inc=s_pool)
        init_done = s_pool.n

        def plan_ln(xv, pre_waits, sq_ps):
            E("vector", (lambda xv: lambda e: e.tensor_reduce(
                out=mv[:, 0:1], in_=xv, axis=AXX, op=ALU.add))(xv),
                waits=list(pre_waits), inc=s_dve)
            E("scalar", (lambda xv, sq_ps: lambda e: e.activation(
                out=sq_ps, in_=xv, func=AF.Square, bias=0.0, scale=1.0))(
                    xv, sq_ps),
                waits=[(s_dve, s_dve.n)], inc=s_act)
            E("vector", (lambda sq_ps: lambda e: e.tensor_reduce(
                out=mv[:, 1:2], in_=sq_ps, axis=AXX, op=ALU.add))(sq_ps),
                waits=[(s_act, s_act.n)], inc=s_dve)
            E("vector", lambda e: e.tensor_scalar(
                out=mv[:, 0:1], in0=mv[:, 0:1], scalar1=1.0 / 1024.0,
                scalar2=None, op0=ALU.mult),
                waits=[(s_dve, s_dve.n)], inc=s_dve)
            E("scalar", lambda e: e.activation(
                out=mv[:, 2:3], in_=mv[:, 0:1], func=AF.Square,
                bias=0.0, scale=1.0),
                waits=[(s_dve, s_dve.n)], inc=s_act)
            E("vector", lambda e: e.scalar_tensor_tensor(
                out=mv[:, 1:2], in0=mv[:, 1:2], scalar=1.0 / 1024.0,
                in1=mv[:, 2:3], op0=ALU.mult, op1=ALU.subtract),
                waits=[(s_act, s_act.n)], inc=s_dve)
            E("scalar", lambda e: e.activation(
                out=mv[:, 2:3], in_=mv[:, 1:2], func=AF.Ln,
                bias=consts[:, 0:1], scale=1.0),
                waits=[(s_dve, s_dve.n)], inc=s_act)
            E("scalar", lambda e: e.activation(
                out=mv[:, 3:4], in_=mv[:, 2:3], func=AF.Exp,
                bias=0.0, scale=-0.5),
                waits=[(s_act, s_act.n)], inc=s_act)
            return s_act.n


        ab2 = ExitStack()
        sb2 = lambda n, shp, dt=F32: ab2.enter_context(nc.sbuf_tensor(n, shp, dt))
        recb = sb2("recb", [64, NQ])
        avT = sb2("avT", [128, 8, NQ], BF16)
        ab1 = ExitStack()
        sb1 = lambda n, shp, dt=F32: ab1.enter_context(nc.sbuf_tensor(n, shp, dt))
        xnT = sb1("xnT", [128, 8, NK], BF16)
        xntok = sb1("xntok", [128, 2, DM], BF16)
        wstg = sb1("wstg", [128, 3, DM])
        wqbf = sb1("wqbf", [128, 8, 128], BF16)
        wkbf = sb1("wkbf", [128, 8, 128], BF16)
        wvbf = sb1("wvbf", [128, 8, 128], BF16)
        qT = sb1("qT", [128, NQ], BF16)
        kT = sb1("kT", [128, NK], BF16)
        vstg = sb1("vstg", [128, 2, 512], BF16)
        vtok = sb1("vtok", [128, 2, 128], BF16)
        vAug = sb1("vAug", [128, 32, 132], BF16)
        ptile = sb1("ptile", [128, 2, NQ], BF16)

        recsb = outst[0:1, :, :].rearrange("p a b -> p (a b)")
        avsb = xst[0:64, :, :].rearrange("p a b -> p (a b)")
        ps_b_st = ExitStack()
        ps_b = ps_b_st.enter_context(nc.psum_tensor("ps_b", [128, 2048], F32))

        def _vaug_ones(gp):
            gp.memset(vAug[:, :, 64:65], 1.0)
            return gp.memset(vAug[:, :, 130:131], 1.0)

        E("gpsimd", _vaug_ones, inc=s_pool)
        vaug_ones_cnt = s_pool.n

        dve_norm = {}
        tp_tile = {}
        act_rstd = {}
        for tt in range(NTILE):
            src = xq_d if tt < NQT else xr_d
            row = (tt % NQT) * 128
            par = tt % 2
            E("sync", (lambda par, s0: lambda e: e.dma_start(
                out=xst[:, par, :], in_=s0))(par, src[row:row + 128, :]),
                waits=[(s_dve, dve_norm.get(tt - 2, 0)),
                       (s_pool, init_done)], inc=s_x, inck=16)
            xv = xst[:, par, :]
            act_rstd[tt] = plan_ln(xv, [(s_x, s_x.n),
                                        (s_act, act_rstd.get(tt - 1, 0))],
                                   ps_a[:, 0:1024])
            E("vector", (lambda xv, par: lambda e: e.tensor_scalar(
                out=xntok[:, par, :], in0=xv, scalar1=mv[:, 0:1],
                scalar2=mv[:, 3:4], op0=ALU.subtract, op1=ALU.mult))(xv, par),
                waits=[(s_act, s_act.n), (s_tp, tp_tile.get(tt - 2, 0))],
                inc=s_dve)
            dve_norm[tt] = s_dve.n
            for a in range(8):
                E("sync", (lambda a, par, tt: lambda e: e.dma_start(
                    out=xnT[:, a, tt * 128:(tt + 1) * 128],
                    in_=xntok[:, par, a * 128:(a + 1) * 128], transpose=True))(
                        a, par, tt),
                    waits=[(s_dve, s_dve.n)], inc=s_tp, inck=16)
            tp_tile[tt] = s_tp.n
        tp_after_A = s_tp.n

        if "xn" in taps:
            E("gpsimd", lambda e: e.dma_start(out=taps["xn"],
                                              in_=xnT[:, :, 0:256]),
              waits=[(s_tp, tp_after_A)], inc=s_out, inck=16)

        wbufs = (wqbf, wkbf, wvbf)
        wsrc = (wq_d, wk_d, wv_d)
        cast_cnt = {}
        wdma_cnt = {}

        def plan_wdma(p):
            for wi in range(3):
                sidx = 3 * p + wi
                spar = sidx % 3
                src = wsrc[wi][:, p * 128:(p + 1) * 128].rearrange(
                    "(a q) m -> q a m", q=128)
                E("sync", (lambda spar, src: lambda e: e.dma_start(
                    out=wstg[:, spar, :].rearrange("q (a m) -> q a m", a=8),
                    in_=src))(spar, src),
                    waits=[(s_pool, cast_cnt.get(sidx - 3, 0))],
                    inc=s_wq3[wi], inck=16)
                wdma_cnt[sidx] = s_wq3[wi].n

        def plan_wcast(p, pe_guard):
            for wi in range(3):
                sidx = 3 * p + wi
                spar = sidx % 3
                E("vector", (lambda spar, wi: lambda e: e.tensor_copy(
                    out=wbufs[wi][:, :, :],
                    in_=wstg[:, spar, :].rearrange("q (a m) -> q a m", a=8)))(
                        spar, wi),
                    waits=[(s_wq3[wi], wdma_cnt[sidx]), (s_pe, pe_guard)],
                    inc=s_pool)
                cast_cnt[sidx] = s_pool.n

        plan_wdma(0)
        plan_wcast(0, 0)
        proj_pe_done = 0
        exp_half = {}

        for p in range(NPAIR):
            pp = p % 2
            if p + 1 < NPAIR:
                plan_wdma(p + 1)
            for wi, (wbf, dst, nch) in enumerate(((wqbf, qT, 4), (wkbf, kT, 8))):
                need_w = cast_cnt[3 * p + wi]
                for c in range(nch):
                    bank = c % 4

                    def mmfn(wbf=wbf, c=c, bank=bank):
                        def f(e):
                            ins = None
                            for a in range(8):
                                ins = e.matmul(
                                    ps_a[:, bank * 512:(bank + 1) * 512],
                                    wbf[:, a, :],
                                    xnT[:, a, c * 512:(c + 1) * 512],
                                    start=(a == 0), stop=(a == 7))
                            return ins
                        return f

                    E("tensor", mmfn(),
                      waits=[(s_pool, need_w), (s_tp, tp_after_A),
                             (s_exp, s_exp.n), (s_dve, s_dve.n)],
                      inc=s_pe)
                    E("vector", (lambda dst, c, bank: lambda e: e.tensor_copy(
                        out=dst[:, c * 512:(c + 1) * 512],
                        in_=ps_a[:, bank * 512:(bank + 1) * 512]))(dst, c, bank),
                        waits=[(s_pe, s_pe.n)], inc=s_dve)
            need_w = cast_cnt[3 * p + 2]
            for c in range(8):
                bank = c % 4

                def mmfn(c=c, bank=bank):
                    def f(e):
                        ins = None
                        for a in range(8):
                            ins = e.matmul(
                                ps_a[:, bank * 512:(bank + 1) * 512],
                                wvbf[:, a, :],
                                xnT[:, a, c * 512:(c + 1) * 512],
                                start=(a == 0), stop=(a == 7))
                        return ins
                    return f

                E("tensor", mmfn(),
                  waits=[(s_pool, need_w), (s_exp, s_exp.n),
                         (s_dve, s_dve.n), (s_tp, s_tp.n)],
                  inc=s_pe)
                E("vector", (lambda c, bank: lambda e: e.tensor_copy(
                    out=vstg[:, c % 2, :],
                    in_=ps_a[:, bank * 512:(bank + 1) * 512]))(c, bank),
                    waits=[(s_pe, s_pe.n), (s_tp, s_tp.n - (64 if c >= 2 else 0))],
                    inc=s_dve)
                for j in range(4):
                    kt = 4 * c + j
                    E("sync", (lambda c, j, kt: lambda e: e.dma_start(
                        out=vtok[:, kt % 2, :],
                        in_=vstg[:, c % 2, j * 128:(j + 1) * 128],
                        transpose=True))(c, j, kt),
                        waits=[(s_dve, s_dve.n), (s_pool, vaug_ones_cnt)],
                        inc=s_tp, inck=16)
                    E("vector", (lambda kt: lambda e: e.tensor_copy(
                        out=vAug[:, kt, :].rearrange(
                            "p (g c) -> p g c", c=66)[:, :, 0:64],
                        in_=vtok[:, kt % 2, :].rearrange(
                            "p (g c) -> p g c", c=64)))(kt),
                        waits=[(s_tp, s_tp.n)], inc=s_dve)
            qkv_done = s_dve.n
            vaug_done = s_tp.n
            proj_pe_done = s_pe.n
            if p + 1 < NPAIR:
                plan_wcast(p + 1, proj_pe_done)

            if p == 0 and "qT" in taps:
                E("gpsimd", lambda e: e.dma_start(out=taps["qT"],
                                                  in_=qT[:, 0:256]),
                  waits=[(s_dve, qkv_done)], inc=s_out, inck=16)
            if p == 0 and "kT" in taps:
                E("gpsimd", lambda e: e.dma_start(out=taps["kT"],
                                                  in_=kT[:, 0:256]),
                  waits=[(s_dve, qkv_done)], inc=s_out, inck=16)
            if p == 0 and "vA" in taps:
                E("gpsimd", lambda e: e.dma_start(
                    out=taps["vA"], in_=vAug[:, 0, :]),
                    waits=[(s_tp, vaug_done)], inc=s_out, inck=16)

            for h in range(2):
                hb = 64 * h
                psb_free_act = s_act.n
                psb_free_dve = s_dve.n
                for kt in range(32):
                    kpar = kt % 2
                    for half in range(2):
                        def scfn(hb=hb, kt=kt, half=half):
                            def f(e):
                                ins = None
                                for qc in (2 * half, 2 * half + 1):
                                    ins = e.matmul(
                                        ps_a[:, qc * 512:(qc + 1) * 512],
                                        kT[hb:hb + 64,
                                           kt * 128:(kt + 1) * 128],
                                        qT[hb:hb + 64,
                                           qc * 512:(qc + 1) * 512],
                                        start=True, stop=True)
                                return ins
                            return f

                        scw = [(s_exp, exp_half.get(("ps", half), 0)),
                               (s_dve, qkv_done), (s_tp, vaug_done)]
                        E("tensor", scfn(), waits=scw, inc=s_pe)
                        E("scalar", (lambda kpar, half: lambda e: e.activation(
                            out=ptile[:, kpar, half * 1024:half * 1024 + 1024],
                            in_=ps_a[:, half * 1024:half * 1024 + 1024],
                            func=AF.Exp, bias=consts[:, 1:2],
                            scale=1.0 / 32.0))(kpar, half),
                            waits=[(s_pe, s_pe.n)], inc=s_exp)
                        exp_half[("ps", half)] = s_exp.n

                    def avfn(h=h, kt=kt, kpar=kpar):
                        base = 0 if h == 0 else 66

                        def f(e):
                            ins = None
                            for qc in range(4):
                                ins = e.matmul(
                                    ps_b[0:65, qc * 512:(qc + 1) * 512],
                                    vAug[:, kt, base:base + 65],
                                    ptile[:, kpar, qc * 512:(qc + 1) * 512],
                                    start=(kt == 0), stop=(kt == 31))
                            return ins
                        return f

                    avw = [(s_exp, s_exp.n)]
                    if kt == 0:
                        avw += [(s_act, psb_free_act), (s_dve, psb_free_dve)]
                    E("tensor", avfn(), waits=avw, inc=s_pe)
                av_done = s_pe.n
                if p == 0 and h == 0 and "pt" in taps:
                    E("gpsimd", lambda e: e.dma_start(out=taps["pt"],
                                                      in_=ptile[:, 1, :]),
                      waits=[(s_exp, s_exp.n)], inc=s_out, inck=16)
                hpar = h
                E("vector", lambda e: e.reciprocal(
                    out=recsb, in_=ps_b[64:65, 0:2048]),
                    waits=[(s_pe, av_done)], inc=s_dve)
                E("gpsimd", (lambda hpar: lambda e: e.dma_start(
                    out=rec_d[hpar, None, :], in_=recsb))(hpar),
                    waits=[(s_dve, s_dve.n)], inc=s_rec, inck=16)
                E("gpsimd", (lambda hpar: lambda e: e.dma_start(
                    out=recb[:, :],
                    in_=rec_d[hpar, None, :].broadcast_to([64, NQ])))(hpar),
                    waits=[(s_rec, s_rec.n), (s_dve, s_dve.n)],
                    inc=s_rec, inck=16)
                E("scalar", lambda e: e.copy(out=avsb[:, :],
                                             in_=ps_b[0:64, 0:2048]),
                  waits=[(s_pe, av_done)], inc=s_act)
                E("vector", (lambda p, hb: lambda e: e.tensor_mul(
                    out=avT[hb:hb + 64, p, :], in0=avsb[:, :],
                    in1=recb[:, :]))(p, hb),
                    waits=[(s_act, s_act.n), (s_rec, s_rec.n)], inc=s_dve)
        avT_done = s_dve.n
        if "avT" in taps:
            E("gpsimd", lambda e: e.dma_start(out=taps["avT"],
                                              in_=avT[:, :, 0:256]),
              waits=[(s_dve, avT_done)], inc=s_out, inck=16)

        E("sync", None, waits=[(s_tp, s_tp.n), (s_w, s_w.n), (s_x, s_x.n),
                               (s_wb, s_wb.n), (s_rec, s_rec.n),
                               (s_res, s_res.n), (s_out, s_out.n),
                               (s_wq3[0], s_wq3[0].n), (s_wq3[1], s_wq3[1].n),
                               (s_wq3[2], s_wq3[2].n)])
        plan.barrier(s_bar)
        ab1.close()

        cst_tmp = ExitStack()
        sbc = lambda n, shp, dt=F32: cst_tmp.enter_context(
            nc.sbuf_tensor(n, shp, dt))
        wostg = sbc("wostg", [128, 2, DM])
        wobf = sbc("wobf", [128, 8, DM], BF16)
        x2tok = sbc("x2tok", [128, 2, DM], BF16)
        r1t = sbc("r1t", [128, 2, DM])
        bo_bf = sbc("bo_bf", [1, DM], BF16)
        E("sync", lambda e: e.dma_start(out=r1t[0:1, 0, :], in_=bo_d[None, :]),
          inc=s_wb, inck=16)
        E("vector", lambda e: e.tensor_copy(out=bo_bf[:, :], in_=r1t[0:1, 0, :]),
          waits=[(s_wb, s_wb.n)], inc=s_pool)
        bocast = s_pool.n

        for a in range(8):
            par = a % 2
            E("sync", (lambda a, par: lambda e: e.dma_start(
                out=wostg[:, par, :],
                in_=wo_d[a * 128:(a + 1) * 128, :]))(a, par),
                waits=[(s_pool, s_pool.n if a >= 2 else 0)],
                inc=s_w, inck=16)
            E("vector", (lambda a, par: lambda e: e.tensor_copy(
                out=wobf[:, a, :], in_=wostg[:, par, :]))(a, par),
                waits=[(s_w, s_w.n)], inc=s_pool)
        wo_cast = s_pool.n

        r1_cnt = {}
        res_cnt = {}
        x2tp = {}
        rstd2 = {}
        for tt in range(NQT):
            par = tt % 2
            E("sync", (lambda tt, par: lambda e: e.dma_start(
                out=xst[:, par, :],
                in_=xq_d[tt * 128:(tt + 1) * 128, :]))(tt, par),
                waits=[(s_dve, max(avT_done, r1_cnt.get(tt - 2, 0)))],
                inc=s_x, inck=16)

            def wofn(tt=tt, par=par):
                def f(e):
                    ins = None
                    for nc2 in range(2):
                        sl = slice(par * 1024 + nc2 * 512,
                                   par * 1024 + nc2 * 512 + 512)
                        for a in range(8):
                            e.matmul(ps_a[:, sl],
                                     avT[:, a, tt * 128:(tt + 1) * 128],
                                     wobf[:, a, nc2 * 512:(nc2 + 1) * 512],
                                     start=(a == 0), stop=False)
                        ins = e.matmul(ps_a[:, sl], onesbf[:, 0:128],
                                       bo_bf[:, nc2 * 512:(nc2 + 1) * 512],
                                       start=False, stop=True)
                    return ins
                return f

            E("tensor", wofn(),
              waits=[(s_pool, max(wo_cast, bocast)),
                     (s_dve, r1_cnt.get(tt - 2, 0)), (s_exp, s_exp.n)],
              inc=s_pe)
            pe_wo = s_pe.n
            E("vector", (lambda par: lambda e: e.tensor_add(
                out=r1t[:, par, :],
                in0=ps_a[:, par * 1024:par * 1024 + 1024],
                in1=xst[:, par, :]))(par),
                waits=[(s_pe, pe_wo), (s_x, s_x.n),
                       (s_res, res_cnt.get(tt - 2, 0)),
                       (s_tp, x2tp.get(tt - 2, 0)),
                       (s_act, rstd2.get(tt - 2, 0)),
                       (s_pool, bocast)],
                inc=s_dve)
            r1_cnt[tt] = s_dve.n
            E("gpsimd", (lambda tt, par: lambda e: e.dma_start(
                out=resid_d[tt * 128:(tt + 1) * 128, :],
                in_=r1t[:, par, :]))(tt, par),
                waits=[(s_dve, s_dve.n)], inc=s_res, inck=16)
            res_cnt[tt] = s_res.n
            if tt == 0 and "r1" in taps:
                E("gpsimd", lambda e: e.dma_start(out=taps["r1"],
                                                  in_=r1t[:, 0, :]),
                  waits=[(s_dve, s_dve.n)], inc=s_out, inck=16)
            rstd2[tt] = plan_ln(r1t[:, par, :],
                                 [(s_dve, s_dve.n),
                                  (s_act, rstd2.get(tt - 1, 0))],
                                 ps_b[:, 0:1024])
            E("vector", (lambda par: lambda e: e.tensor_scalar(
                out=x2tok[:, par, :], in0=r1t[:, par, :],
                scalar1=mv[:, 0:1], scalar2=mv[:, 3:4],
                op0=ALU.subtract, op1=ALU.mult))(par),
                waits=[(s_act, s_act.n), (s_tp, x2tp.get(tt - 2, 0))],
                inc=s_dve)
            for a in range(8):
                E("sync", (lambda a, par, tt: lambda e: e.dma_start(
                    out=xn2T[:, a, tt * 128:(tt + 1) * 128],
                    in_=x2tok[:, par, a * 128:(a + 1) * 128],
                    transpose=True))(a, par, tt),
                    waits=[(s_dve, s_dve.n)], inc=s_tp, inck=16)
            x2tp[tt] = s_tp.n
        xn2_done = s_tp.n
        if "resfull" in taps:
            E("gpsimd", lambda e: e.dma_start(out=taps["resfull"], in_=resid_d),
              waits=[(s_res, s_res.n)], inc=s_out, inck=16)
        if "x2full" in taps:
            E("gpsimd", lambda e: e.dma_start(out=taps["x2full"], in_=xn2T[:, :, :]),
              waits=[(s_tp, xn2_done)], inc=s_out, inck=16)
        if "x2T" in taps:
            E("gpsimd", lambda e: e.dma_start(out=taps["x2T"],
                                              in_=xn2T[:, :, 0:256]),
              waits=[(s_tp, xn2_done)], inc=s_out, inck=16)

        E("sync", None, waits=[(s_tp, s_tp.n), (s_w, s_w.n), (s_x, s_x.n),
                               (s_wb, s_wb.n), (s_rec, s_rec.n),
                               (s_res, s_res.n), (s_out, s_out.n),
                               (s_wq3[0], s_wq3[0].n), (s_wq3[1], s_wq3[1].n),
                               (s_wq3[2], s_wq3[2].n)])
        plan.barrier(s_bar)
        cst_tmp.close()
        ab2.close()
        ps_b_st.close()

        est = ExitStack()
        sbe = lambda n, shp, dt=F32: est.enter_context(
            nc.sbuf_tensor(n, shp, dt))
        hq = sbe("hq", [128, NFC, 1024], BF16)
        w1stg = sbe("w1stg", [128, 2, DM])
        w1bf = sbe("w1bf", [128, 2, 8, 128], BF16)
        w2stg = sbe("w2stg", [128, 2, DM])
        w2bf = sbe("w2bf", [128, 3, DM], BF16)
        TP = sbe("tpool", [128, 2, 4, 1024])
        TB = sbe("tbit", [128, 2, 3, 1024], BF16)
        TI = sbe("tint", [128, 2, 1024], I32)
        zfb = sbe("zfb", [128, 2, 1024])
        b2_bf = sbe("b2_bf", [1, DM], BF16)
        E("sync", lambda e: e.dma_start(out=xst[0:1, 0, :], in_=b2_d[None, :]),
          waits=[(s_res, s_res.n)], inc=s_wb, inck=16)
        E("vector", lambda e: e.tensor_copy(out=b2_bf[:, :], in_=xst[0:1, 0, :]),
          waits=[(s_wb, s_wb.n)], inc=s_pool)
        b2cast = s_pool.n
        ps_c = est.enter_context(nc.psum_tensor("ps_c", [128, 2048], F32))

        for th in range(2):
            qoff = th * 1024
            w1cast = {}
            zf_act = {}
            ch_dve = {}
            ch_pool = {}
            mm1pe = {}
            b1s_dve = {}
            for f in range(NFC):
                par = f % 2
                src = w1_d[:, f * 128:(f + 1) * 128].rearrange(
                    "(a q) m -> q a m", q=128)
                E("sync", (lambda par, src: lambda e: e.dma_start(
                    out=w1stg[:, par, :].rearrange("q (a m) -> q a m", a=8),
                    in_=src))(par, src),
                    waits=[(s_pool, w1cast.get(f - 2, 0))],
                    inc=s_w, inck=16)
                E("sync", (lambda par, f: lambda e: e.dma_start(
                    out=b1t[:, par:par + 1],
                    in_=b1_d[f * 128:(f + 1) * 128].rearrange(
                        "(q o) -> q o", o=1)))(par, f),
                    waits=[(s_dve, b1s_dve.get(f - 2, 0))],
                    inc=s_wb, inck=16)
                E("vector", (lambda par: lambda e: e.tensor_copy(
                    out=w1bf[:, par, :, :],
                    in_=w1stg[:, par, :].rearrange("q (a m) -> q a m",
                                                   a=8)))(par),
                    waits=[(s_w, s_w.n),
                           (s_pe, mm1pe.get(f - 2, 0))], inc=s_pool)
                w1cast[f] = s_pool.n
                E("vector", (lambda par: lambda e: e.tensor_scalar(
                    out=b1s[:, par:par + 1], in0=b1t[:, par:par + 1],
                    scalar1=256.0, scalar2=0.5, op0=ALU.mult,
                    op1=ALU.add))(par),
                    waits=[(s_wb, s_wb.n)], inc=s_dve)
                b1s_cnt = s_dve.n
                b1s_dve[f] = s_dve.n

                def mm1fn(par=par, qoff=qoff):
                    def fx(e):
                        ins = None
                        for c in range(2):
                            sl = slice(par * 1024 + c * 512,
                                       par * 1024 + c * 512 + 512)
                            for a in range(8):
                                ins = e.matmul(
                                    ps_a[:, sl], w1bf[:, par, a, :],
                                    xn2T[:, a,
                                         qoff + c * 512:qoff + c * 512 + 512],
                                    start=(a == 0), stop=(a == 7))
                        return ins
                    return fx

                E("tensor", mm1fn(),
                  waits=[(s_pool, w1cast[f]), (s_tp, xn2_done),
                         (s_act, zf_act.get(f - 2, 0))],
                  inc=s_pe)
                mm1pe[f] = s_pe.n

                T = lambda i: TP[:, par, i, :]
                IT = TI[:, par, :]
                zf = zfb[:, par, :]
                wch = [(s_pe, s_pe.n), (s_dve, b1s_cnt),
                       (s_dve, ch_dve.get(f - 2, 0)),
                       (s_pool, ch_pool.get(f - 2, 0))]
                E("scalar", (lambda zf, par: lambda e: e.activation(
                    out=zf, in_=ps_a[:, par * 1024:par * 1024 + 1024],
                    func=AF.Identity, bias=b1s[:, par:par + 1],
                    scale=256.0))(zf, par),
                    waits=wch, inc=s_act)
                zf_act[f] = s_act.n
                A, B, C, D = T(0), T(1), T(2), T(3)
                U = lambda i: TB[:, par, i, :]
                U0, U1, U2 = U(0), U(1), U(2)
                GE = ALU.is_ge
                E("vector", (lambda IT, zf: lambda e: e.tensor_copy(
                    out=IT, in_=zf))(IT, zf),
                    waits=[(s_act, s_act.n)], inc=s_dve)
                E("vector", (lambda IT, A=A: lambda e: e.tensor_copy(
                    out=A, in_=IT))(IT), inc=s_dve)
                E("vector", (lambda zf, A=A, B=B: lambda e: e.tensor_sub(
                    out=B, in0=zf, in1=A))(zf), inc=s_dve)
                for i, thr in enumerate((-1.0, 0.0, 1.0)):
                    E("vector", (lambda B=B, u=U(i), thr=thr: lambda e:
                       e.tensor_scalar(out=u, in0=B, scalar1=thr,
                                       scalar2=None, op0=GE))(),
                       inc=s_dve)
                E("vector", (lambda U0=U0, U1=U1: lambda e: e.tensor_add(
                    out=U0, in0=U0, in1=U1))(), inc=s_dve)
                E("vector", (lambda U0=U0, U2=U2: lambda e: e.tensor_add(
                    out=U0, in0=U0, in1=U2))(), inc=s_dve)
                E("vector", (lambda A=A, U0=U0, C=C: lambda e:
                   e.scalar_tensor_tensor(out=C, in0=A, scalar=-2.0,
                                          in1=U0, op0=ALU.add,
                                          op1=ALU.add))(), inc=s_dve)
                if th == 0 and f == 0 and "tn" in taps:
                    E("gpsimd", (lambda C=C: lambda e: e.dma_start(
                        out=taps["tn"], in_=C))(),
                        waits=[(s_dve, s_dve.n)], inc=s_out, inck=16)
                E("vector", (lambda zf, B=B: lambda e: e.tensor_scalar(
                    out=B, in0=zf, scalar1=1.0 / 64.0, scalar2=None,
                    op0=ALU.mult))(zf), inc=s_dve)
                E("vector", (lambda IT, B=B: lambda e: e.tensor_copy(
                    out=IT, in_=B))(IT), inc=s_dve)
                E("vector", (lambda IT, A=A: lambda e: e.tensor_copy(
                    out=A, in_=IT))(IT), inc=s_dve)
                E("vector", (lambda B=B, A=A, U0=U0: lambda e:
                   e.tensor_tensor(out=U0, in0=B, in1=A, op=GE))(),
                   inc=s_dve)
                E("vector", (lambda A=A, U0=U0, D=D: lambda e:
                   e.scalar_tensor_tensor(out=D, in0=A, scalar=-1.0,
                                          in1=U0, op0=ALU.add,
                                          op1=ALU.add))(), inc=s_dve)
                E("vector", (lambda D=D, C=C, B=B: lambda e:
                   e.scalar_tensor_tensor(out=B, in0=D, scalar=-64.0,
                                          in1=C, op0=ALU.mult,
                                          op1=ALU.add))(), inc=s_dve)
                dve_r = s_dve.n
                if th == 0 and f == 0 and "tr" in taps:
                    E("gpsimd", (lambda D=D: lambda e: e.dma_start(
                        out=taps["tk4"], in_=D))(),
                        waits=[(s_dve, s_dve.n)], inc=s_out, inck=16)
                    E("gpsimd", (lambda B=B: lambda e: e.dma_start(
                        out=taps["tr"], in_=B))(),
                        inc=s_out, inck=16)
                E("scalar", (lambda D=D, A=A: lambda e: e.activation(
                    out=A, in_=D, func=AF.Gelu, bias=0.0, scale=0.25))(),
                    waits=[(s_dve, dve_r)], inc=s_act)
                E("scalar", (lambda D=D, C=C: lambda e: e.activation(
                    out=C, in_=D, func=AF.Gelu, bias=consts[:, 3:4],
                    scale=0.25))(), inc=s_act)
                act_g = s_act.n
                E("vector", (lambda C=C, A=A: lambda e: e.tensor_sub(
                    out=C, in0=C, in1=A))(),
                    waits=[(s_act, act_g)], inc=s_dve)
                E("vector", (lambda C=C, B=B: lambda e:
                   e.scalar_tensor_tensor(out=C, in0=C, scalar=1.0 / 64.0,
                                          in1=B, op0=ALU.mult,
                                          op1=ALU.mult))(), inc=s_dve)
                E("vector", (lambda C=C, A=A: lambda e: e.tensor_add(
                    out=C, in0=C, in1=A))(), inc=s_dve)
                dve_y = s_dve.n
                if th == 0 and f == 0 and "ty" in taps:
                    E("gpsimd", (lambda C=C: lambda e: e.dma_start(
                        out=taps["ty"], in_=C))(),
                        waits=[(s_dve, s_dve.n)], inc=s_out, inck=16)
                E("scalar", (lambda zf, C=C: lambda e: e.activation(
                    out=zf, in_=C, func=AF.Identity, bias=consts[:, 2:3],
                    scale=256.0))(zf),
                    waits=[(s_dve, dve_y)], inc=s_act)
                E("vector", (lambda IT, zf: lambda e: e.tensor_copy(
                    out=IT, in_=zf))(IT, zf),
                    waits=[(s_act, s_act.n)], inc=s_dve)
                ch_pool[f] = s_pool.n
                E("vector", (lambda IT, A=A: lambda e: e.tensor_copy(
                    out=A, in_=IT))(IT), inc=s_dve)
                E("vector", (lambda zf, A=A, B=B: lambda e: e.tensor_sub(
                    out=B, in0=zf, in1=A))(zf), inc=s_dve)
                E("vector", (lambda B=B, U0=U0: lambda e: e.tensor_scalar(
                    out=U0, in0=B, scalar1=-2.0, scalar2=None,
                    op0=GE))(), inc=s_dve)
                E("vector", (lambda B=B, U1=U1: lambda e: e.tensor_scalar(
                    out=U1, in0=B, scalar1=-1.0, scalar2=None,
                    op0=GE))(), inc=s_dve)
                E("vector", (lambda U0=U0, U1=U1: lambda e: e.tensor_add(
                    out=U0, in0=U0, in1=U1))(), inc=s_dve)
                E("vector", (lambda B=B, U1=U1: lambda e: e.tensor_scalar(
                    out=U1, in0=B, scalar1=0.0, scalar2=None,
                    op0=GE))(), inc=s_dve)
                E("vector", (lambda B=B, U2=U2: lambda e: e.tensor_scalar(
                    out=U2, in0=B, scalar1=1.0, scalar2=None,
                    op0=GE))(), inc=s_dve)
                E("vector", (lambda U1=U1, U2=U2: lambda e: e.tensor_add(
                    out=U1, in0=U1, in1=U2))(), inc=s_dve)
                E("vector", (lambda U0=U0, U1=U1: lambda e: e.tensor_add(
                    out=U0, in0=U0, in1=U1))(), inc=s_dve)
                E("vector", (lambda B=B, U1=U1: lambda e: e.tensor_scalar(
                    out=U1, in0=B, scalar1=2.0, scalar2=None,
                    op0=GE))(), inc=s_dve)
                E("vector", (lambda U0=U0, U1=U1: lambda e: e.tensor_add(
                    out=U0, in0=U0, in1=U1))(), inc=s_dve)
                E("vector", (lambda A=A, U0=U0, f=f: lambda e:
                   e.scalar_tensor_tensor(out=hq[:, f, :], in0=A,
                                          scalar=-3.0, in1=U0,
                                          op0=ALU.add, op1=ALU.add))(),
                   inc=s_dve)
                ch_dve[f] = s_dve.n
                if th == 0 and f == 0 and "zf" in taps:
                    E("gpsimd", (lambda zf: lambda e: e.dma_start(
                        out=taps["zf"], in_=zf))(zf),
                        waits=[(s_act, zf_act[0])], inc=s_out, inck=16)
                if th == 0 and f == 0 and "hq" in taps:
                    E("gpsimd", lambda e: e.dma_start(out=taps["hq"],
                                                      in_=hq[:, 0, :]),
                      waits=[(s_dve, s_dve.n)], inc=s_out, inck=16)
            chains_done = s_dve.n
            if th == 0 and "hqfull" in taps:
                E("gpsimd", lambda e: e.dma_start(out=taps["hqfull"],
                                                  in_=hq[:, :, :]),
                  waits=[(s_dve, chains_done)], inc=s_out, inck=16)

            w2cast = {}
            mm2pe = {}
            q3act = 0
            out_cnt = {}
            for sp in range(2):
                tiles = [th * 8 + sp * 4 + i for i in range(4)]
                for f in range(NFC):
                    par = f % 2
                    E("sync", (lambda par, f: lambda e: e.dma_start(
                        out=w2stg[:, par, :],
                        in_=w2_d[f * 128:(f + 1) * 128, :]))(par, f),
                        waits=[(s_pool, w2cast.get((sp, f - 2), 0))],
                        inc=s_w, inck=16)
                    E("vector", (lambda par, f: lambda e: e.tensor_scalar(
                        out=w2bf[:, f % 3, :], in0=w2stg[:, par, :],
                        scalar1=1.0 / 256.0, scalar2=None,
                        op0=ALU.mult))(par, f),
                        waits=[(s_w, s_w.n),
                               (s_pe, mm2pe.get((sp, f - 3), 0))],
                        inc=s_pool)
                    w2cast[(sp, f)] = s_pool.n

                    def mm2fn(f=f, tiles=tiles, th=th):
                        def fx(e):
                            ins = None
                            for i, tt in enumerate(tiles):
                                ps = ps_a if i < 2 else ps_c
                                off = (i % 2) * 1024
                                lt = hq[:, f,
                                        (tt % 8) * 128:(tt % 8) * 128 + 128]
                                for c in range(2):
                                    ins = e.matmul(
                                        ps[:, off + c * 512:
                                           off + c * 512 + 512],
                                        lt,
                                        w2bf[:, f % 3,
                                             c * 512:(c + 1) * 512],
                                        start=(f == 0), stop=False)
                            return ins
                        return fx

                    E("tensor", mm2fn(),
                      waits=[(s_pool, w2cast[(sp, f)]),
                             (s_dve, chains_done), (s_act, q3act)],
                      inc=s_pe)
                    mm2pe[(sp, f)] = s_pe.n

                def biasfn(tiles=tiles):
                    def fx(e):
                        ins = None
                        for i, tt in enumerate(tiles):
                            ps = ps_a if i < 2 else ps_c
                            off = (i % 2) * 1024
                            for c in range(2):
                                ins = e.matmul(
                                    ps[:, off + c * 512:off + c * 512 + 512],
                                    onesbf[:, 0:128],
                                    b2_bf[:, c * 512:(c + 1) * 512],
                                    start=False, stop=True)
                        return ins
                    return fx

                E("tensor", biasfn(), waits=[(s_pool, b2cast)], inc=s_pe)
                mm2_all = s_pe.n

                for i, tt in enumerate(tiles):
                    ps = ps_a if i < 2 else ps_c
                    off = (i % 2) * 1024
                    par = tt % 2
                    E("sync", (lambda tt, par: lambda e: e.dma_start(
                        out=xst[:, par, :],
                        in_=resid_d[tt * 128:(tt + 1) * 128, :]))(tt, par),
                        waits=[(s_res, 16 * NQT),
                               (s_dve, s_dve.n)],
                        inc=s_x, inck=16)
                    ip = i % 2
                    T2 = TP[:, ip, 2, :]
                    T0 = TP[:, ip, 0, :]
                    T1 = TP[:, ip, 1, :]
                    V0 = TB[:, ip, 0, :]
                    V1 = TB[:, ip, 1, :]
                    V2 = TB[:, ip, 2, :]
                    IT = TI[:, ip, :]
                    E("scalar", (lambda ps, off, T2: lambda e: e.activation(
                        out=T2, in_=ps[:, off:off + 1024], func=AF.Identity,
                        bias=consts[:, 2:3], scale=256.0))(ps, off, T2),
                        waits=[(s_pe, mm2_all), (s_dve, s_dve.n),
                               (s_pool, s_pool.n)],
                        inc=s_act)
                    q3act = s_act.n
                    E("vector", (lambda IT, T2: lambda e: e.tensor_copy(
                        out=IT, in_=T2))(IT, T2),
                        waits=[(s_act, s_act.n)], inc=s_dve)
                    E("vector", (lambda IT, T0: lambda e: e.tensor_copy(
                        out=T0, in_=IT))(IT, T0), inc=s_dve)
                    E("vector", (lambda T2, T0, T1: lambda e: e.tensor_sub(
                        out=T1, in0=T2, in1=T0))(T2, T0, T1), inc=s_dve)
                    for j, thr in enumerate((-1.0, 0.0, 1.0)):
                        E("vector", (lambda T1, u, thr=thr: lambda e:
                           e.tensor_scalar(out=u, in0=T1, scalar1=thr,
                                           scalar2=None, op0=ALU.is_ge))(
                               T1, (V0, V1, V2)[j]),
                           inc=s_dve)
                    E("vector", (lambda V0, V1: lambda e: e.tensor_add(
                        out=V0, in0=V0, in1=V1))(V0, V1), inc=s_dve)
                    E("vector", (lambda V0, V2: lambda e: e.tensor_add(
                        out=V0, in0=V0, in1=V2))(V0, V2), inc=s_dve)
                    E("vector", (lambda T0, V0, T1: lambda e:
                       e.scalar_tensor_tensor(out=T1, in0=T0, scalar=-2.0,
                                              in1=V0, op0=ALU.add,
                                              op1=ALU.add))(T0, V0, T1),
                       inc=s_dve)
                    E("vector", (lambda T1, par, tt: lambda e:
                       e.scalar_tensor_tensor(
                           out=outst[:, tt % 2, :], in0=T1,
                           scalar=1.0 / 256.0, in1=xst[:, par, :],
                           op0=ALU.mult, op1=ALU.add))(T1, par, tt),
                       waits=[(s_x, s_x.n),
                              (s_out, out_cnt.get(tt - 2, 0))],
                       inc=s_dve)
                    E("gpsimd", (lambda tt: lambda e: e.dma_start(
                        out=out_d[tt * 128:(tt + 1) * 128, :],
                        in_=outst[:, tt % 2, :]))(tt),
                        waits=[(s_dve, s_dve.n)], inc=s_out, inck=16)
                    out_cnt[tt] = s_out.n
            if th == 0:
                E("sync", None, waits=[(s_tp, s_tp.n), (s_w, s_w.n),
                                       (s_x, s_x.n), (s_out, s_out.n)])
                plan.barrier(s_bar)

        E("sync", None, waits=[(s_out, s_out.n)])

        blk = ctx.enter_context(nc.Block())

        @blk.sync
        def _(eng):
            plan.run("sync", eng)

        @blk.tensor
        def _(eng):
            plan.run("tensor", eng)

        @blk.vector
        def _(eng):
            plan.run("vector", eng)

        @blk.scalar
        def _(eng):
            plan.run("scalar", eng)

        @blk.gpsimd
        def _(eng):
            plan.run("gpsimd", eng)

        est.close()
    return nc, sorted(taps.keys())


def kernel(L0_attn_x, L0_attn_Wq, L0_attn_Wk, L0_attn_Wv, L0_attn_Wo,
           L0_attn_bo, L0_ffn_w1, L0_ffn_b1, L0_ffn_w2, L0_ffn_b2):
    x = np.ascontiguousarray(np.asarray(L0_attn_x, dtype=np.float32))
    wts = {k: np.ascontiguousarray(np.asarray(v, dtype=np.float32))
           for k, v in (("wq", L0_attn_Wq), ("wk", L0_attn_Wk),
                        ("wv", L0_attn_Wv), ("wo", L0_attn_Wo),
                        ("bo", L0_attn_bo), ("w1", L0_ffn_w1),
                        ("b1", L0_ffn_b1), ("w2", L0_ffn_w2),
                        ("b2", L0_ffn_b2))}
    nc, tapnames = build_program()
    in_maps = []
    for c in range(8):
        b, qh = c // 2, c % 2
        xq = np.ascontiguousarray(x[b, qh * NQ:(qh + 1) * NQ, :])
        xr = np.ascontiguousarray(x[b, (1 - qh) * NQ:(2 - qh) * NQ, :])
        in_maps.append(dict(wts, xq=xq, xr=xr))
    res = run_bass_kernel_spmd(nc, in_maps, core_ids=list(range(8)))
    out = np.empty((4, 4096, 1024), np.float32)
    for c in range(8):
        out[c // 2, (c % 2) * NQ:(c % 2 + 1) * NQ, :] = res.results[c]["out"]
    if tapnames:
        kernel.taps = [{t: np.asarray(r["tap_" + t]) for t in tapnames}
                       for r in res.results]
    return out


# version 04
# speedup vs baseline: 1.4121x; 1.1122x over previous
import numpy as np
import concourse.bass as bass
import concourse.mybir as mybir
from concourse.bass_utils import run_bass_kernel_spmd
from contextlib import ExitStack

F32 = mybir.dt.float32
BF16 = mybir.dt.bfloat16
I32 = mybir.dt.int32
AF = mybir.ActivationFunctionType
ALU = mybir.AluOpType
AXX = mybir.AxisListType.X

NTILE = 32
NQT = 16
NQ = 2048
NK = 4096
DM = 1024
FF = 4096
NPAIR = 8
NFC = 32
EPS = 1e-5

DEBUG_TAPS = []

_REG = {}


def _ensure_custom_ops():
    if _REG:
        return
    import concourse.dve_ops as dve_ops
    from concourse.dve_op import DveOp
    from concourse.dve_spec import Spec, Src0, Src1, Zero, One, C2, lower
    from concourse.dve_table_gen import dve_ver_for

    have = {o.name: o for o in dve_ops.OPS}

    def _f3_ref(in0, in1, s0, s1, imm2):
        d = in0 - in1
        dt_ = in0.dtype
        ge = lambda t: (d >= t).astype(dt_)
        return ((in1 + dt_.type(imm2)) + ge(-1.0)) + (ge(0.0) + ge(1.0))

    D = Src0 - Src1
    body3 = ((Src1 + C2) + (D >= (Zero - One))) + ((D >= Zero) + (D >= One))

    ver = dve_ver_for("TRN2")
    for name, body, ref in (("MK_FLOOR3", body3, _f3_ref),):
        if name in have:
            _REG[name] = have[name]
            continue
        spec = Spec(body=body)
        sha = lower(spec, ver=ver, perf_en=False, name=name).sha(ver)
        op = DveOp(name, spec, subdim=False, uops_sha={ver: sha}, reference=ref)
        dve_ops.OPS.append(op)
        dve_ops.CUSTOM_DVE_SPECS[name] = spec
        dve_ops.CUSTOM_DVE_REFS[name] = ref
        dve_ops._SUB_OPCODE_FOR_NAME[name] = (
            dve_ops._CUSTOM_DVE_ROW_BASE + len(dve_ops.OPS) - 1)
        _REG[name] = op


class Sem:
    def __init__(self, nc, ctx, name):
        self.h = ctx.enter_context(nc.semaphore(name))
        self.n = 0
        self.name = name


class Plan:
    ENGS = ("sync", "tensor", "vector", "scalar", "gpsimd")

    def __init__(self):
        self.q = {e: [] for e in self.ENGS}
        self.lastw = {e: {} for e in self.ENGS}

    def emit(self, eng, fn, waits=(), inc=None, inck=1):
        lw = self.lastw[eng]
        real = []
        for s, v in waits:
            if v > 0 and lw.get(s.name, -1) < v:
                real.append((s, v))
                lw[s.name] = v
        if inc is not None:
            inc.n += inck
        self.q[eng].append((real, fn, inc, inck))

    def barrier(self, s):
        target = s.n + 5
        for e in self.ENGS:
            self.q[e].append(([], lambda eng: eng.nop(), s, 1))
        s.n = target
        for e in self.ENGS:
            self.q[e].append(([(s, target)], None, None, 0))
            self.lastw[e][s.name] = target

    def run(self, eng_name, eng):
        for waits, fn, inc, k in self.q[eng_name]:
            for s, v in waits:
                eng.wait_ge(s.h, v)
            if fn is None:
                continue
            ins = fn(eng)
            if inc is not None and ins is not None:
                ins.then_inc(inc.h, k)


def build_program():
    _ensure_custom_ops()
    F3 = _REG["MK_FLOOR3"]
    nc = bass.Bass("TRN2", target_bir_lowering=False, debug=False)

    xq_d = nc.dram_tensor("xq", [NQ, DM], F32, kind="ExternalInput").ap()
    xr_d = nc.dram_tensor("xr", [NQ, DM], F32, kind="ExternalInput").ap()
    wq_d = nc.dram_tensor("wq", [DM, DM], F32, kind="ExternalInput").ap()
    wk_d = nc.dram_tensor("wk", [DM, DM], F32, kind="ExternalInput").ap()
    wv_d = nc.dram_tensor("wv", [DM, DM], F32, kind="ExternalInput").ap()
    wo_d = nc.dram_tensor("wo", [DM, DM], F32, kind="ExternalInput").ap()
    bo_d = nc.dram_tensor("bo", [DM], F32, kind="ExternalInput").ap()
    w1_d = nc.dram_tensor("w1", [DM, FF], F32, kind="ExternalInput").ap()
    b1_d = nc.dram_tensor("b1", [FF], F32, kind="ExternalInput").ap()
    w2_d = nc.dram_tensor("w2", [FF, DM], F32, kind="ExternalInput").ap()
    b2_d = nc.dram_tensor("b2", [DM], F32, kind="ExternalInput").ap()
    out_d = nc.dram_tensor("out", [NQ, DM], F32, kind="ExternalOutput").ap()
    resid_d = nc.dram_tensor("residsc", [NQ, DM], F32, kind="Internal").ap()
    rec_d = nc.dram_tensor("recsc", [2, NQ], F32, kind="Internal").ap()

    taps = {}

    def mktap(name, shape, dt):
        if name in DEBUG_TAPS:
            taps[name] = nc.dram_tensor("tap_" + name, shape, dt,
                                        kind="ExternalOutput").ap()

    mktap("xn", [128, 8, 256], BF16)
    mktap("qT", [128, 256], BF16)
    mktap("kT", [128, 256], BF16)
    mktap("vA", [128, 132], BF16)
    mktap("pt", [128, NQ], BF16)
    mktap("avT", [128, 8, 256], BF16)
    mktap("r1", [128, DM], F32)
    mktap("x2T", [128, 8, 256], BF16)
    mktap("hq", [128, 1024], BF16)
    mktap("zf", [128, 1024], F32)
    mktap("resfull", [NQ, DM], F32)
    mktap("x2full", [128, 8, NQ], BF16)
    mktap("hqfull", [128, NFC, 1024], BF16)
    mktap("tn", [128, 1024], F32)
    mktap("tk4", [128, 1024], F32)
    mktap("tr", [128, 1024], F32)
    mktap("ty", [128, 1024], F32)

    with ExitStack() as ctx:
        S = lambda n: Sem(nc, ctx, n)
        s_x = S("s_x")
        s_w = S("s_w")
        s_wb = S("s_wb")
        s_wq3 = [S("s_wq0"), S("s_wq1"), S("s_wq2")]
        s_tp = S("s_tp")
        s_rec = S("s_rec")
        s_res = S("s_res")
        s_out = S("s_out")
        s_pe = S("s_pe")
        s_dve = S("s_dve")
        s_act = S("s_act")
        s_pool = S("s_pool")
        s_exp = S("s_exp")
        s_bar = S("s_bar")

        sb = lambda n, shp, dt=F32: ctx.enter_context(nc.sbuf_tensor(n, shp, dt))
        consts = sb("consts", [128, 8])
        onesbf = sb("onesbf", [1, 128], BF16)
        xst = sb("xst", [128, 2, DM])
        stats = sb("stats", [128, 40])
        mv = sb("mv", [128, 4])
        outst = sb("outst", [128, 2, DM])
        b1t = sb("b1t", [128, 2])
        b1s = sb("b1s", [128, 2])
        xn2T = sb("xn2T", [128, 8, NQ], BF16)
        ps_a = ctx.enter_context(nc.psum_tensor("ps_a", [128, 2048], F32))

        plan = Plan()
        E = plan.emit

        def _init(gp):
            gp.memset(consts[:, 0:1], EPS)
            gp.memset(consts[:, 1:2], -1.0)
            gp.memset(consts[:, 2:3], 0.5)
            gp.memset(consts[:, 3:4], 0.25)
            return gp.memset(onesbf[:, :], 1.0)

        E("gpsimd", _init, inc=s_pool)
        init_done = s_pool.n

        def plan_ln(xv, pre_waits, sq_ps):
            E("vector", (lambda xv: lambda e: e.tensor_reduce(
                out=mv[:, 0:1], in_=xv, axis=AXX, op=ALU.add))(xv),
                waits=list(pre_waits), inc=s_dve)
            E("scalar", (lambda xv, sq_ps: lambda e: e.activation(
                out=sq_ps, in_=xv, func=AF.Square, bias=0.0, scale=1.0))(
                    xv, sq_ps),
                waits=[(s_dve, s_dve.n)], inc=s_act)
            E("vector", (lambda sq_ps: lambda e: e.tensor_reduce(
                out=mv[:, 1:2], in_=sq_ps, axis=AXX, op=ALU.add))(sq_ps),
                waits=[(s_act, s_act.n)], inc=s_dve)
            E("vector", lambda e: e.tensor_scalar(
                out=mv[:, 0:1], in0=mv[:, 0:1], scalar1=1.0 / 1024.0,
                scalar2=None, op0=ALU.mult),
                waits=[(s_dve, s_dve.n)], inc=s_dve)
            E("scalar", lambda e: e.activation(
                out=mv[:, 2:3], in_=mv[:, 0:1], func=AF.Square,
                bias=0.0, scale=1.0),
                waits=[(s_dve, s_dve.n)], inc=s_act)
            E("vector", lambda e: e.scalar_tensor_tensor(
                out=mv[:, 1:2], in0=mv[:, 1:2], scalar=1.0 / 1024.0,
                in1=mv[:, 2:3], op0=ALU.mult, op1=ALU.subtract),
                waits=[(s_act, s_act.n)], inc=s_dve)
            E("scalar", lambda e: e.activation(
                out=mv[:, 2:3], in_=mv[:, 1:2], func=AF.Ln,
                bias=consts[:, 0:1], scale=1.0),
                waits=[(s_dve, s_dve.n)], inc=s_act)
            E("scalar", lambda e: e.activation(
                out=mv[:, 3:4], in_=mv[:, 2:3], func=AF.Exp,
                bias=0.0, scale=-0.5),
                waits=[(s_act, s_act.n)], inc=s_act)
            return s_act.n


        ab2 = ExitStack()
        sb2 = lambda n, shp, dt=F32: ab2.enter_context(nc.sbuf_tensor(n, shp, dt))
        recb = sb2("recb", [64, NQ])
        avT = sb2("avT", [128, 8, NQ], BF16)
        ab1 = ExitStack()
        sb1 = lambda n, shp, dt=F32: ab1.enter_context(nc.sbuf_tensor(n, shp, dt))
        xnT = sb1("xnT", [128, 8, NK], BF16)
        xntok = sb1("xntok", [128, 2, DM], BF16)
        wstg = sb1("wstg", [128, 3, DM])
        wqbf = sb1("wqbf", [128, 8, 128], BF16)
        wkbf = sb1("wkbf", [128, 8, 128], BF16)
        wvbf = sb1("wvbf", [128, 8, 128], BF16)
        qT = sb1("qT", [128, NQ], BF16)
        kT = sb1("kT", [128, NK], BF16)
        vstg = sb1("vstg", [128, 2, 512], BF16)
        vtok = sb1("vtok", [128, 2, 128], BF16)
        vAug = sb1("vAug", [128, 32, 132], BF16)
        ptile = sb1("ptile", [128, 2, NQ], BF16)

        recsb = outst[0:1, :, :].rearrange("p a b -> p (a b)")
        avsb = xst[0:64, :, :].rearrange("p a b -> p (a b)")
        ps_b_st = ExitStack()
        ps_b = ps_b_st.enter_context(nc.psum_tensor("ps_b", [128, 2048], F32))

        def _vaug_ones(gp):
            gp.memset(vAug[:, :, 64:65], 1.0)
            return gp.memset(vAug[:, :, 130:131], 1.0)

        E("gpsimd", _vaug_ones, inc=s_pool)
        vaug_ones_cnt = s_pool.n

        dve_norm = {}
        tp_tile = {}
        act_rstd = {}
        for tt in range(NTILE):
            src = xq_d if tt < NQT else xr_d
            row = (tt % NQT) * 128
            par = tt % 2
            E("sync", (lambda par, s0: lambda e: e.dma_start(
                out=xst[:, par, :], in_=s0))(par, src[row:row + 128, :]),
                waits=[(s_dve, dve_norm.get(tt - 2, 0)),
                       (s_pool, init_done)], inc=s_x, inck=16)
            xv = xst[:, par, :]
            act_rstd[tt] = plan_ln(xv, [(s_x, s_x.n),
                                        (s_act, act_rstd.get(tt - 1, 0))],
                                   ps_a[:, 0:1024])
            E("vector", (lambda xv, par: lambda e: e.tensor_scalar(
                out=xntok[:, par, :], in0=xv, scalar1=mv[:, 0:1],
                scalar2=mv[:, 3:4], op0=ALU.subtract, op1=ALU.mult))(xv, par),
                waits=[(s_act, s_act.n), (s_tp, tp_tile.get(tt - 2, 0))],
                inc=s_dve)
            dve_norm[tt] = s_dve.n
            for a in range(8):
                E("sync", (lambda a, par, tt: lambda e: e.dma_start(
                    out=xnT[:, a, tt * 128:(tt + 1) * 128],
                    in_=xntok[:, par, a * 128:(a + 1) * 128], transpose=True))(
                        a, par, tt),
                    waits=[(s_dve, s_dve.n)], inc=s_tp, inck=16)
            tp_tile[tt] = s_tp.n
        tp_after_A = s_tp.n

        if "xn" in taps:
            E("gpsimd", lambda e: e.dma_start(out=taps["xn"],
                                              in_=xnT[:, :, 0:256]),
              waits=[(s_tp, tp_after_A)], inc=s_out, inck=16)

        wbufs = (wqbf, wkbf, wvbf)
        wsrc = (wq_d, wk_d, wv_d)
        cast_cnt = {}
        wdma_cnt = {}

        def plan_wdma(p):
            for wi in range(3):
                sidx = 3 * p + wi
                spar = sidx % 3
                src = wsrc[wi][:, p * 128:(p + 1) * 128].rearrange(
                    "(a q) m -> q a m", q=128)
                E("sync", (lambda spar, src: lambda e: e.dma_start(
                    out=wstg[:, spar, :].rearrange("q (a m) -> q a m", a=8),
                    in_=src))(spar, src),
                    waits=[(s_pool, cast_cnt.get(sidx - 3, 0))],
                    inc=s_wq3[wi], inck=16)
                wdma_cnt[sidx] = s_wq3[wi].n

        def plan_wcast(p, pe_guard):
            for wi in range(3):
                sidx = 3 * p + wi
                spar = sidx % 3
                E("vector", (lambda spar, wi: lambda e: e.tensor_copy(
                    out=wbufs[wi][:, :, :],
                    in_=wstg[:, spar, :].rearrange("q (a m) -> q a m", a=8)))(
                        spar, wi),
                    waits=[(s_wq3[wi], wdma_cnt[sidx]), (s_pe, pe_guard)],
                    inc=s_pool)
                cast_cnt[sidx] = s_pool.n

        plan_wdma(0)
        plan_wcast(0, 0)
        proj_pe_done = 0
        exp_half = {}
        exp_done_for_kt = {}

        for p in range(NPAIR):
            pp = p % 2
            if p + 1 < NPAIR:
                plan_wdma(p + 1)
            for wi, (wbf, dst, nch) in enumerate(((wqbf, qT, 4), (wkbf, kT, 8))):
                need_w = cast_cnt[3 * p + wi]
                for c in range(nch):
                    bank = c % 4

                    def mmfn(wbf=wbf, c=c, bank=bank):
                        def f(e):
                            ins = None
                            for a in range(8):
                                ins = e.matmul(
                                    ps_a[:, bank * 512:(bank + 1) * 512],
                                    wbf[:, a, :],
                                    xnT[:, a, c * 512:(c + 1) * 512],
                                    start=(a == 0), stop=(a == 7))
                            return ins
                        return f

                    E("tensor", mmfn(),
                      waits=[(s_pool, need_w), (s_tp, tp_after_A),
                             (s_exp, s_exp.n), (s_dve, s_dve.n)],
                      inc=s_pe)
                    E("vector", (lambda dst, c, bank: lambda e: e.tensor_copy(
                        out=dst[:, c * 512:(c + 1) * 512],
                        in_=ps_a[:, bank * 512:(bank + 1) * 512]))(dst, c, bank),
                        waits=[(s_pe, s_pe.n)], inc=s_dve)
            need_w = cast_cnt[3 * p + 2]
            for c in range(8):
                bank = c % 4

                def mmfn(c=c, bank=bank):
                    def f(e):
                        ins = None
                        for a in range(8):
                            ins = e.matmul(
                                ps_a[:, bank * 512:(bank + 1) * 512],
                                wvbf[:, a, :],
                                xnT[:, a, c * 512:(c + 1) * 512],
                                start=(a == 0), stop=(a == 7))
                        return ins
                    return f

                E("tensor", mmfn(),
                  waits=[(s_pool, need_w), (s_exp, s_exp.n),
                         (s_dve, s_dve.n), (s_tp, s_tp.n)],
                  inc=s_pe)
                E("vector", (lambda c, bank: lambda e: e.tensor_copy(
                    out=vstg[:, c % 2, :],
                    in_=ps_a[:, bank * 512:(bank + 1) * 512]))(c, bank),
                    waits=[(s_pe, s_pe.n), (s_tp, s_tp.n - (64 if c >= 2 else 0))],
                    inc=s_dve)
                for j in range(4):
                    kt = 4 * c + j
                    E("sync", (lambda c, j, kt: lambda e: e.dma_start(
                        out=vtok[:, kt % 2, :],
                        in_=vstg[:, c % 2, j * 128:(j + 1) * 128],
                        transpose=True))(c, j, kt),
                        waits=[(s_dve, s_dve.n), (s_pool, vaug_ones_cnt)],
                        inc=s_tp, inck=16)
                    E("vector", (lambda kt: lambda e: e.tensor_copy(
                        out=vAug[:, kt, :].rearrange(
                            "p (g c) -> p g c", c=66)[:, :, 0:64],
                        in_=vtok[:, kt % 2, :].rearrange(
                            "p (g c) -> p g c", c=64)))(kt),
                        waits=[(s_tp, s_tp.n)], inc=s_dve)
            qkv_done = s_dve.n
            vaug_done = s_tp.n
            proj_pe_done = s_pe.n
            if p + 1 < NPAIR:
                plan_wcast(p + 1, proj_pe_done)

            if p == 0 and "qT" in taps:
                E("gpsimd", lambda e: e.dma_start(out=taps["qT"],
                                                  in_=qT[:, 0:256]),
                  waits=[(s_dve, qkv_done)], inc=s_out, inck=16)
            if p == 0 and "kT" in taps:
                E("gpsimd", lambda e: e.dma_start(out=taps["kT"],
                                                  in_=kT[:, 0:256]),
                  waits=[(s_dve, qkv_done)], inc=s_out, inck=16)
            if p == 0 and "vA" in taps:
                E("gpsimd", lambda e: e.dma_start(
                    out=taps["vA"], in_=vAug[:, 0, :]),
                    waits=[(s_tp, vaug_done)], inc=s_out, inck=16)

            for h in range(2):
                hb = 64 * h
                psb_free_act = s_act.n
                psb_free_dve = s_dve.n
                def avfn(h, kt, kpar):
                    base = 0 if h == 0 else 66

                    def f(e):
                        ins = None
                        for qc in range(4):
                            ins = e.matmul(
                                ps_b[0:65, qc * 512:(qc + 1) * 512],
                                vAug[:, kt, base:base + 65],
                                ptile[:, kpar, qc * 512:(qc + 1) * 512],
                                start=(kt == 0), stop=(kt == 31))
                        return ins
                    return f

                for kt in range(32):
                    kpar = kt % 2
                    for half in range(2):
                        def scfn(hb=hb, kt=kt, half=half):
                            def f(e):
                                ins = None
                                for qc in (2 * half, 2 * half + 1):
                                    ins = e.matmul(
                                        ps_a[:, qc * 512:(qc + 1) * 512],
                                        kT[hb:hb + 64,
                                           kt * 128:(kt + 1) * 128],
                                        qT[hb:hb + 64,
                                           qc * 512:(qc + 1) * 512],
                                        start=True, stop=True)
                                return ins
                            return f

                        scw = [(s_exp, exp_half.get(("ps", half), 0)),
                               (s_dve, qkv_done), (s_tp, vaug_done)]
                        E("tensor", scfn(), waits=scw, inc=s_pe)
                        E("scalar", (lambda kpar, half: lambda e: e.activation(
                            out=ptile[:, kpar, half * 1024:half * 1024 + 1024],
                            in_=ps_a[:, half * 1024:half * 1024 + 1024],
                            func=AF.Exp, bias=consts[:, 1:2],
                            scale=1.0 / 32.0))(kpar, half),
                            waits=[(s_pe, s_pe.n)], inc=s_exp)
                        exp_half[("ps", half)] = s_exp.n
                    if kt > 0:
                        avw = [(s_exp, exp_done_for_kt[kt - 1])]
                        if kt - 1 == 0:
                            avw += [(s_act, psb_free_act),
                                    (s_dve, psb_free_dve)]
                        E("tensor", avfn(h, kt - 1, (kt - 1) % 2),
                          waits=avw, inc=s_pe)
                    exp_done_for_kt[kt] = s_exp.n
                E("tensor", avfn(h, 31, 31 % 2),
                  waits=[(s_exp, exp_done_for_kt[31])], inc=s_pe)
                av_done = s_pe.n
                if p == 0 and h == 0 and "pt" in taps:
                    E("gpsimd", lambda e: e.dma_start(out=taps["pt"],
                                                      in_=ptile[:, 1, :]),
                      waits=[(s_exp, s_exp.n)], inc=s_out, inck=16)
                hpar = h
                E("vector", lambda e: e.reciprocal(
                    out=recsb, in_=ps_b[64:65, 0:2048]),
                    waits=[(s_pe, av_done)], inc=s_dve)
                E("gpsimd", (lambda hpar: lambda e: e.dma_start(
                    out=rec_d[hpar, None, :], in_=recsb))(hpar),
                    waits=[(s_dve, s_dve.n)], inc=s_rec, inck=16)
                E("gpsimd", (lambda hpar: lambda e: e.dma_start(
                    out=recb[:, :],
                    in_=rec_d[hpar, None, :].broadcast_to([64, NQ])))(hpar),
                    waits=[(s_rec, s_rec.n), (s_dve, s_dve.n)],
                    inc=s_rec, inck=16)
                E("scalar", lambda e: e.copy(out=avsb[:, :],
                                             in_=ps_b[0:64, 0:2048]),
                  waits=[(s_pe, av_done)], inc=s_act)
                E("vector", (lambda p, hb: lambda e: e.tensor_mul(
                    out=avT[hb:hb + 64, p, :], in0=avsb[:, :],
                    in1=recb[:, :]))(p, hb),
                    waits=[(s_act, s_act.n), (s_rec, s_rec.n)], inc=s_dve)
        avT_done = s_dve.n
        if "avT" in taps:
            E("gpsimd", lambda e: e.dma_start(out=taps["avT"],
                                              in_=avT[:, :, 0:256]),
              waits=[(s_dve, avT_done)], inc=s_out, inck=16)

        E("sync", None, waits=[(s_tp, s_tp.n), (s_w, s_w.n), (s_x, s_x.n),
                               (s_wb, s_wb.n), (s_rec, s_rec.n),
                               (s_res, s_res.n), (s_out, s_out.n),
                               (s_wq3[0], s_wq3[0].n), (s_wq3[1], s_wq3[1].n),
                               (s_wq3[2], s_wq3[2].n)])
        plan.barrier(s_bar)
        ab1.close()

        cst_tmp = ExitStack()
        sbc = lambda n, shp, dt=F32: cst_tmp.enter_context(
            nc.sbuf_tensor(n, shp, dt))
        wostg = sbc("wostg", [128, 2, DM])
        wobf = sbc("wobf", [128, 8, DM], BF16)
        x2tok = sbc("x2tok", [128, 2, DM], BF16)
        r1t = sbc("r1t", [128, 2, DM])
        bo_bf = sbc("bo_bf", [1, DM], BF16)
        E("sync", lambda e: e.dma_start(out=r1t[0:1, 0, :], in_=bo_d[None, :]),
          inc=s_wb, inck=16)
        E("vector", lambda e: e.tensor_copy(out=bo_bf[:, :], in_=r1t[0:1, 0, :]),
          waits=[(s_wb, s_wb.n)], inc=s_pool)
        bocast = s_pool.n

        for a in range(8):
            par = a % 2
            E("sync", (lambda a, par: lambda e: e.dma_start(
                out=wostg[:, par, :],
                in_=wo_d[a * 128:(a + 1) * 128, :]))(a, par),
                waits=[(s_pool, s_pool.n if a >= 2 else 0)],
                inc=s_w, inck=16)
            E("vector", (lambda a, par: lambda e: e.tensor_copy(
                out=wobf[:, a, :], in_=wostg[:, par, :]))(a, par),
                waits=[(s_w, s_w.n)], inc=s_pool)
        wo_cast = s_pool.n

        r1_cnt = {}
        res_cnt = {}
        x2tp = {}
        rstd2 = {}
        for tt in range(NQT):
            par = tt % 2
            E("sync", (lambda tt, par: lambda e: e.dma_start(
                out=xst[:, par, :],
                in_=xq_d[tt * 128:(tt + 1) * 128, :]))(tt, par),
                waits=[(s_dve, max(avT_done, r1_cnt.get(tt - 2, 0)))],
                inc=s_x, inck=16)

            def wofn(tt=tt, par=par):
                def f(e):
                    ins = None
                    for nc2 in range(2):
                        sl = slice(par * 1024 + nc2 * 512,
                                   par * 1024 + nc2 * 512 + 512)
                        for a in range(8):
                            e.matmul(ps_a[:, sl],
                                     avT[:, a, tt * 128:(tt + 1) * 128],
                                     wobf[:, a, nc2 * 512:(nc2 + 1) * 512],
                                     start=(a == 0), stop=False)
                        ins = e.matmul(ps_a[:, sl], onesbf[:, 0:128],
                                       bo_bf[:, nc2 * 512:(nc2 + 1) * 512],
                                       start=False, stop=True)
                    return ins
                return f

            E("tensor", wofn(),
              waits=[(s_pool, max(wo_cast, bocast)),
                     (s_dve, r1_cnt.get(tt - 2, 0)), (s_exp, s_exp.n)],
              inc=s_pe)
            pe_wo = s_pe.n
            E("vector", (lambda par: lambda e: e.tensor_add(
                out=r1t[:, par, :],
                in0=ps_a[:, par * 1024:par * 1024 + 1024],
                in1=xst[:, par, :]))(par),
                waits=[(s_pe, pe_wo), (s_x, s_x.n),
                       (s_res, res_cnt.get(tt - 2, 0)),
                       (s_tp, x2tp.get(tt - 2, 0)),
                       (s_act, rstd2.get(tt - 2, 0)),
                       (s_pool, bocast)],
                inc=s_dve)
            r1_cnt[tt] = s_dve.n
            E("gpsimd", (lambda tt, par: lambda e: e.dma_start(
                out=resid_d[tt * 128:(tt + 1) * 128, :],
                in_=r1t[:, par, :]))(tt, par),
                waits=[(s_dve, s_dve.n)], inc=s_res, inck=16)
            res_cnt[tt] = s_res.n
            if tt == 0 and "r1" in taps:
                E("gpsimd", lambda e: e.dma_start(out=taps["r1"],
                                                  in_=r1t[:, 0, :]),
                  waits=[(s_dve, s_dve.n)], inc=s_out, inck=16)
            rstd2[tt] = plan_ln(r1t[:, par, :],
                                 [(s_dve, s_dve.n),
                                  (s_act, rstd2.get(tt - 1, 0))],
                                 ps_b[:, 0:1024])
            E("vector", (lambda par: lambda e: e.tensor_scalar(
                out=x2tok[:, par, :], in0=r1t[:, par, :],
                scalar1=mv[:, 0:1], scalar2=mv[:, 3:4],
                op0=ALU.subtract, op1=ALU.mult))(par),
                waits=[(s_act, s_act.n), (s_tp, x2tp.get(tt - 2, 0))],
                inc=s_dve)
            for a in range(8):
                E("sync", (lambda a, par, tt: lambda e: e.dma_start(
                    out=xn2T[:, a, tt * 128:(tt + 1) * 128],
                    in_=x2tok[:, par, a * 128:(a + 1) * 128],
                    transpose=True))(a, par, tt),
                    waits=[(s_dve, s_dve.n)], inc=s_tp, inck=16)
            x2tp[tt] = s_tp.n
        xn2_done = s_tp.n
        if "resfull" in taps:
            E("gpsimd", lambda e: e.dma_start(out=taps["resfull"], in_=resid_d),
              waits=[(s_res, s_res.n)], inc=s_out, inck=16)
        if "x2full" in taps:
            E("gpsimd", lambda e: e.dma_start(out=taps["x2full"], in_=xn2T[:, :, :]),
              waits=[(s_tp, xn2_done)], inc=s_out, inck=16)
        if "x2T" in taps:
            E("gpsimd", lambda e: e.dma_start(out=taps["x2T"],
                                              in_=xn2T[:, :, 0:256]),
              waits=[(s_tp, xn2_done)], inc=s_out, inck=16)

        E("sync", None, waits=[(s_tp, s_tp.n), (s_w, s_w.n), (s_x, s_x.n),
                               (s_wb, s_wb.n), (s_rec, s_rec.n),
                               (s_res, s_res.n), (s_out, s_out.n),
                               (s_wq3[0], s_wq3[0].n), (s_wq3[1], s_wq3[1].n),
                               (s_wq3[2], s_wq3[2].n)])
        plan.barrier(s_bar)
        cst_tmp.close()
        ab2.close()
        ps_b_st.close()

        est = ExitStack()
        sbe = lambda n, shp, dt=F32: est.enter_context(
            nc.sbuf_tensor(n, shp, dt))
        hq = sbe("hq", [128, NFC, 1024], BF16)
        w1stg = sbe("w1stg", [128, 2, DM])
        w1bf = sbe("w1bf", [128, 2, 8, 128], BF16)
        w2stg = sbe("w2stg", [128, 2, DM])
        w2bf = sbe("w2bf", [128, 3, DM], BF16)
        TP = sbe("tpool", [128, 2, 4, 1024])
        TB = sbe("tbit", [128, 2, 3, 1024], BF16)
        TI = sbe("tint", [128, 2, 1024], I32)
        zfb = sbe("zfb", [128, 2, 1024])
        b2_bf = sbe("b2_bf", [1, DM], BF16)
        E("sync", lambda e: e.dma_start(out=xst[0:1, 0, :], in_=b2_d[None, :]),
          waits=[(s_res, s_res.n)], inc=s_wb, inck=16)
        E("vector", lambda e: e.tensor_copy(out=b2_bf[:, :], in_=xst[0:1, 0, :]),
          waits=[(s_wb, s_wb.n)], inc=s_pool)
        b2cast = s_pool.n
        ps_c = est.enter_context(nc.psum_tensor("ps_c", [128, 2048], F32))

        for th in range(2):
            qoff = th * 1024
            w1cast = {}
            zf_act = {}
            ch_dve = {}
            ch_pool = {}
            mm1pe = {}
            b1s_dve = {}
            for f in range(NFC):
                par = f % 2
                src = w1_d[:, f * 128:(f + 1) * 128].rearrange(
                    "(a q) m -> q a m", q=128)
                E("sync", (lambda par, src: lambda e: e.dma_start(
                    out=w1stg[:, par, :].rearrange("q (a m) -> q a m", a=8),
                    in_=src))(par, src),
                    waits=[(s_pool, w1cast.get(f - 2, 0))],
                    inc=s_w, inck=16)
                E("sync", (lambda par, f: lambda e: e.dma_start(
                    out=b1t[:, par:par + 1],
                    in_=b1_d[f * 128:(f + 1) * 128].rearrange(
                        "(q o) -> q o", o=1)))(par, f),
                    waits=[(s_dve, b1s_dve.get(f - 2, 0))],
                    inc=s_wb, inck=16)
                E("vector", (lambda par: lambda e: e.tensor_copy(
                    out=w1bf[:, par, :, :],
                    in_=w1stg[:, par, :].rearrange("q (a m) -> q a m",
                                                   a=8)))(par),
                    waits=[(s_w, s_w.n),
                           (s_pe, mm1pe.get(f - 2, 0))], inc=s_pool)
                w1cast[f] = s_pool.n
                E("vector", (lambda par: lambda e: e.tensor_scalar(
                    out=b1s[:, par:par + 1], in0=b1t[:, par:par + 1],
                    scalar1=256.0, scalar2=0.5, op0=ALU.mult,
                    op1=ALU.add))(par),
                    waits=[(s_wb, s_wb.n)], inc=s_dve)
                b1s_cnt = s_dve.n
                b1s_dve[f] = s_dve.n

                def mm1fn(par=par, qoff=qoff):
                    def fx(e):
                        ins = None
                        for c in range(2):
                            sl = slice(par * 1024 + c * 512,
                                       par * 1024 + c * 512 + 512)
                            for a in range(8):
                                ins = e.matmul(
                                    ps_a[:, sl], w1bf[:, par, a, :],
                                    xn2T[:, a,
                                         qoff + c * 512:qoff + c * 512 + 512],
                                    start=(a == 0), stop=(a == 7))
                        return ins
                    return fx

                E("tensor", mm1fn(),
                  waits=[(s_pool, w1cast[f]), (s_tp, xn2_done),
                         (s_act, zf_act.get(f - 2, 0))],
                  inc=s_pe)
                mm1pe[f] = s_pe.n

                T = lambda i: TP[:, par, i, :]
                IT = TI[:, par, :]
                zf = zfb[:, par, :]
                wch = [(s_pe, s_pe.n), (s_dve, b1s_cnt),
                       (s_dve, ch_dve.get(f - 2, 0)),
                       (s_pool, ch_pool.get(f - 2, 0))]
                E("scalar", (lambda zf, par: lambda e: e.activation(
                    out=zf, in_=ps_a[:, par * 1024:par * 1024 + 1024],
                    func=AF.Identity, bias=b1s[:, par:par + 1],
                    scale=256.0))(zf, par),
                    waits=wch, inc=s_act)
                zf_act[f] = s_act.n
                A, B, C, D = T(0), T(1), T(2), T(3)
                U = lambda i: TB[:, par, i, :]
                U0, U1, U2 = U(0), U(1), U(2)
                GE = ALU.is_ge
                E("vector", (lambda IT, zf: lambda e: e.tensor_copy(
                    out=IT, in_=zf))(IT, zf),
                    waits=[(s_act, s_act.n)], inc=s_dve)
                E("vector", (lambda IT, A=A: lambda e: e.tensor_copy(
                    out=A, in_=IT))(IT), inc=s_dve)
                E("vector", (lambda zf, A=A, B=B: lambda e: e.tensor_sub(
                    out=B, in0=zf, in1=A))(zf), inc=s_dve)
                for i, thr in enumerate((-1.0, 0.0, 1.0)):
                    E("vector", (lambda B=B, u=U(i), thr=thr: lambda e:
                       e.tensor_scalar(out=u, in0=B, scalar1=thr,
                                       scalar2=None, op0=GE))(),
                       inc=s_dve)
                E("vector", (lambda U0=U0, U1=U1: lambda e: e.tensor_add(
                    out=U0, in0=U0, in1=U1))(), inc=s_dve)
                E("vector", (lambda U0=U0, U2=U2: lambda e: e.tensor_add(
                    out=U0, in0=U0, in1=U2))(), inc=s_dve)
                E("vector", (lambda A=A, U0=U0, C=C: lambda e:
                   e.scalar_tensor_tensor(out=C, in0=A, scalar=-2.0,
                                          in1=U0, op0=ALU.add,
                                          op1=ALU.add))(), inc=s_dve)
                if th == 0 and f == 0 and "tn" in taps:
                    E("gpsimd", (lambda C=C: lambda e: e.dma_start(
                        out=taps["tn"], in_=C))(),
                        waits=[(s_dve, s_dve.n)], inc=s_out, inck=16)
                E("vector", (lambda zf, B=B: lambda e: e.tensor_scalar(
                    out=B, in0=zf, scalar1=1.0 / 64.0, scalar2=None,
                    op0=ALU.mult))(zf), inc=s_dve)
                E("vector", (lambda IT, B=B: lambda e: e.tensor_copy(
                    out=IT, in_=B))(IT), inc=s_dve)
                E("vector", (lambda IT, A=A: lambda e: e.tensor_copy(
                    out=A, in_=IT))(IT), inc=s_dve)
                E("vector", (lambda B=B, A=A, U0=U0: lambda e:
                   e.tensor_tensor(out=U0, in0=B, in1=A, op=GE))(),
                   inc=s_dve)
                E("vector", (lambda A=A, U0=U0, D=D: lambda e:
                   e.scalar_tensor_tensor(out=D, in0=A, scalar=-1.0,
                                          in1=U0, op0=ALU.add,
                                          op1=ALU.add))(), inc=s_dve)
                E("vector", (lambda D=D, C=C, B=B: lambda e:
                   e.scalar_tensor_tensor(out=B, in0=D, scalar=-64.0,
                                          in1=C, op0=ALU.mult,
                                          op1=ALU.add))(), inc=s_dve)
                dve_r = s_dve.n
                if th == 0 and f == 0 and "tr" in taps:
                    E("gpsimd", (lambda D=D: lambda e: e.dma_start(
                        out=taps["tk4"], in_=D))(),
                        waits=[(s_dve, s_dve.n)], inc=s_out, inck=16)
                    E("gpsimd", (lambda B=B: lambda e: e.dma_start(
                        out=taps["tr"], in_=B))(),
                        inc=s_out, inck=16)
                E("scalar", (lambda D=D, A=A: lambda e: e.activation(
                    out=A, in_=D, func=AF.Gelu, bias=0.0, scale=0.25))(),
                    waits=[(s_dve, dve_r)], inc=s_act)
                E("scalar", (lambda D=D, C=C: lambda e: e.activation(
                    out=C, in_=D, func=AF.Gelu, bias=consts[:, 3:4],
                    scale=0.25))(), inc=s_act)
                act_g = s_act.n
                E("vector", (lambda C=C, A=A: lambda e: e.tensor_sub(
                    out=C, in0=C, in1=A))(),
                    waits=[(s_act, act_g)], inc=s_dve)
                E("vector", (lambda C=C, B=B: lambda e:
                   e.scalar_tensor_tensor(out=C, in0=C, scalar=1.0 / 64.0,
                                          in1=B, op0=ALU.mult,
                                          op1=ALU.mult))(), inc=s_dve)
                E("vector", (lambda C=C, A=A: lambda e: e.tensor_add(
                    out=C, in0=C, in1=A))(), inc=s_dve)
                dve_y = s_dve.n
                if th == 0 and f == 0 and "ty" in taps:
                    E("gpsimd", (lambda C=C: lambda e: e.dma_start(
                        out=taps["ty"], in_=C))(),
                        waits=[(s_dve, s_dve.n)], inc=s_out, inck=16)
                E("scalar", (lambda zf, C=C: lambda e: e.activation(
                    out=zf, in_=C, func=AF.Identity, bias=consts[:, 2:3],
                    scale=256.0))(zf),
                    waits=[(s_dve, dve_y)], inc=s_act)
                E("vector", (lambda IT, zf: lambda e: e.tensor_copy(
                    out=IT, in_=zf))(IT, zf),
                    waits=[(s_act, s_act.n)], inc=s_dve)
                ch_pool[f] = s_pool.n
                E("vector", (lambda IT, A=A: lambda e: e.tensor_copy(
                    out=A, in_=IT))(IT), inc=s_dve)
                E("vector", (lambda zf, A=A, B=B: lambda e: e.tensor_sub(
                    out=B, in0=zf, in1=A))(zf), inc=s_dve)
                E("vector", (lambda B=B, U0=U0: lambda e: e.tensor_scalar(
                    out=U0, in0=B, scalar1=-2.0, scalar2=None,
                    op0=GE))(), inc=s_dve)
                E("vector", (lambda B=B, U1=U1: lambda e: e.tensor_scalar(
                    out=U1, in0=B, scalar1=-1.0, scalar2=None,
                    op0=GE))(), inc=s_dve)
                E("vector", (lambda U0=U0, U1=U1: lambda e: e.tensor_add(
                    out=U0, in0=U0, in1=U1))(), inc=s_dve)
                E("vector", (lambda B=B, U1=U1: lambda e: e.tensor_scalar(
                    out=U1, in0=B, scalar1=0.0, scalar2=None,
                    op0=GE))(), inc=s_dve)
                E("vector", (lambda B=B, U2=U2: lambda e: e.tensor_scalar(
                    out=U2, in0=B, scalar1=1.0, scalar2=None,
                    op0=GE))(), inc=s_dve)
                E("vector", (lambda U1=U1, U2=U2: lambda e: e.tensor_add(
                    out=U1, in0=U1, in1=U2))(), inc=s_dve)
                E("vector", (lambda U0=U0, U1=U1: lambda e: e.tensor_add(
                    out=U0, in0=U0, in1=U1))(), inc=s_dve)
                E("vector", (lambda B=B, U1=U1: lambda e: e.tensor_scalar(
                    out=U1, in0=B, scalar1=2.0, scalar2=None,
                    op0=GE))(), inc=s_dve)
                E("vector", (lambda U0=U0, U1=U1: lambda e: e.tensor_add(
                    out=U0, in0=U0, in1=U1))(), inc=s_dve)
                E("vector", (lambda A=A, U0=U0, f=f: lambda e:
                   e.scalar_tensor_tensor(out=hq[:, f, :], in0=A,
                                          scalar=-3.0, in1=U0,
                                          op0=ALU.add, op1=ALU.add))(),
                   inc=s_dve)
                ch_dve[f] = s_dve.n
                if th == 0 and f == 0 and "zf" in taps:
                    E("gpsimd", (lambda zf: lambda e: e.dma_start(
                        out=taps["zf"], in_=zf))(zf),
                        waits=[(s_act, zf_act[0])], inc=s_out, inck=16)
                if th == 0 and f == 0 and "hq" in taps:
                    E("gpsimd", lambda e: e.dma_start(out=taps["hq"],
                                                      in_=hq[:, 0, :]),
                      waits=[(s_dve, s_dve.n)], inc=s_out, inck=16)
            chains_done = s_dve.n
            if th == 0 and "hqfull" in taps:
                E("gpsimd", lambda e: e.dma_start(out=taps["hqfull"],
                                                  in_=hq[:, :, :]),
                  waits=[(s_dve, chains_done)], inc=s_out, inck=16)

            w2cast = {}
            mm2pe = {}
            q3act = 0
            out_cnt = {}
            for sp in range(2):
                tiles = [th * 8 + sp * 4 + i for i in range(4)]
                for f in range(NFC):
                    par = f % 2
                    E("sync", (lambda par, f: lambda e: e.dma_start(
                        out=w2stg[:, par, :],
                        in_=w2_d[f * 128:(f + 1) * 128, :]))(par, f),
                        waits=[(s_pool, w2cast.get((sp, f - 2), 0))],
                        inc=s_w, inck=16)
                    E("vector", (lambda par, f: lambda e: e.tensor_scalar(
                        out=w2bf[:, f % 3, :], in0=w2stg[:, par, :],
                        scalar1=1.0 / 256.0, scalar2=None,
                        op0=ALU.mult))(par, f),
                        waits=[(s_w, s_w.n),
                               (s_pe, mm2pe.get((sp, f - 3), 0))],
                        inc=s_pool)
                    w2cast[(sp, f)] = s_pool.n

                    def mm2fn(f=f, tiles=tiles, th=th):
                        def fx(e):
                            ins = None
                            for i, tt in enumerate(tiles):
                                ps = ps_a if i < 2 else ps_c
                                off = (i % 2) * 1024
                                lt = hq[:, f,
                                        (tt % 8) * 128:(tt % 8) * 128 + 128]
                                for c in range(2):
                                    ins = e.matmul(
                                        ps[:, off + c * 512:
                                           off + c * 512 + 512],
                                        lt,
                                        w2bf[:, f % 3,
                                             c * 512:(c + 1) * 512],
                                        start=(f == 0), stop=False)
                            return ins
                        return fx

                    E("tensor", mm2fn(),
                      waits=[(s_pool, w2cast[(sp, f)]),
                             (s_dve, chains_done), (s_act, q3act)],
                      inc=s_pe)
                    mm2pe[(sp, f)] = s_pe.n

                def biasfn(tiles=tiles):
                    def fx(e):
                        ins = None
                        for i, tt in enumerate(tiles):
                            ps = ps_a if i < 2 else ps_c
                            off = (i % 2) * 1024
                            for c in range(2):
                                ins = e.matmul(
                                    ps[:, off + c * 512:off + c * 512 + 512],
                                    onesbf[:, 0:128],
                                    b2_bf[:, c * 512:(c + 1) * 512],
                                    start=False, stop=True)
                        return ins
                    return fx

                E("tensor", biasfn(), waits=[(s_pool, b2cast)], inc=s_pe)
                mm2_all = s_pe.n

                for i, tt in enumerate(tiles):
                    ps = ps_a if i < 2 else ps_c
                    off = (i % 2) * 1024
                    par = tt % 2
                    E("sync", (lambda tt, par: lambda e: e.dma_start(
                        out=xst[:, par, :],
                        in_=resid_d[tt * 128:(tt + 1) * 128, :]))(tt, par),
                        waits=[(s_res, 16 * NQT),
                               (s_dve, s_dve.n)],
                        inc=s_x, inck=16)
                    ip = i % 2
                    T2 = TP[:, ip, 2, :]
                    T0 = TP[:, ip, 0, :]
                    T1 = TP[:, ip, 1, :]
                    V0 = TB[:, ip, 0, :]
                    V1 = TB[:, ip, 1, :]
                    V2 = TB[:, ip, 2, :]
                    IT = TI[:, ip, :]
                    E("scalar", (lambda ps, off, T2: lambda e: e.activation(
                        out=T2, in_=ps[:, off:off + 1024], func=AF.Identity,
                        bias=consts[:, 2:3], scale=256.0))(ps, off, T2),
                        waits=[(s_pe, mm2_all), (s_dve, s_dve.n),
                               (s_pool, s_pool.n)],
                        inc=s_act)
                    q3act = s_act.n
                    E("vector", (lambda IT, T2: lambda e: e.tensor_copy(
                        out=IT, in_=T2))(IT, T2),
                        waits=[(s_act, s_act.n)], inc=s_dve)
                    E("vector", (lambda IT, T0: lambda e: e.tensor_copy(
                        out=T0, in_=IT))(IT, T0), inc=s_dve)
                    E("vector", (lambda T2, T0, T1: lambda e: e.tensor_sub(
                        out=T1, in0=T2, in1=T0))(T2, T0, T1), inc=s_dve)
                    for j, thr in enumerate((-1.0, 0.0, 1.0)):
                        E("vector", (lambda T1, u, thr=thr: lambda e:
                           e.tensor_scalar(out=u, in0=T1, scalar1=thr,
                                           scalar2=None, op0=ALU.is_ge))(
                               T1, (V0, V1, V2)[j]),
                           inc=s_dve)
                    E("vector", (lambda V0, V1: lambda e: e.tensor_add(
                        out=V0, in0=V0, in1=V1))(V0, V1), inc=s_dve)
                    E("vector", (lambda V0, V2: lambda e: e.tensor_add(
                        out=V0, in0=V0, in1=V2))(V0, V2), inc=s_dve)
                    E("vector", (lambda T0, V0, T1: lambda e:
                       e.scalar_tensor_tensor(out=T1, in0=T0, scalar=-2.0,
                                              in1=V0, op0=ALU.add,
                                              op1=ALU.add))(T0, V0, T1),
                       inc=s_dve)
                    E("vector", (lambda T1, par, tt: lambda e:
                       e.scalar_tensor_tensor(
                           out=outst[:, tt % 2, :], in0=T1,
                           scalar=1.0 / 256.0, in1=xst[:, par, :],
                           op0=ALU.mult, op1=ALU.add))(T1, par, tt),
                       waits=[(s_x, s_x.n),
                              (s_out, out_cnt.get(tt - 2, 0))],
                       inc=s_dve)
                    E("gpsimd", (lambda tt: lambda e: e.dma_start(
                        out=out_d[tt * 128:(tt + 1) * 128, :],
                        in_=outst[:, tt % 2, :]))(tt),
                        waits=[(s_dve, s_dve.n)], inc=s_out, inck=16)
                    out_cnt[tt] = s_out.n
            if th == 0:
                E("sync", None, waits=[(s_tp, s_tp.n), (s_w, s_w.n),
                                       (s_x, s_x.n), (s_out, s_out.n)])
                plan.barrier(s_bar)

        E("sync", None, waits=[(s_out, s_out.n)])

        blk = ctx.enter_context(nc.Block())

        @blk.sync
        def _(eng):
            plan.run("sync", eng)

        @blk.tensor
        def _(eng):
            plan.run("tensor", eng)

        @blk.vector
        def _(eng):
            plan.run("vector", eng)

        @blk.scalar
        def _(eng):
            plan.run("scalar", eng)

        @blk.gpsimd
        def _(eng):
            plan.run("gpsimd", eng)

        est.close()
    return nc, sorted(taps.keys())


def kernel(L0_attn_x, L0_attn_Wq, L0_attn_Wk, L0_attn_Wv, L0_attn_Wo,
           L0_attn_bo, L0_ffn_w1, L0_ffn_b1, L0_ffn_w2, L0_ffn_b2):
    x = np.ascontiguousarray(np.asarray(L0_attn_x, dtype=np.float32))
    wts = {k: np.ascontiguousarray(np.asarray(v, dtype=np.float32))
           for k, v in (("wq", L0_attn_Wq), ("wk", L0_attn_Wk),
                        ("wv", L0_attn_Wv), ("wo", L0_attn_Wo),
                        ("bo", L0_attn_bo), ("w1", L0_ffn_w1),
                        ("b1", L0_ffn_b1), ("w2", L0_ffn_w2),
                        ("b2", L0_ffn_b2))}
    nc, tapnames = build_program()
    in_maps = []
    for c in range(8):
        b, qh = c // 2, c % 2
        xq = np.ascontiguousarray(x[b, qh * NQ:(qh + 1) * NQ, :])
        xr = np.ascontiguousarray(x[b, (1 - qh) * NQ:(2 - qh) * NQ, :])
        in_maps.append(dict(wts, xq=xq, xr=xr))
    res = run_bass_kernel_spmd(nc, in_maps, core_ids=list(range(8)))
    out = np.empty((4, 4096, 1024), np.float32)
    for c in range(8):
        out[c // 2, (c % 2) * NQ:(c % 2 + 1) * NQ, :] = res.results[c]["out"]
    if tapnames:
        kernel.taps = [{t: np.asarray(r["tap_" + t]) for t in tapnames}
                       for r in res.results]
    return out


# version 05
# speedup vs baseline: 1.7901x; 1.2676x over previous
import numpy as np
import concourse.bass as bass
import concourse.mybir as mybir
from concourse.bass_utils import run_bass_kernel_spmd
from contextlib import ExitStack

F32 = mybir.dt.float32
BF16 = mybir.dt.bfloat16
I32 = mybir.dt.int32
AF = mybir.ActivationFunctionType
ALU = mybir.AluOpType
AXX = mybir.AxisListType.X

NTILE = 32
NQT = 16
NQ = 2048
NK = 4096
DM = 1024
FF = 4096
NPAIR = 8
NFC = 32
EPS = 1e-5

DEBUG_TAPS = []

_REG = {}


def _ensure_custom_ops():
    if _REG:
        return
    import concourse.dve_ops as dve_ops
    from concourse.dve_op import DveOp
    from concourse.dve_spec import Spec, Src0, Src1, Zero, One, C2, lower
    from concourse.dve_table_gen import dve_ver_for

    have = {o.name: o for o in dve_ops.OPS}

    def _f3_ref(in0, in1, s0, s1, imm2):
        d = in0 - in1
        dt_ = in0.dtype
        ge = lambda t: (d >= t).astype(dt_)
        return ((in1 + dt_.type(imm2)) + ge(-1.0)) + (ge(0.0) + ge(1.0))

    D = Src0 - Src1
    body3 = ((Src1 + C2) + (D >= (Zero - One))) + ((D >= Zero) + (D >= One))

    ver = dve_ver_for("TRN2")
    for name, body, ref in (("MK_FLOOR3", body3, _f3_ref),):
        if name in have:
            _REG[name] = have[name]
            continue
        spec = Spec(body=body)
        sha = lower(spec, ver=ver, perf_en=False, name=name).sha(ver)
        op = DveOp(name, spec, subdim=False, uops_sha={ver: sha}, reference=ref)
        dve_ops.OPS.append(op)
        dve_ops.CUSTOM_DVE_SPECS[name] = spec
        dve_ops.CUSTOM_DVE_REFS[name] = ref
        dve_ops._SUB_OPCODE_FOR_NAME[name] = (
            dve_ops._CUSTOM_DVE_ROW_BASE + len(dve_ops.OPS) - 1)
        _REG[name] = op


class Sem:
    def __init__(self, nc, ctx, name):
        self.h = ctx.enter_context(nc.semaphore(name))
        self.n = 0
        self.name = name


class Plan:
    ENGS = ("sync", "tensor", "vector", "scalar", "gpsimd")

    def __init__(self):
        self.q = {e: [] for e in self.ENGS}
        self.lastw = {e: {} for e in self.ENGS}

    def emit(self, eng, fn, waits=(), inc=None, inck=1):
        lw = self.lastw[eng]
        real = []
        for s, v in waits:
            if v > 0 and lw.get(s.name, -1) < v:
                real.append((s, v))
                lw[s.name] = v
        if inc is not None:
            inc.n += inck
        self.q[eng].append((real, fn, inc, inck))

    def barrier(self, s):
        target = s.n + 5
        for e in self.ENGS:
            self.q[e].append(([], lambda eng: eng.nop(), s, 1))
        s.n = target
        for e in self.ENGS:
            self.q[e].append(([(s, target)], None, None, 0))
            self.lastw[e][s.name] = target

    def run(self, eng_name, eng):
        for waits, fn, inc, k in self.q[eng_name]:
            for s, v in waits:
                eng.wait_ge(s.h, v)
            if fn is None:
                continue
            ins = fn(eng)
            if inc is not None and ins is not None:
                ins.then_inc(inc.h, k)


def build_program():
    _ensure_custom_ops()
    F3 = _REG["MK_FLOOR3"]
    nc = bass.Bass("TRN2", target_bir_lowering=False, debug=False)

    xq_d = nc.dram_tensor("xq", [NQ, DM], F32, kind="ExternalInput").ap()
    xr_d = nc.dram_tensor("xr", [NQ, DM], F32, kind="ExternalInput").ap()
    wq_d = nc.dram_tensor("wq", [DM, DM], F32, kind="ExternalInput").ap()
    wk_d = nc.dram_tensor("wk", [DM, DM], F32, kind="ExternalInput").ap()
    wv_d = nc.dram_tensor("wv", [DM, DM], F32, kind="ExternalInput").ap()
    wo_d = nc.dram_tensor("wo", [DM, DM], F32, kind="ExternalInput").ap()
    bo_d = nc.dram_tensor("bo", [DM], F32, kind="ExternalInput").ap()
    w1_d = nc.dram_tensor("w1", [DM, FF], F32, kind="ExternalInput").ap()
    b1_d = nc.dram_tensor("b1", [FF], F32, kind="ExternalInput").ap()
    w2_d = nc.dram_tensor("w2", [FF, DM], F32, kind="ExternalInput").ap()
    b2_d = nc.dram_tensor("b2", [DM], F32, kind="ExternalInput").ap()
    out_d = nc.dram_tensor("out", [NQ, DM], F32, kind="ExternalOutput").ap()
    resid_d = nc.dram_tensor("residsc", [NQ, DM], F32, kind="Internal").ap()
    rec_d = nc.dram_tensor("recsc", [2, NQ], F32, kind="Internal").ap()

    taps = {}

    def mktap(name, shape, dt):
        if name in DEBUG_TAPS:
            taps[name] = nc.dram_tensor("tap_" + name, shape, dt,
                                        kind="ExternalOutput").ap()

    mktap("xn", [128, 8, 256], BF16)
    mktap("qT", [128, 256], BF16)
    mktap("kT", [128, 256], BF16)
    mktap("vA", [128, 132], BF16)
    mktap("pt", [128, NQ], BF16)
    mktap("avT", [128, 8, 256], BF16)
    mktap("r1", [128, DM], F32)
    mktap("x2T", [128, 8, 256], BF16)
    mktap("hq", [128, 1024], BF16)
    mktap("zf", [128, 1024], F32)
    mktap("resfull", [NQ, DM], F32)
    mktap("x2full", [128, 8, NQ], BF16)
    mktap("hqfull", [128, NFC, 1024], BF16)
    mktap("tn", [128, 1024], F32)
    mktap("tk4", [128, 1024], F32)
    mktap("tr", [128, 1024], F32)
    mktap("ty", [128, 1024], F32)

    with ExitStack() as ctx:
        S = lambda n: Sem(nc, ctx, n)
        s_x = S("s_x")
        s_w = S("s_w")
        s_wb = S("s_wb")
        s_wq3 = [S("s_wq0"), S("s_wq1"), S("s_wq2")]
        s_tp = S("s_tp")
        s_rec = S("s_rec")
        s_res = S("s_res")
        s_out = S("s_out")
        s_pe = S("s_pe")
        s_dve = S("s_dve")
        s_act = S("s_act")
        s_pool = S("s_pool")
        s_exp = S("s_exp")
        s_bar = S("s_bar")

        sb = lambda n, shp, dt=F32: ctx.enter_context(nc.sbuf_tensor(n, shp, dt))
        consts = sb("consts", [128, 8])
        onesbf = sb("onesbf", [1, 128], BF16)
        xst = sb("xst", [128, 2, DM])
        stats = sb("stats", [128, 40])
        mv = sb("mv", [128, 4])
        outst = sb("outst", [128, 2, DM])
        b1t = sb("b1t", [128, 2])
        b1s = sb("b1s", [128, 2])
        ps_a = ctx.enter_context(nc.psum_tensor("ps_a", [128, 2048], F32))

        plan = Plan()
        E = plan.emit

        def _init(gp):
            gp.memset(consts[:, 0:1], EPS)
            gp.memset(consts[:, 1:2], -1.0)
            gp.memset(consts[:, 2:3], 0.5)
            gp.memset(consts[:, 3:4], 0.25)
            return gp.memset(onesbf[:, :], 1.0)

        E("gpsimd", _init, inc=s_pool)
        init_done = s_pool.n

        def plan_ln(xv, pre_waits, sq_ps):
            E("vector", (lambda xv: lambda e: e.tensor_reduce(
                out=mv[:, 0:1], in_=xv, axis=AXX, op=ALU.add))(xv),
                waits=list(pre_waits), inc=s_dve)
            E("scalar", (lambda xv, sq_ps: lambda e: e.activation(
                out=sq_ps, in_=xv, func=AF.Square, bias=0.0, scale=1.0))(
                    xv, sq_ps),
                waits=[(s_dve, s_dve.n)], inc=s_act)
            E("vector", (lambda sq_ps: lambda e: e.tensor_reduce(
                out=mv[:, 1:2], in_=sq_ps, axis=AXX, op=ALU.add))(sq_ps),
                waits=[(s_act, s_act.n)], inc=s_dve)
            E("vector", lambda e: e.tensor_scalar(
                out=mv[:, 0:1], in0=mv[:, 0:1], scalar1=1.0 / 1024.0,
                scalar2=None, op0=ALU.mult),
                waits=[(s_dve, s_dve.n)], inc=s_dve)
            E("scalar", lambda e: e.activation(
                out=mv[:, 2:3], in_=mv[:, 0:1], func=AF.Square,
                bias=0.0, scale=1.0),
                waits=[(s_dve, s_dve.n)], inc=s_act)
            E("vector", lambda e: e.scalar_tensor_tensor(
                out=mv[:, 1:2], in0=mv[:, 1:2], scalar=1.0 / 1024.0,
                in1=mv[:, 2:3], op0=ALU.mult, op1=ALU.subtract),
                waits=[(s_act, s_act.n)], inc=s_dve)
            E("scalar", lambda e: e.activation(
                out=mv[:, 2:3], in_=mv[:, 1:2], func=AF.Ln,
                bias=consts[:, 0:1], scale=1.0),
                waits=[(s_dve, s_dve.n)], inc=s_act)
            E("scalar", lambda e: e.activation(
                out=mv[:, 3:4], in_=mv[:, 2:3], func=AF.Exp,
                bias=0.0, scale=-0.5),
                waits=[(s_act, s_act.n)], inc=s_act)
            return s_act.n


        ab2 = ExitStack()
        sb2 = lambda n, shp, dt=F32: ab2.enter_context(nc.sbuf_tensor(n, shp, dt))
        avT = sb2("avT", [128, 8, NQ], BF16)
        ab1 = ExitStack()
        sb1 = lambda n, shp, dt=F32: ab1.enter_context(nc.sbuf_tensor(n, shp, dt))
        xnT = sb1("xnT", [128, 8, NK], BF16)
        xntok = sb1("xntok", [128, 2, DM], BF16)
        wstg = sb1("wstg", [128, 3, DM])
        wqbf = sb1("wqbf", [128, 8, 128], BF16)
        wkbf = sb1("wkbf", [128, 8, 128], BF16)
        wvbf = sb1("wvbf", [128, 8, 128], BF16)
        qT = sb1("qT", [128, 2, NQ], BF16)
        kT = sb1("kT", [128, 2, NK], BF16)
        vstg = sb1("vstg", [128, 2, 512], BF16)
        vtok = sb1("vtok", [128, 2, 128], BF16)
        vAug = sb1("vAug", [128, 2, 32, 132], BF16)
        ptile = xntok
        avps = xst[0:65, 0, :]
        recb = xst[0:64, 1, :]

        recsb = outst[0:1, 0, :]
        ps_b_st = ExitStack()
        ps_b = ps_b_st.enter_context(nc.psum_tensor("ps_b", [128, 2048], F32))

        def _vaug_ones(gp):
            gp.memset(vAug[:, :, :, 64:65], 1.0)
            return gp.memset(vAug[:, :, :, 130:131], 1.0)

        E("gpsimd", _vaug_ones, inc=s_pool)
        vaug_ones_cnt = s_pool.n

        dve_norm = {}
        tp_tile = {}
        act_rstd = {}
        for tt in range(NTILE):
            src = xq_d if tt < NQT else xr_d
            row = (tt % NQT) * 128
            par = tt % 2
            E("sync", (lambda par, s0: lambda e: e.dma_start(
                out=xst[:, par, :], in_=s0))(par, src[row:row + 128, :]),
                waits=[(s_dve, dve_norm.get(tt - 2, 0)),
                       (s_pool, init_done)], inc=s_x, inck=16)
            xv = xst[:, par, :]
            act_rstd[tt] = plan_ln(xv, [(s_x, s_x.n),
                                        (s_act, act_rstd.get(tt - 1, 0))],
                                   ps_a[:, 0:1024])
            E("vector", (lambda xv, par: lambda e: e.tensor_scalar(
                out=xntok[:, par, :], in0=xv, scalar1=mv[:, 0:1],
                scalar2=mv[:, 3:4], op0=ALU.subtract, op1=ALU.mult))(xv, par),
                waits=[(s_act, s_act.n), (s_tp, tp_tile.get(tt - 2, 0))],
                inc=s_dve)
            dve_norm[tt] = s_dve.n
            for a in range(8):
                E("sync", (lambda a, par, tt: lambda e: e.dma_start(
                    out=xnT[:, a, tt * 128:(tt + 1) * 128],
                    in_=xntok[:, par, a * 128:(a + 1) * 128], transpose=True))(
                        a, par, tt),
                    waits=[(s_dve, s_dve.n)], inc=s_tp, inck=16)
            tp_tile[tt] = s_tp.n
        tp_after_A = s_tp.n

        if "xn" in taps:
            E("gpsimd", lambda e: e.dma_start(out=taps["xn"],
                                              in_=xnT[:, :, 0:256]),
              waits=[(s_tp, tp_after_A)], inc=s_out, inck=16)

        wbufs = (wqbf, wkbf, wvbf)
        wsrc = (wq_d, wk_d, wv_d)
        cast_cnt = {}
        wdma_cnt = {}
        evac_dve = {}
        vtp_g = {}
        vaug_cp_g = {}
        proj_pe_done = {}
        qkv_done = {}
        vaug_done = {}
        GG = [0]

        def plan_wdma(q):
            for wi in range(3):
                sidx = 3 * q + wi
                spar = sidx % 3
                src = wsrc[wi][:, q * 128:(q + 1) * 128].rearrange(
                    "(a q2) m -> q2 a m", q2=128)
                E("sync", (lambda spar, src: lambda e: e.dma_start(
                    out=wstg[:, spar, :].rearrange("q2 (a m) -> q2 a m", a=8),
                    in_=src))(spar, src),
                    waits=[(s_pool, cast_cnt.get(sidx - 3, 0))],
                    inc=s_wq3[wi], inck=16)
                wdma_cnt[sidx] = s_wq3[wi].n

        def build_pair_jobs(q):
            par = q % 2
            jobs = []

            def wcast_job(wi):
                def j():
                    sidx = 3 * q + wi
                    spar = sidx % 3
                    E("vector", (lambda spar, wi: lambda e: e.tensor_copy(
                        out=wbufs[wi][:, :, :],
                        in_=wstg[:, spar, :].rearrange(
                            "q2 (a m) -> q2 a m", a=8)))(spar, wi),
                        waits=[(s_wq3[wi], wdma_cnt[sidx]),
                               (s_pe, proj_pe_done.get(q - 1, 0))],
                        inc=s_pool)
                    cast_cnt[sidx] = s_pool.n
                return j

            def wdma_job(wi):
                def j():
                    if q + 1 >= NPAIR:
                        return
                    sidx = 3 * (q + 1) + wi
                    spar = sidx % 3
                    src = wsrc[wi][:, (q + 1) * 128:(q + 2) * 128].rearrange(
                        "(a q2) m -> q2 a m", q2=128)
                    E("sync", (lambda spar, src: lambda e: e.dma_start(
                        out=wstg[:, spar, :].rearrange(
                            "q2 (a m) -> q2 a m", a=8),
                        in_=src))(spar, src),
                        waits=[(s_pool, cast_cnt.get(sidx - 3, 0))],
                        inc=s_wq3[wi], inck=16)
                    wdma_cnt[sidx] = s_wq3[wi].n
                return j

            def qk_job(wi, wbf, dst, c):
                def j():
                    g = GG[0]
                    GG[0] += 1
                    buf = g % 2

                    def mmfn(e):
                        ins = None
                        for a in range(8):
                            ins = e.matmul(
                                ps_b[:, 1024 + buf * 512:1536 + buf * 512],
                                wbf[:, a, :],
                                xnT[:, a, c * 512:(c + 1) * 512],
                                start=(a == 0), stop=(a == 7))
                        return ins

                    E("tensor", mmfn,
                      waits=[(s_pool, cast_cnt[3 * q + wi]),
                             (s_dve, evac_dve.get(g - 2, 0)),
                             (s_tp, tp_after_A)],
                      inc=s_pe)
                    E("vector", (lambda buf, c: lambda e: e.tensor_copy(
                        out=dst[:, par, c * 512:(c + 1) * 512],
                        in_=ps_b[:, 1024 + buf * 512:1536 + buf * 512]))(
                            buf, c),
                        waits=[(s_pe, s_pe.n)], inc=s_dve)
                    evac_dve[g] = s_dve.n
                return j

            def v_job(c):
                def j():
                    g = GG[0]
                    GG[0] += 1
                    buf = g % 2
                    gi = 8 * q + c

                    def mmfn(e):
                        ins = None
                        for a in range(8):
                            ins = e.matmul(
                                ps_b[:, 1024 + buf * 512:1536 + buf * 512],
                                wvbf[:, a, :],
                                xnT[:, a, c * 512:(c + 1) * 512],
                                start=(a == 0), stop=(a == 7))
                        return ins

                    E("tensor", mmfn,
                      waits=[(s_pool, cast_cnt[3 * q + 2]),
                             (s_dve, evac_dve.get(g - 2, 0)),
                             (s_tp, tp_after_A)],
                      inc=s_pe)
                    E("vector", (lambda buf, c: lambda e: e.tensor_copy(
                        out=vstg[:, c % 2, :],
                        in_=ps_b[:, 1024 + buf * 512:1536 + buf * 512]))(
                            buf, c),
                        waits=[(s_pe, s_pe.n),
                               (s_tp, vtp_g.get(gi - 2, 0))],
                        inc=s_dve)
                    evac_dve[g] = s_dve.n
                    for jj in range(4):
                        kt = 4 * c + jj
                        ktg = 32 * q + kt
                        E("sync", (lambda c, jj, kt: lambda e: e.dma_start(
                            out=vtok[:, kt % 2, :],
                            in_=vstg[:, c % 2, jj * 128:(jj + 1) * 128],
                            transpose=True))(c, jj, kt),
                            waits=[(s_dve, max(evac_dve[g],
                                               vaug_cp_g.get(ktg - 2, 0))),
                                   (s_pool, vaug_ones_cnt)],
                            inc=s_tp, inck=16)
                        if jj == 3:
                            vtp_g[gi] = s_tp.n
                        E("vector", (lambda kt: lambda e: e.tensor_copy(
                            out=vAug[:, par, kt, :].rearrange(
                                "p (g2 c2) -> p g2 c2", c2=66)[:, :, 0:64],
                            in_=vtok[:, kt % 2, :].rearrange(
                                "p (g2 c2) -> p g2 c2", c2=64)))(kt),
                            waits=[(s_tp, s_tp.n)], inc=s_dve)
                        vaug_cp_g[ktg] = s_dve.n
                return j

            def record_job():
                def j():
                    proj_pe_done[q] = s_pe.n
                    qkv_done[q] = s_dve.n
                    vaug_done[q] = s_tp.n
                return j

            jobs.append(wcast_job(0))
            for c in range(4):
                jobs.append(qk_job(0, wqbf, qT, c))
            jobs.append(wdma_job(0))
            jobs.append(wcast_job(1))
            for c in range(8):
                jobs.append(qk_job(1, wkbf, kT, c))
            jobs.append(wdma_job(1))
            jobs.append(wcast_job(2))
            for c in range(8):
                jobs.append(v_job(c))
            jobs.append(wdma_job(2))
            jobs.append(record_job())
            return jobs

        plan_wdma(0)
        for jb in build_pair_jobs(0):
            jb()

        exp_par = {}
        exp_done = {}
        psb_free_dve = 0
        recb_reader = 0

        for p in range(NPAIR):
            pp = p % 2
            jobs = build_pair_jobs(p + 1) if p + 1 < NPAIR else []
            ji = [0]

            def inject(jobs=jobs, ji=ji):
                if ji[0] < len(jobs):
                    jobs[ji[0]]()
                    ji[0] += 1

            if p == 0 and "qT" in taps:
                E("gpsimd", lambda e: e.dma_start(out=taps["qT"],
                                                  in_=qT[:, 0, 0:256]),
                  waits=[(s_dve, qkv_done[0])], inc=s_out, inck=16)
            if p == 0 and "kT" in taps:
                E("gpsimd", lambda e: e.dma_start(out=taps["kT"],
                                                  in_=kT[:, 0, 0:256]),
                  waits=[(s_dve, qkv_done[0])], inc=s_out, inck=16)
            if p == 0 and "vA" in taps:
                E("gpsimd", lambda e: e.dma_start(
                    out=taps["vA"], in_=vAug[:, 0, 0, :]),
                    waits=[(s_tp, vaug_done[0])], inc=s_out, inck=16)

            for h in range(2):
                hb = 64 * h
                base = 0 if h == 0 else 66
                for qh in range(2):
                    qb = qh * 1024

                    def avfn(kt, kpar, base=base, pp=pp):
                        def f(e):
                            ins = None
                            for qc in range(2):
                                ins = e.matmul(
                                    ps_b[0:65, qc * 512:(qc + 1) * 512],
                                    vAug[:, pp, kt, base:base + 65],
                                    ptile[:, kpar, qc * 512:(qc + 1) * 512],
                                    start=(kt == 0), stop=(kt == 31))
                            return ins
                        return f

                    for kt in range(32):
                        kpar = kt % 2

                        def scfn(hb=hb, kt=kt, qb=qb, pp=pp, kpar=kpar):
                            def f(e):
                                ins = None
                                for qc in range(2):
                                    ins = e.matmul(
                                        ps_a[:, kpar * 1024 + qc * 512:
                                             kpar * 1024 + qc * 512 + 512],
                                        kT[hb:hb + 64, pp,
                                           kt * 128:(kt + 1) * 128],
                                        qT[hb:hb + 64, pp,
                                           qb + qc * 512:qb + qc * 512 + 512],
                                        start=True, stop=True)
                                return ins
                            return f

                        E("tensor", scfn(),
                          waits=[(s_exp, exp_par.get(kpar, 0)),
                                 (s_dve, qkv_done[p]),
                                 (s_tp, vaug_done[p])],
                          inc=s_pe)
                        E("scalar", (lambda kpar: lambda e: e.activation(
                            out=ptile[:, kpar, :],
                            in_=ps_a[:, kpar * 1024:kpar * 1024 + 1024],
                            func=AF.Exp, bias=consts[:, 1:2],
                            scale=1.0 / 32.0))(kpar),
                            waits=[(s_pe, s_pe.n)], inc=s_exp)
                        exp_par[kpar] = s_exp.n
                        exp_done[kt] = s_exp.n
                        if kt > 0:
                            avw = [(s_exp, exp_done[kt - 1])]
                            if kt - 1 == 0:
                                avw.append((s_dve, psb_free_dve))
                            E("tensor", avfn(kt - 1, (kt - 1) % 2),
                              waits=avw, inc=s_pe)
                        if kt % 4 == 1:
                            inject()
                    E("tensor", avfn(31, 1),
                      waits=[(s_exp, exp_done[31])], inc=s_pe)
                    av_done = s_pe.n
                    E("vector", lambda e: e.tensor_copy(
                        out=avps[:, :], in_=ps_b[0:65, 0:1024]),
                        waits=[(s_pe, av_done)], inc=s_dve)
                    psb_free_dve = s_dve.n
                    E("vector", lambda e: e.reciprocal(
                        out=recsb, in_=avps[64:65, :]),
                        waits=[(s_dve, s_dve.n)], inc=s_dve)
                    E("gpsimd", (lambda h, qb: lambda e: e.dma_start(
                        out=rec_d[h, None, qb:qb + 1024], in_=recsb))(h, qb),
                        waits=[(s_dve, s_dve.n), (s_rec, s_rec.n)],
                        inc=s_rec, inck=16)
                    E("gpsimd", (lambda h, qb: lambda e: e.dma_start(
                        out=recb[:, :],
                        in_=rec_d[h, None, qb:qb + 1024].broadcast_to(
                            [64, 1024])))(h, qb),
                        waits=[(s_rec, s_rec.n), (s_dve, recb_reader)],
                        inc=s_rec, inck=16)
                    E("vector", (lambda p, hb, qb: lambda e: e.tensor_mul(
                        out=avT[hb:hb + 64, p, qb:qb + 1024],
                        in0=avps[0:64, :], in1=recb[:, :]))(p, hb, qb),
                        waits=[(s_rec, s_rec.n)], inc=s_dve)
                    recb_reader = s_dve.n
            while ji[0] < len(jobs):
                jobs[ji[0]]()
                ji[0] += 1
        avT_done = s_dve.n
        if "avT" in taps:
            E("gpsimd", lambda e: e.dma_start(out=taps["avT"],
                                              in_=avT[:, :, 0:256]),
              waits=[(s_dve, avT_done)], inc=s_out, inck=16)

        E("sync", None, waits=[(s_tp, s_tp.n), (s_w, s_w.n), (s_x, s_x.n),
                               (s_wb, s_wb.n), (s_rec, s_rec.n),
                               (s_res, s_res.n), (s_out, s_out.n),
                               (s_wq3[0], s_wq3[0].n), (s_wq3[1], s_wq3[1].n),
                               (s_wq3[2], s_wq3[2].n)])
        plan.barrier(s_bar)
        ab1.close()
        xnst = ExitStack()
        xn2T = xnst.enter_context(nc.sbuf_tensor("xn2T", [128, 8, NQ], BF16))

        cst_tmp = ExitStack()
        sbc = lambda n, shp, dt=F32: cst_tmp.enter_context(
            nc.sbuf_tensor(n, shp, dt))
        wostg = sbc("wostg", [128, 2, DM])
        wobf = sbc("wobf", [128, 8, DM], BF16)
        x2tok = sbc("x2tok", [128, 2, DM], BF16)
        r1t = sbc("r1t", [128, 2, DM])
        bo_bf = sbc("bo_bf", [1, DM], BF16)
        E("sync", lambda e: e.dma_start(out=r1t[0:1, 0, :], in_=bo_d[None, :]),
          inc=s_wb, inck=16)
        E("vector", lambda e: e.tensor_copy(out=bo_bf[:, :], in_=r1t[0:1, 0, :]),
          waits=[(s_wb, s_wb.n)], inc=s_pool)
        bocast = s_pool.n

        for a in range(8):
            par = a % 2
            E("sync", (lambda a, par: lambda e: e.dma_start(
                out=wostg[:, par, :],
                in_=wo_d[a * 128:(a + 1) * 128, :]))(a, par),
                waits=[(s_pool, s_pool.n if a >= 2 else 0)],
                inc=s_w, inck=16)
            E("vector", (lambda a, par: lambda e: e.tensor_copy(
                out=wobf[:, a, :], in_=wostg[:, par, :]))(a, par),
                waits=[(s_w, s_w.n)], inc=s_pool)
        wo_cast = s_pool.n

        r1_cnt = {}
        res_cnt = {}
        x2tp = {}
        rstd2 = {}
        for tt in range(NQT):
            par = tt % 2
            E("sync", (lambda tt, par: lambda e: e.dma_start(
                out=xst[:, par, :],
                in_=xq_d[tt * 128:(tt + 1) * 128, :]))(tt, par),
                waits=[(s_dve, max(avT_done, r1_cnt.get(tt - 2, 0)))],
                inc=s_x, inck=16)

            def wofn(tt=tt, par=par):
                def f(e):
                    ins = None
                    for nc2 in range(2):
                        sl = slice(par * 1024 + nc2 * 512,
                                   par * 1024 + nc2 * 512 + 512)
                        for a in range(8):
                            e.matmul(ps_a[:, sl],
                                     avT[:, a, tt * 128:(tt + 1) * 128],
                                     wobf[:, a, nc2 * 512:(nc2 + 1) * 512],
                                     start=(a == 0), stop=False)
                        ins = e.matmul(ps_a[:, sl], onesbf[:, 0:128],
                                       bo_bf[:, nc2 * 512:(nc2 + 1) * 512],
                                       start=False, stop=True)
                    return ins
                return f

            E("tensor", wofn(),
              waits=[(s_pool, max(wo_cast, bocast)),
                     (s_dve, r1_cnt.get(tt - 2, 0)), (s_exp, s_exp.n)],
              inc=s_pe)
            pe_wo = s_pe.n
            E("vector", (lambda par: lambda e: e.tensor_add(
                out=r1t[:, par, :],
                in0=ps_a[:, par * 1024:par * 1024 + 1024],
                in1=xst[:, par, :]))(par),
                waits=[(s_pe, pe_wo), (s_x, s_x.n),
                       (s_res, res_cnt.get(tt - 2, 0)),
                       (s_tp, x2tp.get(tt - 2, 0)),
                       (s_act, rstd2.get(tt - 2, 0)),
                       (s_pool, bocast)],
                inc=s_dve)
            r1_cnt[tt] = s_dve.n
            E("gpsimd", (lambda tt, par: lambda e: e.dma_start(
                out=resid_d[tt * 128:(tt + 1) * 128, :],
                in_=r1t[:, par, :]))(tt, par),
                waits=[(s_dve, s_dve.n)], inc=s_res, inck=16)
            res_cnt[tt] = s_res.n
            if tt == 0 and "r1" in taps:
                E("gpsimd", lambda e: e.dma_start(out=taps["r1"],
                                                  in_=r1t[:, 0, :]),
                  waits=[(s_dve, s_dve.n)], inc=s_out, inck=16)
            rstd2[tt] = plan_ln(r1t[:, par, :],
                                 [(s_dve, s_dve.n),
                                  (s_act, rstd2.get(tt - 1, 0))],
                                 ps_b[:, 0:1024])
            E("vector", (lambda par: lambda e: e.tensor_scalar(
                out=x2tok[:, par, :], in0=r1t[:, par, :],
                scalar1=mv[:, 0:1], scalar2=mv[:, 3:4],
                op0=ALU.subtract, op1=ALU.mult))(par),
                waits=[(s_act, s_act.n), (s_tp, x2tp.get(tt - 2, 0))],
                inc=s_dve)
            for a in range(8):
                E("sync", (lambda a, par, tt: lambda e: e.dma_start(
                    out=xn2T[:, a, tt * 128:(tt + 1) * 128],
                    in_=x2tok[:, par, a * 128:(a + 1) * 128],
                    transpose=True))(a, par, tt),
                    waits=[(s_dve, s_dve.n)], inc=s_tp, inck=16)
            x2tp[tt] = s_tp.n
        xn2_done = s_tp.n
        if "resfull" in taps:
            E("gpsimd", lambda e: e.dma_start(out=taps["resfull"], in_=resid_d),
              waits=[(s_res, s_res.n)], inc=s_out, inck=16)
        if "x2full" in taps:
            E("gpsimd", lambda e: e.dma_start(out=taps["x2full"], in_=xn2T[:, :, :]),
              waits=[(s_tp, xn2_done)], inc=s_out, inck=16)
        if "x2T" in taps:
            E("gpsimd", lambda e: e.dma_start(out=taps["x2T"],
                                              in_=xn2T[:, :, 0:256]),
              waits=[(s_tp, xn2_done)], inc=s_out, inck=16)

        E("sync", None, waits=[(s_tp, s_tp.n), (s_w, s_w.n), (s_x, s_x.n),
                               (s_wb, s_wb.n), (s_rec, s_rec.n),
                               (s_res, s_res.n), (s_out, s_out.n),
                               (s_wq3[0], s_wq3[0].n), (s_wq3[1], s_wq3[1].n),
                               (s_wq3[2], s_wq3[2].n)])
        plan.barrier(s_bar)
        cst_tmp.close()
        ps_b_st.close()

        est = ExitStack()
        sbe = lambda n, shp, dt=F32: est.enter_context(
            nc.sbuf_tensor(n, shp, dt))
        hq = sbe("hq", [128, NFC, 1024], BF16)
        w1stg = sbe("w1stg", [128, 2, DM])
        stg_cast = {0: 0, 1: 0}
        w1bf = sbe("w1bf", [128, 2, 8, 128], BF16)
        w2stg = w1stg
        w2v = avT[:, 4:6, :].rearrange("p a m -> p (a m)")
        TP = sbe("tpool", [128, 2, 4, 1024])
        TBv = avT.rearrange("p a (b c) -> p (a b) c", b=2)
        TI = sbe("tint", [128, 2, 1024], I32)
        zfb = sbe("zfb", [128, 2, 1024])
        b2_bf = sbe("b2_bf", [1, DM], BF16)
        E("sync", lambda e: e.dma_start(out=xst[0:1, 0, :], in_=b2_d[None, :]),
          waits=[(s_res, s_res.n)], inc=s_wb, inck=16)
        E("vector", lambda e: e.tensor_copy(out=b2_bf[:, :], in_=xst[0:1, 0, :]),
          waits=[(s_wb, s_wb.n)], inc=s_pool)
        b2cast = s_pool.n
        ps_c = est.enter_context(nc.psum_tensor("ps_c", [128, 2048], F32))

        for th in range(2):
            qoff = th * 1024
            w1cast = {}
            zf_act = {}
            ch_dve = {}
            ch_pool = {}
            mm1pe = {}
            b1s_dve = {}
            for f in range(NFC):
                par = f % 2
                src = w1_d[:, f * 128:(f + 1) * 128].rearrange(
                    "(a q) m -> q a m", q=128)
                E("sync", (lambda par, src: lambda e: e.dma_start(
                    out=w1stg[:, par, :].rearrange("q (a m) -> q a m", a=8),
                    in_=src))(par, src),
                    waits=[(s_pool, stg_cast[par])],
                    inc=s_w, inck=16)
                E("sync", (lambda par, f: lambda e: e.dma_start(
                    out=b1t[:, par:par + 1],
                    in_=b1_d[f * 128:(f + 1) * 128].rearrange(
                        "(q o) -> q o", o=1)))(par, f),
                    waits=[(s_dve, b1s_dve.get(f - 2, 0))],
                    inc=s_wb, inck=16)
                E("vector", (lambda par: lambda e: e.tensor_copy(
                    out=w1bf[:, par, :, :],
                    in_=w1stg[:, par, :].rearrange("q (a m) -> q a m",
                                                   a=8)))(par),
                    waits=[(s_w, s_w.n),
                           (s_pe, mm1pe.get(f - 2, 0))], inc=s_pool)
                w1cast[f] = s_pool.n
                stg_cast[par] = s_pool.n
                E("vector", (lambda par: lambda e: e.tensor_scalar(
                    out=b1s[:, par:par + 1], in0=b1t[:, par:par + 1],
                    scalar1=256.0, scalar2=0.5, op0=ALU.mult,
                    op1=ALU.add))(par),
                    waits=[(s_wb, s_wb.n)], inc=s_dve)
                b1s_cnt = s_dve.n
                b1s_dve[f] = s_dve.n

                def mm1fn(par=par, qoff=qoff):
                    def fx(e):
                        ins = None
                        for c in range(2):
                            sl = slice(par * 1024 + c * 512,
                                       par * 1024 + c * 512 + 512)
                            for a in range(8):
                                ins = e.matmul(
                                    ps_a[:, sl], w1bf[:, par, a, :],
                                    xn2T[:, a,
                                         qoff + c * 512:qoff + c * 512 + 512],
                                    start=(a == 0), stop=(a == 7))
                        return ins
                    return fx

                E("tensor", mm1fn(),
                  waits=[(s_pool, w1cast[f]), (s_tp, xn2_done),
                         (s_act, zf_act.get(f - 2, 0))],
                  inc=s_pe)
                mm1pe[f] = s_pe.n

                T = lambda i: TP[:, par, i, :]
                IT = TI[:, par, :]
                zf = zfb[:, par, :]
                wch = [(s_pe, s_pe.n), (s_dve, b1s_cnt),
                       (s_dve, ch_dve.get(f - 2, 0)),
                       (s_pool, ch_pool.get(f - 2, 0))]
                E("scalar", (lambda zf, par: lambda e: e.activation(
                    out=zf, in_=ps_a[:, par * 1024:par * 1024 + 1024],
                    func=AF.Identity, bias=b1s[:, par:par + 1],
                    scale=256.0))(zf, par),
                    waits=wch, inc=s_act)
                zf_act[f] = s_act.n
                A, B, C, D = T(0), T(1), T(2), T(3)
                U = lambda i: TBv[:, par * 3 + i, :]
                U0, U1, U2 = U(0), U(1), U(2)
                GE = ALU.is_ge
                E("vector", (lambda IT, zf: lambda e: e.tensor_copy(
                    out=IT, in_=zf))(IT, zf),
                    waits=[(s_act, s_act.n)], inc=s_dve)
                E("vector", (lambda IT, A=A: lambda e: e.tensor_copy(
                    out=A, in_=IT))(IT), inc=s_dve)
                E("vector", (lambda zf, A=A, B=B: lambda e: e.tensor_sub(
                    out=B, in0=zf, in1=A))(zf), inc=s_dve)
                for i, thr in enumerate((-1.0, 0.0, 1.0)):
                    E("vector", (lambda B=B, u=U(i), thr=thr: lambda e:
                       e.tensor_scalar(out=u, in0=B, scalar1=thr,
                                       scalar2=None, op0=GE))(),
                       inc=s_dve)
                E("vector", (lambda U0=U0, U1=U1: lambda e: e.tensor_add(
                    out=U0, in0=U0, in1=U1))(), inc=s_dve)
                E("vector", (lambda U0=U0, U2=U2: lambda e: e.tensor_add(
                    out=U0, in0=U0, in1=U2))(), inc=s_dve)
                E("vector", (lambda A=A, U0=U0, C=C: lambda e:
                   e.scalar_tensor_tensor(out=C, in0=A, scalar=-2.0,
                                          in1=U0, op0=ALU.add,
                                          op1=ALU.add))(), inc=s_dve)
                if th == 0 and f == 0 and "tn" in taps:
                    E("gpsimd", (lambda C=C: lambda e: e.dma_start(
                        out=taps["tn"], in_=C))(),
                        waits=[(s_dve, s_dve.n)], inc=s_out, inck=16)
                E("vector", (lambda zf, B=B: lambda e: e.tensor_scalar(
                    out=B, in0=zf, scalar1=1.0 / 64.0, scalar2=None,
                    op0=ALU.mult))(zf), inc=s_dve)
                E("vector", (lambda IT, B=B: lambda e: e.tensor_copy(
                    out=IT, in_=B))(IT), inc=s_dve)
                E("vector", (lambda IT, A=A: lambda e: e.tensor_copy(
                    out=A, in_=IT))(IT), inc=s_dve)
                E("vector", (lambda B=B, A=A, U0=U0: lambda e:
                   e.tensor_tensor(out=U0, in0=B, in1=A, op=GE))(),
                   inc=s_dve)
                E("vector", (lambda A=A, U0=U0, D=D: lambda e:
                   e.scalar_tensor_tensor(out=D, in0=A, scalar=-1.0,
                                          in1=U0, op0=ALU.add,
                                          op1=ALU.add))(), inc=s_dve)
                E("vector", (lambda D=D, C=C, B=B: lambda e:
                   e.scalar_tensor_tensor(out=B, in0=D, scalar=-64.0,
                                          in1=C, op0=ALU.mult,
                                          op1=ALU.add))(), inc=s_dve)
                dve_r = s_dve.n
                if th == 0 and f == 0 and "tr" in taps:
                    E("gpsimd", (lambda D=D: lambda e: e.dma_start(
                        out=taps["tk4"], in_=D))(),
                        waits=[(s_dve, s_dve.n)], inc=s_out, inck=16)
                    E("gpsimd", (lambda B=B: lambda e: e.dma_start(
                        out=taps["tr"], in_=B))(),
                        inc=s_out, inck=16)
                E("scalar", (lambda D=D, A=A: lambda e: e.activation(
                    out=A, in_=D, func=AF.Gelu, bias=0.0, scale=0.25))(),
                    waits=[(s_dve, dve_r)], inc=s_act)
                E("scalar", (lambda D=D, C=C: lambda e: e.activation(
                    out=C, in_=D, func=AF.Gelu, bias=consts[:, 3:4],
                    scale=0.25))(), inc=s_act)
                act_g = s_act.n
                E("vector", (lambda C=C, A=A: lambda e: e.tensor_sub(
                    out=C, in0=C, in1=A))(),
                    waits=[(s_act, act_g)], inc=s_dve)
                E("vector", (lambda C=C, B=B: lambda e:
                   e.scalar_tensor_tensor(out=C, in0=C, scalar=1.0 / 64.0,
                                          in1=B, op0=ALU.mult,
                                          op1=ALU.mult))(), inc=s_dve)
                E("vector", (lambda C=C, A=A: lambda e: e.tensor_add(
                    out=C, in0=C, in1=A))(), inc=s_dve)
                dve_y = s_dve.n
                if th == 0 and f == 0 and "ty" in taps:
                    E("gpsimd", (lambda C=C: lambda e: e.dma_start(
                        out=taps["ty"], in_=C))(),
                        waits=[(s_dve, s_dve.n)], inc=s_out, inck=16)
                E("scalar", (lambda zf, C=C: lambda e: e.activation(
                    out=zf, in_=C, func=AF.Identity, bias=consts[:, 2:3],
                    scale=256.0))(zf),
                    waits=[(s_dve, dve_y)], inc=s_act)
                E("vector", (lambda IT, zf: lambda e: e.tensor_copy(
                    out=IT, in_=zf))(IT, zf),
                    waits=[(s_act, s_act.n)], inc=s_dve)
                ch_pool[f] = s_pool.n
                E("vector", (lambda IT, A=A: lambda e: e.tensor_copy(
                    out=A, in_=IT))(IT), inc=s_dve)
                E("vector", (lambda zf, A=A, B=B: lambda e: e.tensor_sub(
                    out=B, in0=zf, in1=A))(zf), inc=s_dve)
                E("vector", (lambda B=B, U0=U0: lambda e: e.tensor_scalar(
                    out=U0, in0=B, scalar1=-2.0, scalar2=None,
                    op0=GE))(), inc=s_dve)
                E("vector", (lambda B=B, U1=U1: lambda e: e.tensor_scalar(
                    out=U1, in0=B, scalar1=-1.0, scalar2=None,
                    op0=GE))(), inc=s_dve)
                E("vector", (lambda U0=U0, U1=U1: lambda e: e.tensor_add(
                    out=U0, in0=U0, in1=U1))(), inc=s_dve)
                E("vector", (lambda B=B, U1=U1: lambda e: e.tensor_scalar(
                    out=U1, in0=B, scalar1=0.0, scalar2=None,
                    op0=GE))(), inc=s_dve)
                E("vector", (lambda B=B, U2=U2: lambda e: e.tensor_scalar(
                    out=U2, in0=B, scalar1=1.0, scalar2=None,
                    op0=GE))(), inc=s_dve)
                E("vector", (lambda U1=U1, U2=U2: lambda e: e.tensor_add(
                    out=U1, in0=U1, in1=U2))(), inc=s_dve)
                E("vector", (lambda U0=U0, U1=U1: lambda e: e.tensor_add(
                    out=U0, in0=U0, in1=U1))(), inc=s_dve)
                E("vector", (lambda B=B, U1=U1: lambda e: e.tensor_scalar(
                    out=U1, in0=B, scalar1=2.0, scalar2=None,
                    op0=GE))(), inc=s_dve)
                E("vector", (lambda U0=U0, U1=U1: lambda e: e.tensor_add(
                    out=U0, in0=U0, in1=U1))(), inc=s_dve)
                E("vector", (lambda A=A, U0=U0, f=f: lambda e:
                   e.scalar_tensor_tensor(out=hq[:, f, :], in0=A,
                                          scalar=-3.0, in1=U0,
                                          op0=ALU.add, op1=ALU.add))(),
                   inc=s_dve)
                ch_dve[f] = s_dve.n
                if th == 0 and f == 0 and "zf" in taps:
                    E("gpsimd", (lambda zf: lambda e: e.dma_start(
                        out=taps["zf"], in_=zf))(zf),
                        waits=[(s_act, zf_act[0])], inc=s_out, inck=16)
                if th == 0 and f == 0 and "hq" in taps:
                    E("gpsimd", lambda e: e.dma_start(out=taps["hq"],
                                                      in_=hq[:, 0, :]),
                      waits=[(s_dve, s_dve.n)], inc=s_out, inck=16)
            chains_done = s_dve.n
            if th == 0 and "hqfull" in taps:
                E("gpsimd", lambda e: e.dma_start(out=taps["hqfull"],
                                                  in_=hq[:, :, :]),
                  waits=[(s_dve, chains_done)], inc=s_out, inck=16)

            w2cast = {}
            mm2pe = {}
            q3act = 0
            out_cnt = {}
            for sp in range(2):
                tiles = [th * 8 + sp * 4 + i for i in range(4)]
                for f in range(NFC):
                    par = f % 2
                    E("sync", (lambda par, f: lambda e: e.dma_start(
                        out=w2stg[:, par, :],
                        in_=w2_d[f * 128:(f + 1) * 128, :]))(par, f),
                        waits=[(s_pool, stg_cast[par])],
                        inc=s_w, inck=16)
                    E("vector", (lambda par, f: lambda e: e.tensor_scalar(
                        out=w2v[:, (f % 3) * 1024:(f % 3) * 1024 + 1024],
                        in0=w2stg[:, par, :],
                        scalar1=1.0 / 256.0, scalar2=None,
                        op0=ALU.mult))(par, f),
                        waits=[(s_w, s_w.n),
                               (s_pe, mm2pe.get((sp, f - 3), 0))],
                        inc=s_pool)
                    w2cast[(sp, f)] = s_pool.n
                    stg_cast[par] = s_pool.n

                    def mm2fn(f=f, tiles=tiles, th=th):
                        def fx(e):
                            ins = None
                            for i, tt in enumerate(tiles):
                                ps = ps_a if i < 2 else ps_c
                                off = (i % 2) * 1024
                                lt = hq[:, f,
                                        (tt % 8) * 128:(tt % 8) * 128 + 128]
                                for c in range(2):
                                    ins = e.matmul(
                                        ps[:, off + c * 512:
                                           off + c * 512 + 512],
                                        lt,
                                        w2v[:, (f % 3) * 1024 + c * 512:
                                            (f % 3) * 1024 + c * 512 + 512],
                                        start=(f == 0), stop=False)
                            return ins
                        return fx

                    E("tensor", mm2fn(),
                      waits=[(s_pool, w2cast[(sp, f)]),
                             (s_dve, chains_done), (s_act, q3act)],
                      inc=s_pe)
                    mm2pe[(sp, f)] = s_pe.n

                def biasfn(tiles=tiles):
                    def fx(e):
                        ins = None
                        for i, tt in enumerate(tiles):
                            ps = ps_a if i < 2 else ps_c
                            off = (i % 2) * 1024
                            for c in range(2):
                                ins = e.matmul(
                                    ps[:, off + c * 512:off + c * 512 + 512],
                                    onesbf[:, 0:128],
                                    b2_bf[:, c * 512:(c + 1) * 512],
                                    start=False, stop=True)
                        return ins
                    return fx

                E("tensor", biasfn(), waits=[(s_pool, b2cast)], inc=s_pe)
                mm2_all = s_pe.n

                for i, tt in enumerate(tiles):
                    ps = ps_a if i < 2 else ps_c
                    off = (i % 2) * 1024
                    par = tt % 2
                    E("sync", (lambda tt, par: lambda e: e.dma_start(
                        out=xst[:, par, :],
                        in_=resid_d[tt * 128:(tt + 1) * 128, :]))(tt, par),
                        waits=[(s_res, 16 * NQT),
                               (s_dve, s_dve.n)],
                        inc=s_x, inck=16)
                    ip = i % 2
                    T2 = TP[:, ip, 2, :]
                    T0 = TP[:, ip, 0, :]
                    T1 = TP[:, ip, 1, :]
                    V0 = TBv[:, ip * 3 + 0, :]
                    V1 = TBv[:, ip * 3 + 1, :]
                    V2 = TBv[:, ip * 3 + 2, :]
                    IT = TI[:, ip, :]
                    E("scalar", (lambda ps, off, T2: lambda e: e.activation(
                        out=T2, in_=ps[:, off:off + 1024], func=AF.Identity,
                        bias=consts[:, 2:3], scale=256.0))(ps, off, T2),
                        waits=[(s_pe, mm2_all), (s_dve, s_dve.n),
                               (s_pool, s_pool.n)],
                        inc=s_act)
                    q3act = s_act.n
                    E("vector", (lambda IT, T2: lambda e: e.tensor_copy(
                        out=IT, in_=T2))(IT, T2),
                        waits=[(s_act, s_act.n)], inc=s_dve)
                    E("vector", (lambda IT, T0: lambda e: e.tensor_copy(
                        out=T0, in_=IT))(IT, T0), inc=s_dve)
                    E("vector", (lambda T2, T0, T1: lambda e: e.tensor_sub(
                        out=T1, in0=T2, in1=T0))(T2, T0, T1), inc=s_dve)
                    for j, thr in enumerate((-1.0, 0.0, 1.0)):
                        E("vector", (lambda T1, u, thr=thr: lambda e:
                           e.tensor_scalar(out=u, in0=T1, scalar1=thr,
                                           scalar2=None, op0=ALU.is_ge))(
                               T1, (V0, V1, V2)[j]),
                           inc=s_dve)
                    E("vector", (lambda V0, V1: lambda e: e.tensor_add(
                        out=V0, in0=V0, in1=V1))(V0, V1), inc=s_dve)
                    E("vector", (lambda V0, V2: lambda e: e.tensor_add(
                        out=V0, in0=V0, in1=V2))(V0, V2), inc=s_dve)
                    E("vector", (lambda T0, V0, T1: lambda e:
                       e.scalar_tensor_tensor(out=T1, in0=T0, scalar=-2.0,
                                              in1=V0, op0=ALU.add,
                                              op1=ALU.add))(T0, V0, T1),
                       inc=s_dve)
                    E("vector", (lambda T1, par, tt: lambda e:
                       e.scalar_tensor_tensor(
                           out=outst[:, tt % 2, :], in0=T1,
                           scalar=1.0 / 256.0, in1=xst[:, par, :],
                           op0=ALU.mult, op1=ALU.add))(T1, par, tt),
                       waits=[(s_x, s_x.n),
                              (s_out, out_cnt.get(tt - 2, 0))],
                       inc=s_dve)
                    E("gpsimd", (lambda tt: lambda e: e.dma_start(
                        out=out_d[tt * 128:(tt + 1) * 128, :],
                        in_=outst[:, tt % 2, :]))(tt),
                        waits=[(s_dve, s_dve.n)], inc=s_out, inck=16)
                    out_cnt[tt] = s_out.n
            if th == 0:
                E("sync", None, waits=[(s_tp, s_tp.n), (s_w, s_w.n),
                                       (s_x, s_x.n), (s_out, s_out.n)])
                plan.barrier(s_bar)

        E("sync", None, waits=[(s_out, s_out.n)])

        blk = ctx.enter_context(nc.Block())

        @blk.sync
        def _(eng):
            plan.run("sync", eng)

        @blk.tensor
        def _(eng):
            plan.run("tensor", eng)

        @blk.vector
        def _(eng):
            plan.run("vector", eng)

        @blk.scalar
        def _(eng):
            plan.run("scalar", eng)

        @blk.gpsimd
        def _(eng):
            plan.run("gpsimd", eng)

        est.close()
        xnst.close()
        ab2.close()
    return nc, sorted(taps.keys())


def kernel(L0_attn_x, L0_attn_Wq, L0_attn_Wk, L0_attn_Wv, L0_attn_Wo,
           L0_attn_bo, L0_ffn_w1, L0_ffn_b1, L0_ffn_w2, L0_ffn_b2):
    x = np.ascontiguousarray(np.asarray(L0_attn_x, dtype=np.float32))
    wts = {k: np.ascontiguousarray(np.asarray(v, dtype=np.float32))
           for k, v in (("wq", L0_attn_Wq), ("wk", L0_attn_Wk),
                        ("wv", L0_attn_Wv), ("wo", L0_attn_Wo),
                        ("bo", L0_attn_bo), ("w1", L0_ffn_w1),
                        ("b1", L0_ffn_b1), ("w2", L0_ffn_w2),
                        ("b2", L0_ffn_b2))}
    nc, tapnames = build_program()
    in_maps = []
    for c in range(8):
        b, qh = c // 2, c % 2
        xq = np.ascontiguousarray(x[b, qh * NQ:(qh + 1) * NQ, :])
        xr = np.ascontiguousarray(x[b, (1 - qh) * NQ:(2 - qh) * NQ, :])
        in_maps.append(dict(wts, xq=xq, xr=xr))
    res = run_bass_kernel_spmd(nc, in_maps, core_ids=list(range(8)))
    out = np.empty((4, 4096, 1024), np.float32)
    for c in range(8):
        out[c // 2, (c % 2) * NQ:(c % 2 + 1) * NQ, :] = res.results[c]["out"]
    if tapnames:
        kernel.taps = [{t: np.asarray(r["tap_" + t]) for t in tapnames}
                       for r in res.results]
    return out


# version 06
# speedup vs baseline: 1.7920x; 1.0011x over previous
import numpy as np
import concourse.bass as bass
import concourse.mybir as mybir
from concourse.bass_utils import run_bass_kernel_spmd
from contextlib import ExitStack

F32 = mybir.dt.float32
BF16 = mybir.dt.bfloat16
I32 = mybir.dt.int32
AF = mybir.ActivationFunctionType
ALU = mybir.AluOpType
AXX = mybir.AxisListType.X

NTILE = 32
NQT = 16
NQ = 2048
NK = 4096
DM = 1024
FF = 4096
NPAIR = 8
NFC = 32
EPS = 1e-5

DEBUG_TAPS = []

_REG = {}


def _ensure_custom_ops():
    if _REG:
        return
    import concourse.dve_ops as dve_ops
    from concourse.dve_op import DveOp
    from concourse.dve_spec import Spec, Src0, Src1, Zero, One, C2, lower
    from concourse.dve_table_gen import dve_ver_for

    have = {o.name: o for o in dve_ops.OPS}

    def _f3_ref(in0, in1, s0, s1, imm2):
        d = in0 - in1
        dt_ = in0.dtype
        ge = lambda t: (d >= t).astype(dt_)
        return ((in1 + dt_.type(imm2)) + ge(-1.0)) + (ge(0.0) + ge(1.0))

    D = Src0 - Src1
    body3 = ((Src1 + C2) + (D >= (Zero - One))) + ((D >= Zero) + (D >= One))

    ver = dve_ver_for("TRN2")
    for name, body, ref in (("MK_FLOOR3", body3, _f3_ref),):
        if name in have:
            _REG[name] = have[name]
            continue
        spec = Spec(body=body)
        sha = lower(spec, ver=ver, perf_en=False, name=name).sha(ver)
        op = DveOp(name, spec, subdim=False, uops_sha={ver: sha}, reference=ref)
        dve_ops.OPS.append(op)
        dve_ops.CUSTOM_DVE_SPECS[name] = spec
        dve_ops.CUSTOM_DVE_REFS[name] = ref
        dve_ops._SUB_OPCODE_FOR_NAME[name] = (
            dve_ops._CUSTOM_DVE_ROW_BASE + len(dve_ops.OPS) - 1)
        _REG[name] = op


class Sem:
    def __init__(self, nc, ctx, name):
        self.h = ctx.enter_context(nc.semaphore(name))
        self.n = 0
        self.name = name


class Plan:
    ENGS = ("sync", "tensor", "vector", "scalar", "gpsimd")

    def __init__(self):
        self.q = {e: [] for e in self.ENGS}
        self.lastw = {e: {} for e in self.ENGS}

    def emit(self, eng, fn, waits=(), inc=None, inck=1):
        lw = self.lastw[eng]
        real = []
        for s, v in waits:
            if v > 0 and lw.get(s.name, -1) < v:
                real.append((s, v))
                lw[s.name] = v
        if inc is not None:
            inc.n += inck
        self.q[eng].append((real, fn, inc, inck))

    def barrier(self, s):
        target = s.n + 5
        for e in self.ENGS:
            self.q[e].append(([], lambda eng: eng.nop(), s, 1))
        s.n = target
        for e in self.ENGS:
            self.q[e].append(([(s, target)], None, None, 0))
            self.lastw[e][s.name] = target

    def run(self, eng_name, eng):
        for waits, fn, inc, k in self.q[eng_name]:
            for s, v in waits:
                eng.wait_ge(s.h, v)
            if fn is None:
                continue
            ins = fn(eng)
            if inc is not None and ins is not None:
                ins.then_inc(inc.h, k)


def build_program():
    _ensure_custom_ops()
    F3 = _REG["MK_FLOOR3"]
    nc = bass.Bass("TRN2", target_bir_lowering=False, debug=False)

    xq_d = nc.dram_tensor("xq", [NQ, DM], F32, kind="ExternalInput").ap()
    xr_d = nc.dram_tensor("xr", [NQ, DM], F32, kind="ExternalInput").ap()
    wq_d = nc.dram_tensor("wq", [DM, DM], F32, kind="ExternalInput").ap()
    wk_d = nc.dram_tensor("wk", [DM, DM], F32, kind="ExternalInput").ap()
    wv_d = nc.dram_tensor("wv", [DM, DM], F32, kind="ExternalInput").ap()
    wo_d = nc.dram_tensor("wo", [DM, DM], F32, kind="ExternalInput").ap()
    bo_d = nc.dram_tensor("bo", [DM], F32, kind="ExternalInput").ap()
    w1_d = nc.dram_tensor("w1", [DM, FF], F32, kind="ExternalInput").ap()
    b1_d = nc.dram_tensor("b1", [FF], F32, kind="ExternalInput").ap()
    w2_d = nc.dram_tensor("w2", [FF, DM], F32, kind="ExternalInput").ap()
    b2_d = nc.dram_tensor("b2", [DM], F32, kind="ExternalInput").ap()
    out_d = nc.dram_tensor("out", [NQ, DM], F32, kind="ExternalOutput").ap()
    resid_d = nc.dram_tensor("residsc", [NQ, DM], F32, kind="Internal").ap()
    rec_d = nc.dram_tensor("recsc", [2, NQ], F32, kind="Internal").ap()

    taps = {}

    def mktap(name, shape, dt):
        if name in DEBUG_TAPS:
            taps[name] = nc.dram_tensor("tap_" + name, shape, dt,
                                        kind="ExternalOutput").ap()

    mktap("xn", [128, 8, 256], BF16)
    mktap("qT", [128, 256], BF16)
    mktap("kT", [128, 256], BF16)
    mktap("vA", [128, 132], BF16)
    mktap("pt", [128, NQ], BF16)
    mktap("avT", [128, 8, 256], BF16)
    mktap("r1", [128, DM], F32)
    mktap("x2T", [128, 8, 256], BF16)
    mktap("hq", [128, 1024], BF16)
    mktap("zf", [128, 1024], F32)
    mktap("resfull", [NQ, DM], F32)
    mktap("x2full", [128, 8, NQ], BF16)
    mktap("hqfull", [128, NFC, 1024], BF16)
    mktap("tn", [128, 1024], F32)
    mktap("tk4", [128, 1024], F32)
    mktap("tr", [128, 1024], F32)
    mktap("ty", [128, 1024], F32)

    with ExitStack() as ctx:
        S = lambda n: Sem(nc, ctx, n)
        s_x2 = [S("s_x0"), S("s_x1")]
        s_w2 = [S("s_w0"), S("s_w1")]
        s_wb2 = [S("s_wb0"), S("s_wb1")]
        s_wc = S("s_wc")
        s_wq3 = [S("s_wq0"), S("s_wq1"), S("s_wq2")]
        s_ta2 = [S("s_ta0"), S("s_ta1")]
        s_vk2 = [S("s_vk0"), S("s_vk1")]
        s_rec = S("s_rec")
        s_res2 = [S("s_res0"), S("s_res1")]
        s_out2 = [S("s_out0"), S("s_out1")]
        s_pe = S("s_pe")
        s_dve = S("s_dve")
        s_act = S("s_act")
        s_pool = S("s_pool")
        s_exp = S("s_exp")
        s_bar = S("s_bar")

        sb = lambda n, shp, dt=F32: ctx.enter_context(nc.sbuf_tensor(n, shp, dt))
        consts = sb("consts", [128, 8])
        onesbf = sb("onesbf", [1, 128], BF16)
        xst = sb("xst", [128, 2, DM])
        stats = sb("stats", [128, 40])
        mv = sb("mv", [128, 4])
        outst = sb("outst", [128, 2, DM])
        b1t = sb("b1t", [128, 2])
        b1s = sb("b1s", [128, 2])
        ps_a = ctx.enter_context(nc.psum_tensor("ps_a", [128, 2048], F32))

        plan = Plan()
        E = plan.emit

        def drain_waits():
            return [(s_ta2[0], s_ta2[0].n), (s_ta2[1], s_ta2[1].n),
                    (s_vk2[0], s_vk2[0].n), (s_vk2[1], s_vk2[1].n),
                    (s_w2[0], s_w2[0].n), (s_w2[1], s_w2[1].n),
                    (s_x2[0], s_x2[0].n), (s_x2[1], s_x2[1].n),
                    (s_wb2[0], s_wb2[0].n), (s_wb2[1], s_wb2[1].n),
                    (s_wc, s_wc.n), (s_rec, s_rec.n),
                    (s_res2[0], s_res2[0].n), (s_res2[1], s_res2[1].n),
                    (s_out2[0], s_out2[0].n), (s_out2[1], s_out2[1].n),
                    (s_wq3[0], s_wq3[0].n),
                    (s_wq3[1], s_wq3[1].n), (s_wq3[2], s_wq3[2].n)]

        def _init(gp):
            gp.memset(consts[:, 0:1], EPS)
            gp.memset(consts[:, 1:2], -1.0)
            gp.memset(consts[:, 2:3], 0.5)
            gp.memset(consts[:, 3:4], 0.25)
            return gp.memset(onesbf[:, :], 1.0)

        E("gpsimd", _init, inc=s_pool)
        init_done = s_pool.n

        def plan_ln(xv, pre_waits, sq_ps):
            E("vector", (lambda xv: lambda e: e.tensor_reduce(
                out=mv[:, 0:1], in_=xv, axis=AXX, op=ALU.add))(xv),
                waits=list(pre_waits), inc=s_dve)
            E("scalar", (lambda xv, sq_ps: lambda e: e.activation(
                out=sq_ps, in_=xv, func=AF.Square, bias=0.0, scale=1.0))(
                    xv, sq_ps),
                waits=[(s_dve, s_dve.n)], inc=s_act)
            E("vector", (lambda sq_ps: lambda e: e.tensor_reduce(
                out=mv[:, 1:2], in_=sq_ps, axis=AXX, op=ALU.add))(sq_ps),
                waits=[(s_act, s_act.n)], inc=s_dve)
            E("vector", lambda e: e.tensor_scalar(
                out=mv[:, 0:1], in0=mv[:, 0:1], scalar1=1.0 / 1024.0,
                scalar2=None, op0=ALU.mult),
                waits=[(s_dve, s_dve.n)], inc=s_dve)
            E("scalar", lambda e: e.activation(
                out=mv[:, 2:3], in_=mv[:, 0:1], func=AF.Square,
                bias=0.0, scale=1.0),
                waits=[(s_dve, s_dve.n)], inc=s_act)
            E("vector", lambda e: e.scalar_tensor_tensor(
                out=mv[:, 1:2], in0=mv[:, 1:2], scalar=1.0 / 1024.0,
                in1=mv[:, 2:3], op0=ALU.mult, op1=ALU.subtract),
                waits=[(s_act, s_act.n)], inc=s_dve)
            E("scalar", lambda e: e.activation(
                out=mv[:, 2:3], in_=mv[:, 1:2], func=AF.Ln,
                bias=consts[:, 0:1], scale=1.0),
                waits=[(s_dve, s_dve.n)], inc=s_act)
            E("scalar", lambda e: e.activation(
                out=mv[:, 3:4], in_=mv[:, 2:3], func=AF.Exp,
                bias=0.0, scale=-0.5),
                waits=[(s_act, s_act.n)], inc=s_act)
            return s_act.n


        ab2 = ExitStack()
        sb2 = lambda n, shp, dt=F32: ab2.enter_context(nc.sbuf_tensor(n, shp, dt))
        avT = sb2("avT", [128, 8, NQ], BF16)
        ab1 = ExitStack()
        sb1 = lambda n, shp, dt=F32: ab1.enter_context(nc.sbuf_tensor(n, shp, dt))
        xnT = sb1("xnT", [128, 8, NK], BF16)
        xntok = sb1("xntok", [128, 2, DM], BF16)
        wstg = sb1("wstg", [128, 3, DM])
        wqbf = sb1("wqbf", [128, 8, 128], BF16)
        wkbf = sb1("wkbf", [128, 8, 128], BF16)
        wvbf = sb1("wvbf", [128, 8, 128], BF16)
        qT = sb1("qT", [128, 2, NQ], BF16)
        kT = sb1("kT", [128, 2, NK], BF16)
        vstg = sb1("vstg", [128, 2, 512], BF16)
        vtok = sb1("vtok", [128, 2, 128], BF16)
        vAug = sb1("vAug", [128, 2, 32, 132], BF16)
        ptile = xntok
        avps = xst[0:65, 0, :]
        recb = xst[0:64, 1, :]

        recsb = outst[0:1, 0, :]
        ps_b_st = ExitStack()
        ps_b = ps_b_st.enter_context(nc.psum_tensor("ps_b", [128, 2048], F32))

        def _vaug_ones(gp):
            gp.memset(vAug[:, :, :, 64:65], 1.0)
            return gp.memset(vAug[:, :, :, 130:131], 1.0)

        E("gpsimd", _vaug_ones, inc=s_pool)
        vaug_ones_cnt = s_pool.n

        dve_norm = {}
        tp_tile = {}
        act_rstd = {}
        for tt in range(NTILE):
            src = xq_d if tt < NQT else xr_d
            row = (tt % NQT) * 128
            par = tt % 2
            E("sync", (lambda par, s0: lambda e: e.dma_start(
                out=xst[:, par, :], in_=s0))(par, src[row:row + 128, :]),
                waits=[(s_dve, dve_norm.get(tt - 2, 0)),
                       (s_pool, init_done)], inc=s_x2[par], inck=16)
            xv = xst[:, par, :]
            act_rstd[tt] = plan_ln(xv, [(s_x2[par], s_x2[par].n),
                                        (s_act, act_rstd.get(tt - 1, 0))],
                                   ps_a[:, 0:1024])
            E("vector", (lambda xv, par: lambda e: e.tensor_scalar(
                out=xntok[:, par, :], in0=xv, scalar1=mv[:, 0:1],
                scalar2=mv[:, 3:4], op0=ALU.subtract, op1=ALU.mult))(xv, par),
                waits=[(s_act, s_act.n),
                       (s_ta2[par], tp_tile.get(tt - 2, 0))],
                inc=s_dve)
            dve_norm[tt] = s_dve.n
            for a in range(8):
                E("sync", (lambda a, par, tt: lambda e: e.dma_start(
                    out=xnT[:, a, tt * 128:(tt + 1) * 128],
                    in_=xntok[:, par, a * 128:(a + 1) * 128], transpose=True))(
                        a, par, tt),
                    waits=[(s_dve, s_dve.n)], inc=s_ta2[par], inck=16)
            tp_tile[tt] = s_ta2[par].n
        tpA_w = [(s_ta2[0], s_ta2[0].n), (s_ta2[1], s_ta2[1].n)]

        if "xn" in taps:
            E("gpsimd", lambda e: e.dma_start(out=taps["xn"],
                                              in_=xnT[:, :, 0:256]),
              waits=list(tpA_w), inc=s_out2[0], inck=16)

        wbufs = (wqbf, wkbf, wvbf)
        wsrc = (wq_d, wk_d, wv_d)
        cast_cnt = {}
        wdma_cnt = {}
        evac_dve = {}
        vaug_cp_g = {}
        proj_pe_done = {}
        qkv_done = {}
        GG = [0]

        def plan_wdma(q):
            for wi in range(3):
                sidx = 3 * q + wi
                spar = sidx % 3
                src = wsrc[wi][:, q * 128:(q + 1) * 128].rearrange(
                    "(a q2) m -> q2 a m", q2=128)
                E("sync", (lambda spar, src: lambda e: e.dma_start(
                    out=wstg[:, spar, :].rearrange("q2 (a m) -> q2 a m", a=8),
                    in_=src))(spar, src),
                    waits=[(s_pool, cast_cnt.get(sidx - 3, 0))],
                    inc=s_wq3[wi], inck=16)
                wdma_cnt[sidx] = s_wq3[wi].n

        def build_pair_jobs(q):
            par = q % 2
            jobs = []

            def wcast_job(wi):
                def j():
                    sidx = 3 * q + wi
                    spar = sidx % 3
                    E("vector", (lambda spar, wi: lambda e: e.tensor_copy(
                        out=wbufs[wi][:, :, :],
                        in_=wstg[:, spar, :].rearrange(
                            "q2 (a m) -> q2 a m", a=8)))(spar, wi),
                        waits=[(s_wq3[wi], wdma_cnt[sidx]),
                               (s_pe, proj_pe_done.get(q - 1, 0))],
                        inc=s_pool)
                    cast_cnt[sidx] = s_pool.n
                return j

            def wdma_job(wi):
                def j():
                    if q + 1 >= NPAIR:
                        return
                    sidx = 3 * (q + 1) + wi
                    spar = sidx % 3
                    src = wsrc[wi][:, (q + 1) * 128:(q + 2) * 128].rearrange(
                        "(a q2) m -> q2 a m", q2=128)
                    E("sync", (lambda spar, src: lambda e: e.dma_start(
                        out=wstg[:, spar, :].rearrange(
                            "q2 (a m) -> q2 a m", a=8),
                        in_=src))(spar, src),
                        waits=[(s_pool, cast_cnt.get(sidx - 3, 0))],
                        inc=s_wq3[wi], inck=16)
                    wdma_cnt[sidx] = s_wq3[wi].n
                return j

            def qk_job(wi, wbf, dst, c):
                def j():
                    g = GG[0]
                    GG[0] += 1
                    buf = g % 2

                    def mmfn(e):
                        ins = None
                        for a in range(8):
                            ins = e.matmul(
                                ps_b[:, 1024 + buf * 512:1536 + buf * 512],
                                wbf[:, a, :],
                                xnT[:, a, c * 512:(c + 1) * 512],
                                start=(a == 0), stop=(a == 7))
                        return ins

                    E("tensor", mmfn,
                      waits=[(s_pool, cast_cnt[3 * q + wi]),
                             (s_dve, evac_dve.get(g - 2, 0))] + tpA_w,
                      inc=s_pe)
                    E("vector", (lambda buf, c: lambda e: e.tensor_copy(
                        out=dst[:, par, c * 512:(c + 1) * 512],
                        in_=ps_b[:, 1024 + buf * 512:1536 + buf * 512]))(
                            buf, c),
                        waits=[(s_pe, s_pe.n)], inc=s_dve)
                    evac_dve[g] = s_dve.n
                return j

            def v_job(c):
                def j():
                    g = GG[0]
                    GG[0] += 1
                    buf = g % 2

                    def mmfn(e):
                        ins = None
                        for a in range(8):
                            ins = e.matmul(
                                ps_b[:, 1024 + buf * 512:1536 + buf * 512],
                                wvbf[:, a, :],
                                xnT[:, a, c * 512:(c + 1) * 512],
                                start=(a == 0), stop=(a == 7))
                        return ins

                    E("tensor", mmfn,
                      waits=[(s_pool, cast_cnt[3 * q + 2]),
                             (s_dve, evac_dve.get(g - 2, 0))] + tpA_w,
                      inc=s_pe)
                    E("vector", (lambda buf, c: lambda e: e.tensor_copy(
                        out=vstg[:, c % 2, :],
                        in_=ps_b[:, 1024 + buf * 512:1536 + buf * 512]))(
                            buf, c),
                        waits=[(s_pe, s_pe.n)], inc=s_dve)
                    evac_dve[g] = s_dve.n
                    for jj in range(4):
                        kt = 4 * c + jj
                        ktg = 32 * q + kt
                        E("sync", (lambda c, jj, kt: lambda e: e.dma_start(
                            out=vtok[:, kt % 2, :],
                            in_=vstg[:, c % 2, jj * 128:(jj + 1) * 128],
                            transpose=True))(c, jj, kt),
                            waits=[(s_dve, max(evac_dve[g],
                                               vaug_cp_g.get(ktg - 2, 0))),
                                   (s_pool, vaug_ones_cnt)],
                            inc=s_vk2[ktg % 2], inck=16)
                        E("vector", (lambda kt: lambda e: e.tensor_copy(
                            out=vAug[:, par, kt, :].rearrange(
                                "p (g2 c2) -> p g2 c2", c2=66)[:, :, 0:64],
                            in_=vtok[:, kt % 2, :].rearrange(
                                "p (g2 c2) -> p g2 c2", c2=64)))(kt),
                            waits=[(s_vk2[ktg % 2], s_vk2[ktg % 2].n)],
                            inc=s_dve)
                        vaug_cp_g[ktg] = s_dve.n
                return j

            def record_job():
                def j():
                    proj_pe_done[q] = s_pe.n
                    qkv_done[q] = s_dve.n
                return j

            jobs.append(wcast_job(0))
            for c in range(4):
                jobs.append(qk_job(0, wqbf, qT, c))
            jobs.append(wdma_job(0))
            jobs.append(wcast_job(1))
            for c in range(8):
                jobs.append(qk_job(1, wkbf, kT, c))
            jobs.append(wdma_job(1))
            jobs.append(wcast_job(2))
            for c in range(8):
                jobs.append(v_job(c))
            jobs.append(wdma_job(2))
            jobs.append(record_job())
            return jobs

        plan_wdma(0)
        for jb in build_pair_jobs(0):
            jb()

        exp_par = {}
        exp_done = {}
        psb_free_dve = 0
        recb_reader = 0

        for p in range(NPAIR):
            pp = p % 2
            jobs = build_pair_jobs(p + 1) if p + 1 < NPAIR else []
            ji = [0]

            def inject(jobs=jobs, ji=ji):
                if ji[0] < len(jobs):
                    jobs[ji[0]]()
                    ji[0] += 1

            if p == 0 and "qT" in taps:
                E("gpsimd", lambda e: e.dma_start(out=taps["qT"],
                                                  in_=qT[:, 0, 0:256]),
                  waits=[(s_dve, qkv_done[0])], inc=s_out2[0], inck=16)
            if p == 0 and "kT" in taps:
                E("gpsimd", lambda e: e.dma_start(out=taps["kT"],
                                                  in_=kT[:, 0, 0:256]),
                  waits=[(s_dve, qkv_done[0])], inc=s_out2[0], inck=16)
            if p == 0 and "vA" in taps:
                E("gpsimd", lambda e: e.dma_start(
                    out=taps["vA"], in_=vAug[:, 0, 0, :]),
                    waits=[(s_dve, qkv_done[0])], inc=s_out2[0], inck=16)

            for h in range(2):
                hb = 64 * h
                base = 0 if h == 0 else 66
                for qh in range(2):
                    qb = qh * 1024

                    def avfn(kt, kpar, base=base, pp=pp):
                        def f(e):
                            ins = None
                            for qc in range(2):
                                ins = e.matmul(
                                    ps_b[0:65, qc * 512:(qc + 1) * 512],
                                    vAug[:, pp, kt, base:base + 65],
                                    ptile[:, kpar, qc * 512:(qc + 1) * 512],
                                    start=(kt == 0), stop=(kt == 31))
                            return ins
                        return f

                    for kt in range(32):
                        kpar = kt % 2

                        def scfn(hb=hb, kt=kt, qb=qb, pp=pp, kpar=kpar):
                            def f(e):
                                ins = None
                                for qc in range(2):
                                    ins = e.matmul(
                                        ps_a[:, kpar * 1024 + qc * 512:
                                             kpar * 1024 + qc * 512 + 512],
                                        kT[hb:hb + 64, pp,
                                           kt * 128:(kt + 1) * 128],
                                        qT[hb:hb + 64, pp,
                                           qb + qc * 512:qb + qc * 512 + 512],
                                        start=True, stop=True)
                                return ins
                            return f

                        E("tensor", scfn(),
                          waits=[(s_exp, exp_par.get(kpar, 0)),
                                 (s_dve, qkv_done[p])],
                          inc=s_pe)
                        E("scalar", (lambda kpar: lambda e: e.activation(
                            out=ptile[:, kpar, :],
                            in_=ps_a[:, kpar * 1024:kpar * 1024 + 1024],
                            func=AF.Exp, bias=consts[:, 1:2],
                            scale=1.0 / 32.0))(kpar),
                            waits=[(s_pe, s_pe.n)], inc=s_exp)
                        exp_par[kpar] = s_exp.n
                        exp_done[kt] = s_exp.n
                        if kt > 0:
                            avw = [(s_exp, exp_done[kt - 1])]
                            if kt - 1 == 0:
                                avw.append((s_dve, psb_free_dve))
                            E("tensor", avfn(kt - 1, (kt - 1) % 2),
                              waits=avw, inc=s_pe)
                        if kt % 4 == 1:
                            inject()
                    E("tensor", avfn(31, 1),
                      waits=[(s_exp, exp_done[31])], inc=s_pe)
                    av_done = s_pe.n
                    E("vector", lambda e: e.tensor_copy(
                        out=avps[:, :], in_=ps_b[0:65, 0:1024]),
                        waits=[(s_pe, av_done)], inc=s_dve)
                    psb_free_dve = s_dve.n
                    E("vector", lambda e: e.reciprocal(
                        out=recsb, in_=avps[64:65, :]),
                        waits=[(s_dve, s_dve.n)], inc=s_dve)
                    E("gpsimd", (lambda h, qb: lambda e: e.dma_start(
                        out=rec_d[h, None, qb:qb + 1024], in_=recsb))(h, qb),
                        waits=[(s_dve, s_dve.n), (s_rec, s_rec.n)],
                        inc=s_rec, inck=16)
                    E("gpsimd", (lambda h, qb: lambda e: e.dma_start(
                        out=recb[:, :],
                        in_=rec_d[h, None, qb:qb + 1024].broadcast_to(
                            [64, 1024])))(h, qb),
                        waits=[(s_rec, s_rec.n), (s_dve, recb_reader)],
                        inc=s_rec, inck=16)
                    E("vector", (lambda p, hb, qb: lambda e: e.tensor_mul(
                        out=avT[hb:hb + 64, p, qb:qb + 1024],
                        in0=avps[0:64, :], in1=recb[:, :]))(p, hb, qb),
                        waits=[(s_rec, s_rec.n)], inc=s_dve)
                    recb_reader = s_dve.n
            while ji[0] < len(jobs):
                jobs[ji[0]]()
                ji[0] += 1
        avT_done = s_dve.n
        if "avT" in taps:
            E("gpsimd", lambda e: e.dma_start(out=taps["avT"],
                                              in_=avT[:, :, 0:256]),
              waits=[(s_dve, avT_done)], inc=s_out2[0], inck=16)

        E("sync", None, waits=drain_waits())
        plan.barrier(s_bar)
        ab1.close()
        xnst = ExitStack()
        xn2T = xnst.enter_context(nc.sbuf_tensor("xn2T", [128, 8, NQ], BF16))

        cst_tmp = ExitStack()
        sbc = lambda n, shp, dt=F32: cst_tmp.enter_context(
            nc.sbuf_tensor(n, shp, dt))
        wostg = sbc("wostg", [128, 2, DM])
        wobf = sbc("wobf", [128, 8, DM], BF16)
        x2tok = sbc("x2tok", [128, 2, DM], BF16)
        r1t = sbc("r1t", [128, 2, DM])
        bo_bf = sbc("bo_bf", [1, DM], BF16)
        E("sync", lambda e: e.dma_start(out=r1t[0:1, 0, :], in_=bo_d[None, :]),
          inc=s_wc, inck=16)
        E("vector", lambda e: e.tensor_copy(out=bo_bf[:, :], in_=r1t[0:1, 0, :]),
          waits=[(s_wc, s_wc.n)], inc=s_pool)
        bocast = s_pool.n

        for a in range(8):
            par = a % 2
            E("sync", (lambda a, par: lambda e: e.dma_start(
                out=wostg[:, par, :],
                in_=wo_d[a * 128:(a + 1) * 128, :]))(a, par),
                waits=[(s_pool, s_pool.n if a >= 2 else 0)],
                inc=s_w2[par], inck=16)
            E("vector", (lambda a, par: lambda e: e.tensor_copy(
                out=wobf[:, a, :], in_=wostg[:, par, :]))(a, par),
                waits=[(s_w2[par], s_w2[par].n)], inc=s_pool)
        wo_cast = s_pool.n

        r1_cnt = {}
        res_cnt = {}
        x2tp = {}
        rstd2 = {}
        for tt in range(NQT):
            par = tt % 2
            E("sync", (lambda tt, par: lambda e: e.dma_start(
                out=xst[:, par, :],
                in_=xq_d[tt * 128:(tt + 1) * 128, :]))(tt, par),
                waits=[(s_dve, max(avT_done, r1_cnt.get(tt - 2, 0)))],
                inc=s_x2[par], inck=16)

            def wofn(tt=tt, par=par):
                def f(e):
                    ins = None
                    for nc2 in range(2):
                        sl = slice(par * 1024 + nc2 * 512,
                                   par * 1024 + nc2 * 512 + 512)
                        for a in range(8):
                            e.matmul(ps_a[:, sl],
                                     avT[:, a, tt * 128:(tt + 1) * 128],
                                     wobf[:, a, nc2 * 512:(nc2 + 1) * 512],
                                     start=(a == 0), stop=False)
                        ins = e.matmul(ps_a[:, sl], onesbf[:, 0:128],
                                       bo_bf[:, nc2 * 512:(nc2 + 1) * 512],
                                       start=False, stop=True)
                    return ins
                return f

            E("tensor", wofn(),
              waits=[(s_pool, max(wo_cast, bocast)),
                     (s_dve, r1_cnt.get(tt - 2, 0)), (s_exp, s_exp.n)],
              inc=s_pe)
            pe_wo = s_pe.n
            E("vector", (lambda par: lambda e: e.tensor_add(
                out=r1t[:, par, :],
                in0=ps_a[:, par * 1024:par * 1024 + 1024],
                in1=xst[:, par, :]))(par),
                waits=[(s_pe, pe_wo), (s_x2[par], s_x2[par].n),
                       (s_res2[par], res_cnt.get(tt - 2, 0)),
                       (s_ta2[par], x2tp.get(tt - 2, 0)),
                       (s_act, rstd2.get(tt - 2, 0)),
                       (s_pool, bocast)],
                inc=s_dve)
            r1_cnt[tt] = s_dve.n
            E("gpsimd", (lambda tt, par: lambda e: e.dma_start(
                out=resid_d[tt * 128:(tt + 1) * 128, :],
                in_=r1t[:, par, :]))(tt, par),
                waits=[(s_dve, s_dve.n)], inc=s_res2[par], inck=16)
            res_cnt[tt] = s_res2[par].n
            if tt == 0 and "r1" in taps:
                E("gpsimd", lambda e: e.dma_start(out=taps["r1"],
                                                  in_=r1t[:, 0, :]),
                  waits=[(s_dve, s_dve.n)], inc=s_out2[0], inck=16)
            rstd2[tt] = plan_ln(r1t[:, par, :],
                                 [(s_dve, s_dve.n),
                                  (s_act, rstd2.get(tt - 1, 0))],
                                 ps_b[:, 0:1024])
            E("vector", (lambda par: lambda e: e.tensor_scalar(
                out=x2tok[:, par, :], in0=r1t[:, par, :],
                scalar1=mv[:, 0:1], scalar2=mv[:, 3:4],
                op0=ALU.subtract, op1=ALU.mult))(par),
                waits=[(s_act, s_act.n),
                       (s_ta2[par], x2tp.get(tt - 2, 0))],
                inc=s_dve)
            for a in range(8):
                E("sync", (lambda a, par, tt: lambda e: e.dma_start(
                    out=xn2T[:, a, tt * 128:(tt + 1) * 128],
                    in_=x2tok[:, par, a * 128:(a + 1) * 128],
                    transpose=True))(a, par, tt),
                    waits=[(s_dve, s_dve.n)], inc=s_ta2[par], inck=16)
            x2tp[tt] = s_ta2[par].n
        xn2_w = [(s_ta2[0], s_ta2[0].n), (s_ta2[1], s_ta2[1].n)]
        res_w = [(s_res2[0], s_res2[0].n), (s_res2[1], s_res2[1].n)]
        if "resfull" in taps:
            E("gpsimd", lambda e: e.dma_start(out=taps["resfull"], in_=resid_d),
              waits=list(res_w), inc=s_out2[0], inck=16)
        if "x2full" in taps:
            E("gpsimd", lambda e: e.dma_start(out=taps["x2full"], in_=xn2T[:, :, :]),
              waits=list(xn2_w), inc=s_out2[0], inck=16)
        if "x2T" in taps:
            E("gpsimd", lambda e: e.dma_start(out=taps["x2T"],
                                              in_=xn2T[:, :, 0:256]),
              waits=list(xn2_w), inc=s_out2[0], inck=16)

        E("sync", None, waits=drain_waits())
        plan.barrier(s_bar)
        cst_tmp.close()
        ps_b_st.close()

        est = ExitStack()
        sbe = lambda n, shp, dt=F32: est.enter_context(
            nc.sbuf_tensor(n, shp, dt))
        hq = sbe("hq", [128, NFC, 1024], BF16)
        b1_guard = {0: 0, 1: 0}
        w1stg = sbe("w1stg", [128, 2, DM])
        stg_cast = {0: 0, 1: 0}
        w1bf = sbe("w1bf", [128, 2, 8, 128], BF16)
        w2stg = w1stg
        w2v = avT[:, 4:6, :].rearrange("p a m -> p (a m)")
        TP = sbe("tpool", [128, 2, 4, 1024])
        TBv = avT.rearrange("p a (b c) -> p (a b) c", b=2)
        TI = sbe("tint", [128, 2, 1024], I32)
        zfb = sbe("zfb", [128, 2, 1024])
        b2_bf = sbe("b2_bf", [1, DM], BF16)
        E("sync", lambda e: e.dma_start(out=xst[0:1, 0, :], in_=b2_d[None, :]),
          waits=list(res_w), inc=s_wc, inck=16)
        E("vector", lambda e: e.tensor_copy(out=b2_bf[:, :], in_=xst[0:1, 0, :]),
          waits=[(s_wc, s_wc.n)], inc=s_pool)
        b2cast = s_pool.n
        ps_c = est.enter_context(nc.psum_tensor("ps_c", [128, 2048], F32))

        for th in range(2):
            qoff = th * 1024
            w1cast = {}
            zf_act = {}
            ch_dve = {}
            ch_pool = {}
            mm1pe = {}
            b1s_dve = {}
            for f in range(NFC):
                par = f % 2
                src = w1_d[:, f * 128:(f + 1) * 128].rearrange(
                    "(a q) m -> q a m", q=128)
                E("sync", (lambda par, src: lambda e: e.dma_start(
                    out=w1stg[:, par, :].rearrange("q (a m) -> q a m", a=8),
                    in_=src))(par, src),
                    waits=[(s_pool, stg_cast[par])],
                    inc=s_w2[par], inck=16)
                E("sync", (lambda par, f: lambda e: e.dma_start(
                    out=b1t[:, par:par + 1],
                    in_=b1_d[f * 128:(f + 1) * 128].rearrange(
                        "(q o) -> q o", o=1)))(par, f),
                    waits=[(s_dve, b1_guard[par])],
                    inc=s_wb2[par], inck=16)
                E("vector", (lambda par: lambda e: e.tensor_copy(
                    out=w1bf[:, par, :, :],
                    in_=w1stg[:, par, :].rearrange("q (a m) -> q a m",
                                                   a=8)))(par),
                    waits=[(s_w2[par], s_w2[par].n),
                           (s_pe, mm1pe.get(f - 2, 0))], inc=s_pool)
                w1cast[f] = s_pool.n
                stg_cast[par] = s_pool.n
                E("vector", (lambda par: lambda e: e.tensor_scalar(
                    out=b1s[:, par:par + 1], in0=b1t[:, par:par + 1],
                    scalar1=256.0, scalar2=0.5, op0=ALU.mult,
                    op1=ALU.add))(par),
                    waits=[(s_wb2[par], s_wb2[par].n)], inc=s_dve)
                b1s_cnt = s_dve.n
                b1s_dve[f] = s_dve.n
                b1_guard[par] = s_dve.n

                def mm1fn(par=par, qoff=qoff):
                    def fx(e):
                        ins = None
                        for c in range(2):
                            sl = slice(par * 1024 + c * 512,
                                       par * 1024 + c * 512 + 512)
                            for a in range(8):
                                ins = e.matmul(
                                    ps_a[:, sl], w1bf[:, par, a, :],
                                    xn2T[:, a,
                                         qoff + c * 512:qoff + c * 512 + 512],
                                    start=(a == 0), stop=(a == 7))
                        return ins
                    return fx

                E("tensor", mm1fn(),
                  waits=[(s_pool, w1cast[f]),
                         (s_act, zf_act.get(f - 2, 0))] + xn2_w,
                  inc=s_pe)
                mm1pe[f] = s_pe.n

                T = lambda i: TP[:, par, i, :]
                IT = TI[:, par, :]
                zf = zfb[:, par, :]
                wch = [(s_pe, s_pe.n), (s_dve, b1s_cnt),
                       (s_dve, ch_dve.get(f - 2, 0)),
                       (s_pool, ch_pool.get(f - 2, 0))]
                E("scalar", (lambda zf, par: lambda e: e.activation(
                    out=zf, in_=ps_a[:, par * 1024:par * 1024 + 1024],
                    func=AF.Identity, bias=b1s[:, par:par + 1],
                    scale=256.0))(zf, par),
                    waits=wch, inc=s_act)
                zf_act[f] = s_act.n
                A, B, C, D = T(0), T(1), T(2), T(3)
                U = lambda i: TBv[:, par * 3 + i, :]
                U0, U1, U2 = U(0), U(1), U(2)
                GE = ALU.is_ge
                E("vector", (lambda IT, zf: lambda e: e.tensor_copy(
                    out=IT, in_=zf))(IT, zf),
                    waits=[(s_act, s_act.n)], inc=s_dve)
                E("vector", (lambda IT, A=A: lambda e: e.tensor_copy(
                    out=A, in_=IT))(IT), inc=s_dve)
                E("vector", (lambda zf, A=A, B=B: lambda e: e.tensor_sub(
                    out=B, in0=zf, in1=A))(zf), inc=s_dve)
                for i, thr in enumerate((-1.0, 0.0, 1.0)):
                    E("vector", (lambda B=B, u=U(i), thr=thr: lambda e:
                       e.tensor_scalar(out=u, in0=B, scalar1=thr,
                                       scalar2=None, op0=GE))(),
                       inc=s_dve)
                E("vector", (lambda U0=U0, U1=U1: lambda e: e.tensor_add(
                    out=U0, in0=U0, in1=U1))(), inc=s_dve)
                E("vector", (lambda U0=U0, U2=U2: lambda e: e.tensor_add(
                    out=U0, in0=U0, in1=U2))(), inc=s_dve)
                E("vector", (lambda A=A, U0=U0, C=C: lambda e:
                   e.scalar_tensor_tensor(out=C, in0=A, scalar=-2.0,
                                          in1=U0, op0=ALU.add,
                                          op1=ALU.add))(), inc=s_dve)
                if th == 0 and f == 0 and "tn" in taps:
                    E("gpsimd", (lambda C=C: lambda e: e.dma_start(
                        out=taps["tn"], in_=C))(),
                        waits=[(s_dve, s_dve.n)], inc=s_out2[0], inck=16)
                E("vector", (lambda zf, B=B: lambda e: e.tensor_scalar(
                    out=B, in0=zf, scalar1=1.0 / 64.0, scalar2=None,
                    op0=ALU.mult))(zf), inc=s_dve)
                E("vector", (lambda IT, B=B: lambda e: e.tensor_copy(
                    out=IT, in_=B))(IT), inc=s_dve)
                E("vector", (lambda IT, A=A: lambda e: e.tensor_copy(
                    out=A, in_=IT))(IT), inc=s_dve)
                E("vector", (lambda B=B, A=A, U0=U0: lambda e:
                   e.tensor_tensor(out=U0, in0=B, in1=A, op=GE))(),
                   inc=s_dve)
                E("vector", (lambda A=A, U0=U0, D=D: lambda e:
                   e.scalar_tensor_tensor(out=D, in0=A, scalar=-1.0,
                                          in1=U0, op0=ALU.add,
                                          op1=ALU.add))(), inc=s_dve)
                E("vector", (lambda D=D, C=C, B=B: lambda e:
                   e.scalar_tensor_tensor(out=B, in0=D, scalar=-64.0,
                                          in1=C, op0=ALU.mult,
                                          op1=ALU.add))(), inc=s_dve)
                dve_r = s_dve.n
                if th == 0 and f == 0 and "tr" in taps:
                    E("gpsimd", (lambda D=D: lambda e: e.dma_start(
                        out=taps["tk4"], in_=D))(),
                        waits=[(s_dve, s_dve.n)], inc=s_out2[0], inck=16)
                    E("gpsimd", (lambda B=B: lambda e: e.dma_start(
                        out=taps["tr"], in_=B))(),
                        inc=s_out2[0], inck=16)
                E("scalar", (lambda D=D, A=A: lambda e: e.activation(
                    out=A, in_=D, func=AF.Gelu, bias=0.0, scale=0.25))(),
                    waits=[(s_dve, dve_r)], inc=s_act)
                E("scalar", (lambda D=D, C=C: lambda e: e.activation(
                    out=C, in_=D, func=AF.Gelu, bias=consts[:, 3:4],
                    scale=0.25))(), inc=s_act)
                act_g = s_act.n
                E("vector", (lambda C=C, A=A: lambda e: e.tensor_sub(
                    out=C, in0=C, in1=A))(),
                    waits=[(s_act, act_g)], inc=s_dve)
                E("vector", (lambda C=C, B=B: lambda e:
                   e.scalar_tensor_tensor(out=C, in0=C, scalar=1.0 / 64.0,
                                          in1=B, op0=ALU.mult,
                                          op1=ALU.mult))(), inc=s_dve)
                E("vector", (lambda C=C, A=A: lambda e: e.tensor_add(
                    out=C, in0=C, in1=A))(), inc=s_dve)
                dve_y = s_dve.n
                if th == 0 and f == 0 and "ty" in taps:
                    E("gpsimd", (lambda C=C: lambda e: e.dma_start(
                        out=taps["ty"], in_=C))(),
                        waits=[(s_dve, s_dve.n)], inc=s_out2[0], inck=16)
                E("scalar", (lambda zf, C=C: lambda e: e.activation(
                    out=zf, in_=C, func=AF.Identity, bias=consts[:, 2:3],
                    scale=256.0))(zf),
                    waits=[(s_dve, dve_y)], inc=s_act)
                E("vector", (lambda IT, zf: lambda e: e.tensor_copy(
                    out=IT, in_=zf))(IT, zf),
                    waits=[(s_act, s_act.n)], inc=s_dve)
                ch_pool[f] = s_pool.n
                E("vector", (lambda IT, A=A: lambda e: e.tensor_copy(
                    out=A, in_=IT))(IT), inc=s_dve)
                E("vector", (lambda zf, A=A, B=B: lambda e: e.tensor_sub(
                    out=B, in0=zf, in1=A))(zf), inc=s_dve)
                E("vector", (lambda B=B, U0=U0: lambda e: e.tensor_scalar(
                    out=U0, in0=B, scalar1=-2.0, scalar2=None,
                    op0=GE))(), inc=s_dve)
                E("vector", (lambda B=B, U1=U1: lambda e: e.tensor_scalar(
                    out=U1, in0=B, scalar1=-1.0, scalar2=None,
                    op0=GE))(), inc=s_dve)
                E("vector", (lambda U0=U0, U1=U1: lambda e: e.tensor_add(
                    out=U0, in0=U0, in1=U1))(), inc=s_dve)
                E("vector", (lambda B=B, U1=U1: lambda e: e.tensor_scalar(
                    out=U1, in0=B, scalar1=0.0, scalar2=None,
                    op0=GE))(), inc=s_dve)
                E("vector", (lambda B=B, U2=U2: lambda e: e.tensor_scalar(
                    out=U2, in0=B, scalar1=1.0, scalar2=None,
                    op0=GE))(), inc=s_dve)
                E("vector", (lambda U1=U1, U2=U2: lambda e: e.tensor_add(
                    out=U1, in0=U1, in1=U2))(), inc=s_dve)
                E("vector", (lambda U0=U0, U1=U1: lambda e: e.tensor_add(
                    out=U0, in0=U0, in1=U1))(), inc=s_dve)
                E("vector", (lambda B=B, U1=U1: lambda e: e.tensor_scalar(
                    out=U1, in0=B, scalar1=2.0, scalar2=None,
                    op0=GE))(), inc=s_dve)
                E("vector", (lambda U0=U0, U1=U1: lambda e: e.tensor_add(
                    out=U0, in0=U0, in1=U1))(), inc=s_dve)
                E("vector", (lambda A=A, U0=U0, f=f: lambda e:
                   e.scalar_tensor_tensor(out=hq[:, f, :], in0=A,
                                          scalar=-3.0, in1=U0,
                                          op0=ALU.add, op1=ALU.add))(),
                   inc=s_dve)
                ch_dve[f] = s_dve.n
                if th == 0 and f == 0 and "zf" in taps:
                    E("gpsimd", (lambda zf: lambda e: e.dma_start(
                        out=taps["zf"], in_=zf))(zf),
                        waits=[(s_act, zf_act[0])], inc=s_out2[0], inck=16)
                if th == 0 and f == 0 and "hq" in taps:
                    E("gpsimd", lambda e: e.dma_start(out=taps["hq"],
                                                      in_=hq[:, 0, :]),
                      waits=[(s_dve, s_dve.n)], inc=s_out2[0], inck=16)
            chains_done = s_dve.n
            if th == 0 and "hqfull" in taps:
                E("gpsimd", lambda e: e.dma_start(out=taps["hqfull"],
                                                  in_=hq[:, :, :]),
                  waits=[(s_dve, chains_done)], inc=s_out2[0], inck=16)

            w2cast = {}
            mm2pe = {}
            q3act = 0
            out_cnt = {}
            for sp in range(2):
                tiles = [th * 8 + sp * 4 + i for i in range(4)]
                for f in range(NFC):
                    par = f % 2
                    E("sync", (lambda par, f: lambda e: e.dma_start(
                        out=w2stg[:, par, :],
                        in_=w2_d[f * 128:(f + 1) * 128, :]))(par, f),
                        waits=[(s_pool, stg_cast[par])],
                        inc=s_w2[par], inck=16)
                    E("vector", (lambda par, f: lambda e: e.tensor_scalar(
                        out=w2v[:, (f % 3) * 1024:(f % 3) * 1024 + 1024],
                        in0=w2stg[:, par, :],
                        scalar1=1.0 / 256.0, scalar2=None,
                        op0=ALU.mult))(par, f),
                        waits=[(s_w2[par], s_w2[par].n),
                               (s_pe, mm2pe.get((sp, f - 3), 0))],
                        inc=s_pool)
                    w2cast[(sp, f)] = s_pool.n
                    stg_cast[par] = s_pool.n

                    def mm2fn(f=f, tiles=tiles, th=th):
                        def fx(e):
                            ins = None
                            for i, tt in enumerate(tiles):
                                ps = ps_a if i < 2 else ps_c
                                off = (i % 2) * 1024
                                lt = hq[:, f,
                                        (tt % 8) * 128:(tt % 8) * 128 + 128]
                                for c in range(2):
                                    ins = e.matmul(
                                        ps[:, off + c * 512:
                                           off + c * 512 + 512],
                                        lt,
                                        w2v[:, (f % 3) * 1024 + c * 512:
                                            (f % 3) * 1024 + c * 512 + 512],
                                        start=(f == 0), stop=False)
                            return ins
                        return fx

                    E("tensor", mm2fn(),
                      waits=[(s_pool, w2cast[(sp, f)]),
                             (s_dve, chains_done), (s_act, q3act)],
                      inc=s_pe)
                    mm2pe[(sp, f)] = s_pe.n

                def biasfn(tiles=tiles):
                    def fx(e):
                        ins = None
                        for i, tt in enumerate(tiles):
                            ps = ps_a if i < 2 else ps_c
                            off = (i % 2) * 1024
                            for c in range(2):
                                ins = e.matmul(
                                    ps[:, off + c * 512:off + c * 512 + 512],
                                    onesbf[:, 0:128],
                                    b2_bf[:, c * 512:(c + 1) * 512],
                                    start=False, stop=True)
                        return ins
                    return fx

                E("tensor", biasfn(), waits=[(s_pool, b2cast)], inc=s_pe)
                mm2_all = s_pe.n

                for i, tt in enumerate(tiles):
                    ps = ps_a if i < 2 else ps_c
                    off = (i % 2) * 1024
                    par = tt % 2
                    E("sync", (lambda tt, par: lambda e: e.dma_start(
                        out=xst[:, par, :],
                        in_=resid_d[tt * 128:(tt + 1) * 128, :]))(tt, par),
                        waits=res_w + [(s_dve, s_dve.n)],
                        inc=s_x2[par], inck=16)
                    ip = i % 2
                    T2 = TP[:, ip, 2, :]
                    T0 = TP[:, ip, 0, :]
                    T1 = TP[:, ip, 1, :]
                    V0 = TBv[:, ip * 3 + 0, :]
                    V1 = TBv[:, ip * 3 + 1, :]
                    V2 = TBv[:, ip * 3 + 2, :]
                    IT = TI[:, ip, :]
                    E("scalar", (lambda ps, off, T2: lambda e: e.activation(
                        out=T2, in_=ps[:, off:off + 1024], func=AF.Identity,
                        bias=consts[:, 2:3], scale=256.0))(ps, off, T2),
                        waits=[(s_pe, mm2_all), (s_dve, s_dve.n),
                               (s_pool, s_pool.n)],
                        inc=s_act)
                    q3act = s_act.n
                    E("vector", (lambda IT, T2: lambda e: e.tensor_copy(
                        out=IT, in_=T2))(IT, T2),
                        waits=[(s_act, s_act.n)], inc=s_dve)
                    E("vector", (lambda IT, T0: lambda e: e.tensor_copy(
                        out=T0, in_=IT))(IT, T0), inc=s_dve)
                    E("vector", (lambda T2, T0, T1: lambda e: e.tensor_sub(
                        out=T1, in0=T2, in1=T0))(T2, T0, T1), inc=s_dve)
                    for j, thr in enumerate((-1.0, 0.0, 1.0)):
                        E("vector", (lambda T1, u, thr=thr: lambda e:
                           e.tensor_scalar(out=u, in0=T1, scalar1=thr,
                                           scalar2=None, op0=ALU.is_ge))(
                               T1, (V0, V1, V2)[j]),
                           inc=s_dve)
                    E("vector", (lambda V0, V1: lambda e: e.tensor_add(
                        out=V0, in0=V0, in1=V1))(V0, V1), inc=s_dve)
                    E("vector", (lambda V0, V2: lambda e: e.tensor_add(
                        out=V0, in0=V0, in1=V2))(V0, V2), inc=s_dve)
                    E("vector", (lambda T0, V0, T1: lambda e:
                       e.scalar_tensor_tensor(out=T1, in0=T0, scalar=-2.0,
                                              in1=V0, op0=ALU.add,
                                              op1=ALU.add))(T0, V0, T1),
                       inc=s_dve)
                    E("vector", (lambda T1, par, tt: lambda e:
                       e.scalar_tensor_tensor(
                           out=outst[:, tt % 2, :], in0=T1,
                           scalar=1.0 / 256.0, in1=xst[:, par, :],
                           op0=ALU.mult, op1=ALU.add))(T1, par, tt),
                       waits=[(s_x2[par], s_x2[par].n),
                              (s_out2[tt % 2], out_cnt.get(tt - 2, 0))],
                       inc=s_dve)
                    E("gpsimd", (lambda tt: lambda e: e.dma_start(
                        out=out_d[tt * 128:(tt + 1) * 128, :],
                        in_=outst[:, tt % 2, :]))(tt),
                        waits=[(s_dve, s_dve.n)], inc=s_out2[tt % 2],
                        inck=16)
                    out_cnt[tt] = s_out2[tt % 2].n
            if th == 0:
                E("sync", None, waits=drain_waits())
                plan.barrier(s_bar)

        E("sync", None, waits=drain_waits())

        blk = ctx.enter_context(nc.Block())

        @blk.sync
        def _(eng):
            plan.run("sync", eng)

        @blk.tensor
        def _(eng):
            plan.run("tensor", eng)

        @blk.vector
        def _(eng):
            plan.run("vector", eng)

        @blk.scalar
        def _(eng):
            plan.run("scalar", eng)

        @blk.gpsimd
        def _(eng):
            plan.run("gpsimd", eng)

        est.close()
        xnst.close()
        ab2.close()
    return nc, sorted(taps.keys())


def kernel(L0_attn_x, L0_attn_Wq, L0_attn_Wk, L0_attn_Wv, L0_attn_Wo,
           L0_attn_bo, L0_ffn_w1, L0_ffn_b1, L0_ffn_w2, L0_ffn_b2):
    x = np.ascontiguousarray(np.asarray(L0_attn_x, dtype=np.float32))
    wts = {k: np.ascontiguousarray(np.asarray(v, dtype=np.float32))
           for k, v in (("wq", L0_attn_Wq), ("wk", L0_attn_Wk),
                        ("wv", L0_attn_Wv), ("wo", L0_attn_Wo),
                        ("bo", L0_attn_bo), ("w1", L0_ffn_w1),
                        ("b1", L0_ffn_b1), ("w2", L0_ffn_w2),
                        ("b2", L0_ffn_b2))}
    nc, tapnames = build_program()
    in_maps = []
    for c in range(8):
        b, qh = c // 2, c % 2
        xq = np.ascontiguousarray(x[b, qh * NQ:(qh + 1) * NQ, :])
        xr = np.ascontiguousarray(x[b, (1 - qh) * NQ:(2 - qh) * NQ, :])
        in_maps.append(dict(wts, xq=xq, xr=xr))
    res = run_bass_kernel_spmd(nc, in_maps, core_ids=list(range(8)))
    out = np.empty((4, 4096, 1024), np.float32)
    for c in range(8):
        out[c // 2, (c % 2) * NQ:(c % 2 + 1) * NQ, :] = res.results[c]["out"]
    if tapnames:
        kernel.taps = [{t: np.asarray(r["tap_" + t]) for t in tapnames}
                       for r in res.results]
    return out
